# version 01
# baseline (speedup 1.0000x reference)
import numpy as np
import ml_dtypes
import contextlib
import concourse.bass as bass
import concourse.mybir as mybir
from concourse.bass_utils import run_bass_kernel_spmd
from concourse.expressions import RuntimeValue

F32 = mybir.dt.float32
BF16 = mybir.dt.bfloat16
U8 = mybir.dt.uint8
I32 = mybir.dt.int32
AF = mybir.ActivationFunctionType
OP = mybir.AluOpType

import os
STAGE = int(os.environ.get("MK_STAGE", "4"))
SUB_OQ = int(os.environ.get("MK_OQ", "4"))
SUB_CH = int(os.environ.get("MK_CH", "32"))
SUB_KV = int(os.environ.get("MK_KV", "1"))
SUB_EXC = int(os.environ.get("MK_EXC", "3"))
SUB_VEV = int(os.environ.get("MK_VEV", "1"))
SUB_A3 = int(os.environ.get("MK_A3", "5"))
SUB_QT = int(os.environ.get("MK_QT", "16"))
SUB_NJ = int(os.environ.get("MK_NJ", "999"))
SUB_GST = int(os.environ.get("MK_GST", "1"))
SUB_GDV = int(os.environ.get("MK_GDV", "1"))
SUB_B = int(os.environ.get("MK_B", "9"))
SUB_JLATE = int(os.environ.get("MK_JLATE", "0"))
GSEM = os.environ.get("MK_GSEM", "vg")
SUB_SMODE = int(os.environ.get("MK_SMODE", "0"))
DEBUG = bool(int(os.environ.get("MK_DBG", "1")))
NCORE = 8

D = 2048
S_OWN = 1024
NCH = 32
INTER = 5504
NKT = 16
H = 32
JOBW = 1024


class Plan:
    def __init__(self):
        self.ops = {"sync": [], "gpsimd": [], "vector": [], "scalar": [],
                    "tensor": []}
        self.cnt = {}

    def em(self, eng, fn):
        self.ops[eng].append(fn)

    def inc(self, sem, n=1):
        self.cnt[sem] = self.cnt.get(sem, 0) + n
        return self.cnt[sem]

    def now(self, sem):
        return self.cnt.get(sem, 0)


def build_program():
    nc = bass.Bass("TRN2", target_bir_lowering=False, debug=False,
                   detect_race_conditions=False)
    P = Plan()

    dt = lambda n, s, d, k="ExternalInput", **kw: nc.dram_tensor(
        n, s, d, kind=k, **kw).ap()
    x_own = dt("x_own", [D, S_OWN], F32)
    x_ext = dt("x_ext", [D, 3072], F32)
    wq_d = dt("wq", [D, D], F32)
    wk_d = dt("wk", [D, D], F32)
    wv_d = dt("wv", [D, D], F32)
    wo_d = dt("wo", [D, D], F32)
    bo_d = dt("bo", [128, 16], F32)
    lut_d = dt("lut", [128, 16], F32)
    sel_d = dt("sel", [32, D], BF16)
    gidx_d = dt("gidx", [1024, INTER], U8)
    uidx_d = dt("uidx", [1024, INTER], U8)
    didx_d = dt("didx", [2816, D], U8)
    runid_d = dt("runid", [1, 1], I32)
    eid_d = dt("eid", [1, 1], I32)
    out_d = dt("out", [D, S_OWN], F32, k="ExternalOutput")
    if DEBUG and STAGE == 1:
        dbg_kv = dt("dbg_kv", [128, 1040], F32, k="ExternalOutput")
    if DEBUG and STAGE == 2:
        dbg_x1 = dt("dbg_x1", [128, 16 * S_OWN], F32, k="ExternalOutput")

    x1_spill = dt("x1_spill", [2, D, 512], F32, k="Internal")
    Wloc_g = dt("Wloc_g", [1024, INTER], BF16, k="Internal")
    Wloc_u = dt("Wloc_u", [1024, INTER], BF16, k="Internal")
    Wloc_d = dt("Wloc_d", [2816, D], BF16, k="Internal")
    shd = lambda n, s, d: nc.dram_tensor(n, s, d, kind="Internal",
                                         addr_space="Shared").ap()
    Wg_n = shd("Wg_n", [2048, INTER], BF16)
    Wu_n = shd("Wu_n", [2048, INTER], BF16)
    Wd_n = shd("Wd_n", [5632, 2048], BF16)
    Wd4 = shd("Wd4", [16, 128, 44, 128], BF16)
    kvx = shd("kvx", [256, 1040], F32)
    flags = shd("flags", [4, 1, 2], I32)

    es = contextlib.ExitStack()
    sb = lambda n, s, d: es.enter_context(nc.sbuf_tensor(n, s, d))

    xb = sb("xb", [128, NKT * S_OWN], BF16)
    h_hi = sb("h_hi", [128, 16384], BF16)
    wsl = sb("wsl", [128, 16384], BF16)
    xck = sb("xck", [128, 2 * 2048], BF16)
    ktm = sb("ktm", [128, 2 * 512], BF16)
    vpad = sb("vpad", [128, 2 * 520], BF16)
    kv_b = sb("kv_b", [128, 1040], BF16)
    gsc_i = sb("gsc_i", [128, 2 * JOBW], U8)
    gsc_b = sb("gsc_b", [128, JOBW], BF16)
    gsc_w = sb("gsc_w", [128, 2 * JOBW], BF16)
    gsc_t = sb("gsc_t", [128, JOBW], BF16)
    sel_s = sb("sel_s", [32, D], BF16)
    y1b = sb("y1b", [128, 512], BF16)
    y1sq = sb("y1sq", [128, 512], BF16)
    rnb = sb("rnb", [32, 512], BF16)
    ones_c = sb("ones_c", [128, 1], BF16)
    ones_r32 = sb("ones_r32", [1, 128], F32)

    big1 = sb("big1", [128, 8192], F32)
    qsc = sb("qsc", [128, 512], F32)
    hsil = sb("hsil", [128, 512], F32)
    xres = sb("xres", [128, 2 * 512], F32)
    nrow0 = sb("nrow0", [1, 512], F32)
    nrow1 = sb("nrow1", [1, 512], F32)
    norms = sb("norms", [32, 512], F32)
    s_mean = sb("s_mean", [1, 512], F32)
    s_sq = sb("s_sq", [1, 512], F32)
    s_var = sb("s_var", [1, 512], F32)
    s_rstd = sb("s_rstd", [1, 512], F32)
    eps1 = sb("eps1", [1, 1], F32)
    lutbc = sb("lutbc", [128, 16], F32)
    bo_sb = sb("bo_sb", [128, 16], F32)
    rid_s = sb("rid_s", [1, 1], I32)
    eid_s = sb("eid_s", [1, 1], I32)
    flg_s = sb("flg_s", [1, 2], I32)

    kv_mine = big1[:, 0:1040]
    kv_peer = big1[:, 1040:2080]

    B = [es.enter_context(nc.psum_tensor(f"pb{i}", [128, 512], F32))
         for i in range(8)]

    sems = {}
    for name in ("sq", "gq", "pe", "ac", "ve", "vg", "gc", "po"):
        sems[name] = es.enter_context(nc.semaphore(name))
    SQ, GQ, PE_, AC, VE, VG, GC, PO = (sems[k] for k in
                                       ("sq", "gq", "pe", "ac", "ve", "vg",
                                        "gc", "po"))

    def w(eng, sem, val):
        if val <= 0:
            return
        s = sems[sem]
        P.em(eng, lambda e, s=s, v=val: e.wait_ge(s, v))

    def wnow(eng, sem):
        w(eng, sem, P.now(sem))

    def dma(q, out, in_):
        sem = "sq" if q == "sync" else "gq"
        v = P.inc(sem, 16)
        s = sems[sem]
        P.em(q, lambda e, o=out, i=in_, s=s: e.dma_start(out=o, in_=i
                                                         ).then_inc(s, 16))
        return v

    def mm(lhsT, rhs, out, start, stop):
        P.em("tensor", lambda e, l=lhsT, r=rhs, o=out, st=start, sp=stop:
             e.matmul(o, l, r, start=st, stop=sp, skip_group_check=True))

    def mm_inc(lhsT, rhs, out, start, stop):
        v = P.inc("pe", 1)
        P.em("tensor", lambda e, l=lhsT, r=rhs, o=out, st=start, sp=stop:
             e.matmul(o, l, r, start=st, stop=sp, skip_group_check=True
                      ).then_inc(PE_, 1))
        return v

    def mm_seq(ops_list):
        for (l, r, o, st, sp) in ops_list[:-1]:
            mm(l, r, o, st, sp)
        l, r, o, st, sp = ops_list[-1]
        return mm_inc(l, r, o, st, sp)

    def act(fn, inc=False):
        if inc:
            v = P.inc("ac", 1)
            P.em("scalar", lambda e, f=fn: f(e).then_inc(AC, 1))
            return v
        P.em("scalar", fn)

    def vec(fn, inc=False, sem="ve"):
        if inc:
            v = P.inc(sem, 1)
            s = sems[sem]
            P.em("vector", lambda e, f=fn, s=s: f(e).then_inc(s, 1))
            return v
        P.em("vector", fn)

    dma("sync", lutbc[:], lut_d[:])
    dma("sync", bo_sb[:], bo_d[:])
    dma("sync", sel_s[:], sel_d[:])
    dma("sync", rid_s[:], runid_d[:])
    dma("sync", eid_s[:], eid_d[:])
    v_consts = P.now("sq")
    v_xb = dma("gpsimd", xb[:].rearrange("p (k t) -> p k t", k=NKT),
               x_own.rearrange("(k p) t -> p k t", p=128))

    def _memsets(e):
        if DEBUG:
            e.memset(big1[:], 0.0)
        e.memset(ones_c[:], 1.0)
        e.memset(ones_r32[:], 1.0)
        e.memset(eps1[:], 1e-5)
        va = vpad[:].rearrange("p (b h c) -> p b h c", b=2, c=65)
        return e.memset(va[:, :, :, 64:65], 1.0)
    v_memset = vec(_memsets, inc=True)

    greg = {}

    def gp_regs(e):
        cid = e.partition_id()
        greg["e"] = cid % 2
        greg["peer"] = 1 - cid % 2
    P.em("gpsimd", gp_regs)

    def _lazy_regs(e):
        if "ebgu" in greg:
            return
        ev = e.value_load(eid_s[0:1, 0:1], min_val=0, max_val=1)
        er = e.to_reg(ev)
        ebgu = e.to_reg(0)
        e.reg_mul(ebgu, er, 1024)
        greg["ebgu"] = ebgu
        ebd = e.to_reg(0)
        e.reg_mul(ebd, er, 2816)
        greg["ebd"] = ebd
        greg["rowoff"] = e.to_reg(0)

    def dyn_rows(e, base_reg, const_rows, max_start):
        _lazy_regs(e)
        e.reg_alu(greg["rowoff"], greg[base_reg], const_rows, OP.add)
        return bass.DynSlice(
            RuntimeValue(val=greg["rowoff"], min_val=0, max_val=max_start),
            128)

    sreg = {}

    def sy_regs(e):
        cid = e.partition_id()
        sreg["e"] = cid % 2
    P.em("sync", sy_regs)

    gstate = {"n": 0, "done": 0}
    v_job_g = {}

    def emit_gather_job(mat, kt, c0, width):
        j = gstate["n"]
        gstate["n"] += 1
        par = j % 2
        idx_t = gsc_i[:, JOBW * par:JOBW * par + width]
        wac = gsc_w[:, JOBW * par:JOBW * par + width]
        src = {"g": gidx_d, "u": uidx_d, "d": didx_d}[mat]
        w("sync", GSEM, v_job_g.get(j - 2, 0))
        v_idx = dma("sync", idx_t,
                    src[128 * kt:128 * kt + 128, c0:c0 + width])
        w("vector", "sq", v_idx)
        wnow("vector", "gq")

        def _gather(e, it=idx_t, wa=wac, wd=width):
            e.tensor_copy(gsc_b[:, 0:wd], it)
            if not SUB_GDV:
                return e.tensor_copy(wa, gsc_b[:, 0:wd])
            e.tensor_scalar(wa, gsc_b[:, 0:wd], 0.0, lutbc[:, 0:1],
                            OP.is_equal, OP.mult)
            for jj in range(1, 15):
                e.tensor_scalar(gsc_t[:, 0:wd], gsc_b[:, 0:wd], float(jj),
                                lutbc[:, jj:jj + 1], OP.is_equal, OP.mult)
                e.tensor_add(wa, wa, gsc_t[:, 0:wd])
            e.tensor_scalar(gsc_t[:, 0:wd], gsc_b[:, 0:wd], 15.0,
                            lutbc[:, 15:16], OP.is_equal, OP.mult)
            return e.tensor_add(wa, wa, gsc_t[:, 0:wd])
        v_job_g[j] = vec(_gather, inc=True, sem=GSEM)
        w("gpsimd", GSEM, v_job_g[j])
        Wloc = {"g": Wloc_g, "u": Wloc_u, "d": Wloc_d}[mat]

        def _store(e, Wloc=Wloc, kt=kt, c0=c0, width=width, wa=wac):
            return e.dma_start(out=Wloc[128 * kt:128 * kt + 128,
                                        c0:c0 + width],
                               in_=wa).then_inc(GQ, 16)
        if SUB_GST:
            P.inc("gq", 16)
            P.em("gpsimd", _store)

    gate_jobs = [("g", kt, c0, min(JOBW, INTER - c0))
                 for kt in range(8) for c0 in range(0, INTER, JOBW)]
    up_jobs = [("u", kt, c0, min(JOBW, INTER - c0))
               for kt in range(8) for c0 in range(0, INTER, JOBW)]
    down_jobs = [("d", kt, c0, JOBW) for kt in range(22)
                 for c0 in range(0, D, JOBW)]
    job_q = gate_jobs + up_jobs + down_jobs
    NG, NU = len(gate_jobs), len(up_jobs)
    flag_after = {NG: 1, NG + NU: 2, NG + NU + len(down_jobs): 3}

    def emit_jobs(n):
        if STAGE < 3:
            return
        for _ in range(n):
            if not job_q or gstate["done"] >= SUB_NJ:
                return
            emit_gather_job(*job_q.pop(0))
            gstate["done"] += 1
            if gstate["done"] in flag_after:
                fi = flag_after[gstate["done"]]
                wnow("gpsimd", "gq")
                w("gpsimd", "sq", v_consts)
                if fi == 1:
                    def _pub(e):
                        return e.dma_start(
                            out=Wg_n[bass.ts(greg["e"], 1024), :],
                            in_=Wloc_g[:, :]).then_inc(GQ, 16)
                elif fi == 2:
                    def _pub(e):
                        return e.dma_start(
                            out=Wu_n[bass.ts(greg["e"], 1024), :],
                            in_=Wloc_u[:, :]).then_inc(GQ, 16)
                else:
                    def _pub(e):
                        return e.dma_start(
                            out=Wd_n[bass.ts(greg["e"], 2816), :],
                            in_=Wloc_d[:, :]).then_inc(GQ, 16)
                P.inc("gq", 16)
                P.em("gpsimd", _pub)
                wnow("gpsimd", "gq")

                def _flag(e, fi=fi):
                    return e.dma_start(
                        out=flags[fi, 0:1, bass.ts(greg["e"], 1)],
                        in_=rid_s[:]).then_inc(GQ, 16)
                P.inc("gq", 16)
                P.em("gpsimd", _flag)

    def emit_all_jobs():
        emit_jobs(len(job_q))

    def emit_poll(fi):
        def _poll(e, fi=fi):
            if "po_polls" not in greg:
                greg["po_polls"] = e.to_reg(0)
                greg["po_tgt"] = e.to_reg(0)
                greg["po_nd"] = e.to_reg(1)
                greg["po_msc"] = e.to_reg(0)
                rr = e.value_load(rid_s[0:1, 0:1])
                greg["po_rid"] = e.to_reg(rr)
            polls, tgt = greg["po_polls"], greg["po_tgt"]
            nd, msc, rreg = greg["po_nd"], greg["po_msc"], greg["po_rid"]

            def cond():
                e.reg_add(polls, polls, 1)
                e.reg_mul(tgt, polls, 16)
                e.dma_start(out=flg_s[:], in_=flags[fi, :, :]).then_inc(PO, 16)
                e.wait_ge(PO, tgt)
                e.reg_mov(nd, 0)
                for cc in range(2):
                    vc = e.value_load(flg_s[0:1, cc:cc + 1])
                    e.reg_alu(msc, vc, rreg, OP.not_equal)
                    e.reg_alu(nd, nd, msc, OP.add)
                return nd
            with e.While(cond):
                pass
            return e.sem_inc(GC, 1)
        v = P.inc("gc", 1)
        P.em("gpsimd", _poll)
        return v

    v_evac_v = {}
    v_kvmm = {}
    gch = -1

    def head_slice(h):
        oq, hh = h // 8, h % 8
        return 64 * (hh % 2), 260 * oq + 65 * (hh // 2)

    for oq in range(SUB_OQ):
        wnow("gpsimd", "pe")
        o0 = 512 * oq
        v_wk = dma("gpsimd",
                   wsl[:, 0:8192].rearrange("p (k o) -> p k o", o=512),
                   wk_d[:, o0:o0 + 512].rearrange("(k p) o -> p k o", p=128))
        v_wv = dma("gpsimd",
                   wsl[:, 8192:16384].rearrange("p (k o) -> p k o", o=512),
                   wv_d[:, o0:o0 + 512].rearrange("(k p) o -> p k o", p=128))
        for ch in range(SUB_CH):
            gch += 1
            par = gch % 2
            own = ch < 8
            if own:
                xti = lambda k, ch=ch: xb[:, S_OWN * k + 128 * ch:
                                          S_OWN * k + 128 * ch + 128]
            else:
                fc = ch - 8
                w("gpsimd", "pe", v_kvmm.get(gch - 2, 0))
                v_ck = dma("gpsimd",
                           xck[:, 2048 * par:2048 * par + 2048].rearrange(
                               "p (k t) -> p k t", t=128),
                           x_ext[:, 128 * fc:128 * fc + 128].rearrange(
                               "(k p) t -> p k t", p=128))
                xti = lambda k, par=par: xck[:, 2048 * par + 128 * k:
                                             2048 * par + 128 * k + 128]
            pb_k = B[par]
            pb_v = B[2 + par]
            if ch == 0:
                w("tensor", "gq", max(v_wk, v_wv, v_xb))
            elif not own:
                w("tensor", "gq", v_ck)
            w("tensor", "ac", v_evac_v.get(gch - 2, 0))
            v_pk = mm_seq([(xti(k), wsl[:, 512 * k:512 * k + 512],
                            pb_k[:, :], k == 0, k == NKT - 1)
                           for k in range(NKT)])
            v_pv = mm_seq([(xti(k), wsl[:, 8192 + 512 * k:8192 + 512 * k + 512],
                            pb_v[:, :], k == 0, k == NKT - 1)
                           for k in range(NKT)])
            w("scalar", "pe", v_kvmm.get(gch - 2, 0))
            w("scalar", "pe", v_pk)
            if gch == 0:
                w("scalar", "ve", v_memset)
            act(lambda e, par=par, pb=pb_k: e.activation(
                ktm[:, 512 * par:512 * par + 512], pb[:, :], AF.Relu))
            w("scalar", "pe", v_pv)
            va = vpad[:].rearrange("p (b h c) -> p b h c", b=2, c=65)
            if SUB_VEV:
                v_evac_v[gch] = act(
                    lambda e, par=par, pb=pb_v, va=va: e.activation(
                        va[:, par, :, 0:64],
                        pb[:, :].rearrange("p (h c) -> p h c", c=64),
                        AF.Identity),
                    inc=True)
            else:
                v_evac_v[gch] = act(
                    lambda e, par=par, pb=pb_v: e.activation(
                        vpad[:, 520 * par:520 * par + 512], pb[:, :],
                        AF.Identity),
                    inc=True)
            w("tensor", "ac", v_evac_v[gch])
            if SUB_KV:
                kvops = []
                for hh in range(8):
                    r0 = 64 * (hh % 2)
                    kvops.append((
                        ktm[:, 512 * par + 64 * hh:512 * par + 64 * hh + 64],
                        vpad[:, 520 * par + 65 * hh:520 * par + 65 * hh + 65],
                        B[4 + hh // 2][r0:r0 + 64, 0:65],
                        ch == 0, ch == NCH - 1))
                v_kvmm[gch] = mm_seq(kvops)
            else:
                v_kvmm[gch] = P.now("pe")
        if SUB_KV and SUB_CH == NCH:
            w("vector", "pe", v_kvmm[gch])
            for b4 in range(3):
                vec(lambda e, oq=oq, b4=b4: e.tensor_copy(
                    kv_mine[:, 260 * oq + 65 * b4:260 * oq + 65 * b4 + 65],
                    B[4 + b4][:, 0:65]))
            v_kvev = vec(lambda e, oq=oq: e.tensor_copy(
                kv_mine[:, 260 * oq + 195:260 * oq + 260],
                B[7][:, 0:65]), inc=True)
            if oq < 3:
                w("tensor", "ve", v_kvev)
        if not SUB_JLATE:
            emit_jobs(12)

    if SUB_EXC >= 1:
        wnow("gpsimd", "ve")

        def _kv_out(e):
            return e.dma_start(out=kvx[bass.ts(greg["e"], 128), :],
                               in_=kv_mine).then_inc(GQ, 16)
        P.inc("gq", 16)
        P.em("gpsimd", _kv_out)
        wnow("gpsimd", "gq")
        w("gpsimd", "sq", v_consts)

        def _kv_flag(e):
            return e.dma_start(out=flags[0, 0:1, bass.ts(greg["e"], 1)],
                               in_=rid_s[:]).then_inc(GQ, 16)
        P.inc("gq", 16)
        P.em("gpsimd", _kv_flag)
    if SUB_EXC >= 2:
        v_kvpoll = emit_poll(0)
    if SUB_EXC >= 3:
        w("gpsimd", "gc", v_kvpoll)

        def _kv_in(e):
            return e.dma_start(out=kv_peer,
                               in_=kvx[bass.ts(greg["peer"], 128), :]
                               ).then_inc(GQ, 16)
        v_peer = P.inc("gq", 16)
        P.em("gpsimd", _kv_in)
        w("vector", "gq", v_peer)
        vec(lambda e: e.tensor_add(kv_mine, kv_mine, kv_peer))
    v_kvfull = vec(lambda e: e.tensor_copy(kv_b[:], kv_mine), inc=True)

    if DEBUG and STAGE == 1:
        wnow("gpsimd", "ve")
        dma("gpsimd", dbg_kv[:], kv_mine)

    qb = h_hi[:, 0:16384]
    if STAGE >= 2:
        v_qev = {}
        wnow("tensor", "ac")
        wnow("tensor", "ve")
        for oh in range(2):
            wnow("gpsimd", "pe")
            v_wq = dma("gpsimd",
                       wsl[:, 0:16384].rearrange("p (k o) -> p k o", o=1024),
                       wq_d[:, 1024 * oh:1024 * oh + 1024].rearrange(
                           "(k p) o -> p k o", p=128))
            w("tensor", "gq", v_wq)
            for t8 in range(8):
                t = 8 * oh + t8
                if t >= SUB_QT:
                    continue
                w("tensor", "ac", v_qev.get(t - 2, 0))
                for sc in range(2):
                    pb = B[2 * (t8 % 2) + sc]
                    v_qmm = mm_seq([(wsl[:, 1024 * k + 128 * t8:
                                         1024 * k + 128 * t8 + 128],
                                     xb[:, S_OWN * k + 512 * sc:
                                        S_OWN * k + 512 * sc + 512],
                                     pb[:, :], k == 0, k == NKT - 1)
                                    for k in range(NKT)])
                    w("scalar", "pe", v_qmm)
                    v_qev[t] = act(lambda e, pb=pb, t=t, sc=sc: e.activation(
                        qb[:, 1024 * t + 512 * sc:1024 * t + 512 * sc + 512],
                        pb[:, :], AF.Relu), inc=True)
        v_q_done = P.now("ac")
        if not SUB_JLATE:
            emit_jobs(8)

    v_spill = {}
    v_apply = {}
    if STAGE >= 2 and SUB_A3 >= 2:
        for sh in range(2):
            qkv_b = xb[:, 0:8192]
            v_nev = {}
            v_ndma = {}
            for h in range(H):
                r0, c0 = head_slice(h)
                pb = B[2 + h % 2]
                if h < 2:
                    w("tensor", "ve", v_kvfull)
                    w("tensor", "ac", v_q_done)
                    if sh == 1:
                        w("tensor", "ve", v_apply.get(0, 0))
                if h >= 2:
                    w("tensor", "ac", v_nev[h - 2])
                mm_n = mm_inc(
                    kv_b[r0:r0 + 64, c0 + 64:c0 + 65],
                    qb[64 * (h % 2):64 * (h % 2) + 64,
                       1024 * (h // 2) + 512 * sh:
                       1024 * (h // 2) + 512 * sh + 512],
                    pb[0:1, :], start=True, stop=True)
                w("scalar", "pe", mm_n)
                w("scalar", "gq", v_ndma.get(h - 2, 0))
                nr = nrow0 if h % 2 == 0 else nrow1
                v_nev[h] = act(lambda e, pb=pb, nr=nr: e.activation(
                    nr[:], pb[0:1, :], AF.Identity), inc=True)
                w("gpsimd", "ac", v_nev[h])
                v_ndma[h] = dma("gpsimd", norms[h:h + 1, :], nr[:])
            wnow("vector", "gq")
            vec(lambda e: e.tensor_scalar(norms[:], norms[:], 1e-6, None,
                                          OP.max))
            vec(lambda e: e.reciprocal(norms[:], norms[:]))
            v_rnb = vec(lambda e: e.tensor_copy(rnb[:], norms[:]), inc=True)
            if SUB_A3 < 3:
                continue

            v_qkvb = {}
            for t in range(NKT):
                pb = B[t % 2]
                pbr = B[5]
                wv_prev = v_qkvb.get(t - 2, 0)
                w("tensor", "ve", wv_prev)
                qkops = []
                for hh in range(2):
                    h = 2 * t + hh
                    r0, c0 = head_slice(h)
                    qkops.append((kv_b[r0:r0 + 64, c0:c0 + 64],
                                  qb[64 * hh:64 * hh + 64,
                                     1024 * t + 512 * sh:
                                     1024 * t + 512 * sh + 512],
                                  pb[64 * hh:64 * hh + 64, :], True, True))
                v_qkvmm = mm_seq(qkops)
                w("tensor", "ve", v_rnb)
                if t >= 1:
                    w("tensor", "ve", v_qkvb.get(t - 1, 0))
                v_bcmm = mm_inc(sel_s[:, 128 * t:128 * t + 128], rnb[:, :],
                                pbr[:, :], start=True, stop=True)
                w("scalar", "pe", v_qkvmm)
                if t >= 1:
                    w("scalar", "ve", v_qkvb.get(t - 1, 0))
                v_qe = act(lambda e, pb=pb: e.activation(
                    qsc[:], pb[:, :], AF.Identity), inc=True)
                w("vector", "ac", v_qe)
                w("vector", "pe", v_bcmm)
                v_qkvb[t] = vec(lambda e, t=t, pbr=pbr: e.tensor_tensor(
                    qkv_b[:, 512 * t:512 * t + 512], qsc[:], pbr[:, :],
                    OP.mult), inc=True)

            if SUB_A3 < 4:
                continue
            wnow("tensor", "ac")
            if sh == 1:
                wnow("vector", "gq")
            v_y1ev = {}
            v_add = {}
            for oh in range(2):
                wnow("gpsimd", "pe")
                v_wo = dma("gpsimd",
                           wsl[:, 0:16384].rearrange("p (k o) -> p k o",
                                                     o=1024),
                           wo_d[:, 1024 * oh:1024 * oh + 1024].rearrange(
                               "(k p) o -> p k o", p=128))
                w("tensor", "gq", v_wo)
                for t8 in range(8):
                    g = 8 * oh + t8
                    pb = B[2 + t8 % 2]
                    w("gpsimd", "ve", v_add.get(g - 2, 0))
                    v_xr = dma("gpsimd",
                               xres[:, 512 * (g % 2):512 * (g % 2) + 512],
                               x_own[128 * g:128 * g + 128,
                                     512 * sh:512 * sh + 512])
                    w("tensor", "ve", v_qkvb[NKT - 1])
                    w("tensor", "ac", v_y1ev.get(g - 2, 0))
                    v_womm = mm_seq([(wsl[:, 1024 * k + 128 * t8:
                                           1024 * k + 128 * t8 + 128],
                                       qkv_b[:, 512 * k:512 * k + 512],
                                       pb[:, :], k == 0, k == NKT - 1)
                                      for k in range(NKT)])
                    w("scalar", "pe", v_womm)
                    w("scalar", "ve", v_add.get(g - 1, 0))
                    v_y1ev[g] = act(lambda e, pb=pb, g=g: e.activation(
                        hsil[:], pb[:, :], AF.Identity,
                        bias=bo_sb[:, g:g + 1]), inc=True)
                    w("vector", "ac", v_y1ev[g])
                    w("vector", "gq", v_xr)
                    v_add[g] = vec(lambda e, g=g: e.tensor_add(
                        big1[:, 512 * g:512 * g + 512], hsil[:],
                        xres[:, 512 * (g % 2):512 * (g % 2) + 512]),
                        inc=True)
                    v_y1b = vec(lambda e, g=g: e.tensor_copy(
                        y1b[:], big1[:, 512 * g:512 * g + 512]), inc=True)
                    w("scalar", "ve", v_y1b)
                    v_sqa = act(lambda e: e.activation(
                        y1sq[:], y1b[:], AF.Square), inc=True)
                    w("tensor", "ve", v_y1b)
                    mm(ones_c[:], y1b[:], B[6][0:1, :],
                       start=(g == 0), stop=(g == 15))
                    w("tensor", "ac", v_sqa)
                    v_stmm = mm_inc(ones_c[:], y1sq[:], B[7][0:1, :],
                                    start=(g == 0), stop=(g == 15))
                    w("vector", "pe", v_stmm)
            if SUB_A3 < 5:
                v_apply[sh] = P.now("ve")
                v_spill[sh] = P.now("gq")
                continue
            v_stats = P.now("pe")
            w("vector", "pe", v_stats)
            vec(lambda e: e.tensor_scalar(s_mean[:], B[6][0:1, :],
                                          1.0 / D, None, OP.mult))
            vec(lambda e: e.tensor_scalar(s_sq[:], B[7][0:1, :],
                                          1.0 / D, None, OP.mult))
            vec(lambda e: e.tensor_mul(s_var[:], s_mean[:], s_mean[:]))
            v_var = vec(lambda e: e.tensor_sub(s_var[:], s_sq[:], s_var[:]),
                        inc=True)
            w("scalar", "ve", v_var)
            act(lambda e: e.activation(s_rstd[:], s_var[:], AF.Ln,
                                       bias=eps1[:]))
            v_rstd = act(lambda e: e.activation(s_rstd[:], s_rstd[:],
                                                AF.Exp, scale=-0.5),
                         inc=True)
            w("tensor", "ac", v_rstd)
            w("tensor", "ve", v_var)
            mm(ones_r32[:], s_mean[:], B[2][:, :], start=True, stop=True)
            v_bc = mm_inc(ones_r32[:], s_rstd[:], B[3][:, :],
                          start=True, stop=True)
            w("vector", "pe", v_bc)
            last = None
            for g in range(NKT):
                vec(lambda e, g=g: e.tensor_sub(
                    hsil[:], big1[:, 512 * g:512 * g + 512], B[2][:, :]))
                last = vec(lambda e, g=g: e.tensor_mul(
                    big1[:, 512 * g:512 * g + 512], hsil[:], B[3][:, :]),
                    inc=True)
            v_apply[sh] = last
            w("gpsimd", "ve", v_apply[sh])
            v_spill[sh] = dma("gpsimd",
                              x1_spill[sh, :, :].rearrange(
                                  "(k p) t -> p k t", p=128),
                              big1[:].rearrange("p (k t) -> p k t", t=512))
            if DEBUG and STAGE == 2:
                dma("gpsimd", dbg_x1[:, 8192 * sh:8192 * sh + 8192],
                    big1[:])
            if not SUB_JLATE:
                emit_jobs(7)

    emit_all_jobs()

    if STAGE >= 4 and SUB_B >= 1:
        v_pollg = emit_poll(1)
        v_pollu = emit_poll(2)
        v_polld = emit_poll(3)

        def h_of(k):
            if k < 16:
                return xb[:, 8192 + 512 * k:8192 + 512 * k + 512]
            return h_hi[:, 512 * (k - 16):512 * (k - 16) + 512]

        v_out_store = 0
        for sh in (range(2) if SUB_B >= 2 else []):
            if sh == 1:
                w("sync", "gq", v_out_store)
            w("sync", "gq", v_spill[sh])
            v_x1r = dma("sync", big1[:].rearrange("p (k t) -> p k t", t=512),
                        x1_spill[sh, :, :].rearrange("(k p) t -> p k t",
                                                     p=128))
            x1b_sh = xb[:, 0:8192]
            w("gpsimd", "gq", v_spill[1])
            wnow("gpsimd", "pe")
            v_x1bc = dma("gpsimd",
                         x1b_sh.rearrange("p (k t) -> p k t", t=512),
                         x1_spill[sh, :, :].rearrange("(k p) t -> p k t",
                                                      p=128))
            if SUB_B < 3:
                w("gpsimd", "gq", v_x1bc)
                v_out_store = dma(
                    "gpsimd",
                    out_d.rearrange("(k p) t -> p k t", p=128)[
                        :, :, 512 * sh:512 * sh + 512],
                    big1[:].rearrange("p (k t) -> p k t", t=512))
                v_out_ve = P.now("ve")
                continue
            w("sync", "gc", v_pollu)
            w("sync", "gc", v_pollg)
            GSL = wsl[:, 0:8192]
            USL = wsl[:, 8192:16384]
            v_ht = {}
            v_pe_after = {}
            MACROS = [(m, min(512, INTER - 512 * m)) for m in range(11)]

            def plan_b1_load(m):
                mw = MACROS[m][1]
                w("sync", "pe", v_pe_after.get(m - 1, P.now("pe")
                                               if m < 1 else 0))
                vg_ = 0
                vu_ = 0
                for k in range(NKT):
                    vg_ = dma("sync", GSL[:, 512 * k:512 * k + mw],
                              Wg_n[128 * k:128 * k + 128,
                                   512 * m:512 * m + mw])
                    vu_ = dma("sync", USL[:, 512 * k:512 * k + mw],
                              Wu_n[128 * k:128 * k + 128,
                                   512 * m:512 * m + mw])
                return max(vg_, vu_)

            v_msl = {0: plan_b1_load(0)}
            for m, mw in MACROS:
                nt = (mw + 127) // 128
                for otl in range(nt):
                    ot = 4 * m + otl
                    par = ot % 2
                    pbg = B[par]
                    pbu = B[2 + par]
                    w("tensor", "sq", v_msl[m])
                    w("tensor", "gq", v_x1bc)
                    if sh == 1 or ot >= 2:
                        w("tensor", "ve", v_ht.get(ot - 2, 0))
                    if ot == 0:
                        w("tensor", "ve", v_apply[1] if sh == 0
                          else v_out_ve)
                    v_gmm = mm_seq([(GSL[:, 512 * k + 128 * otl:
                                         512 * k + 128 * otl + 128],
                                     x1b_sh[:, 512 * k:512 * k + 512],
                                     pbg[:, :], k == 0, k == NKT - 1)
                                    for k in range(NKT)])
                    v_umm = mm_seq([(USL[:, 512 * k + 128 * otl:
                                         512 * k + 128 * otl + 128],
                                     x1b_sh[:, 512 * k:512 * k + 512],
                                     pbu[:, :], k == 0, k == NKT - 1)
                                    for k in range(NKT)])
                    w("scalar", "pe", v_gmm)
                    w("scalar", "ve", v_ht.get(ot - 1, 0))
                    v_sil = act(lambda e, pbg=pbg: e.activation(
                        hsil[:], pbg[:, :], AF.Silu), inc=True)
                    w("vector", "ac", v_sil)
                    w("vector", "pe", v_umm)
                    v_ht[ot] = vec(lambda e, ot=ot, pbu=pbu: e.tensor_tensor(
                        h_of(ot), hsil[:], pbu[:, :], OP.mult), inc=True)
                v_pe_after[m] = P.now("pe")
                if m + 1 < 11:
                    v_msl[m + 1] = plan_b1_load(m + 1)
            v_h_done = P.now("ve")
            if SUB_B < 4:
                w("gpsimd", "ve", v_h_done)
                v_out_store = dma(
                    "gpsimd",
                    out_d.rearrange("(k p) t -> p k t", p=128)[
                        :, :, 512 * sh:512 * sh + 512],
                    big1[:].rearrange("p (k t) -> p k t", t=512))
                v_out_ve = P.now("ve")
                continue
            if sh == 0:
                w("sync", "gc", v_polld)
                for ot2 in range(16):
                    v_rp = dma("sync", Wd4[ot2, :, :, :],
                               Wd_n[:, 128 * ot2:128 * ot2 + 128].rearrange(
                                   "(k p) o -> p k o", p=128))
                v_repack = v_rp
            DS = [wsl[:, 0:5632], wsl[:, 5632:11264]]
            v_dsl = {}
            v_pe_d = {}

            def plan_b2_load(ot2):
                par = ot2 % 2
                if ot2 < 2:
                    w("sync", "pe", v_pe_after[10])
                    w("sync", "sq", v_repack)
                else:
                    w("sync", "pe", v_pe_d[ot2 - 2])
                v_dsl[ot2] = dma("sync", DS[par].rearrange(
                    "p (k o) -> p k o", o=128), Wd4[ot2, :, :, :])

            plan_b2_load(0)
            plan_b2_load(1)
            v_y2b_prev = 0
            for ot2 in range(16):
                par = ot2 % 2
                pbd = B[4 + par]
                w("tensor", "sq", v_dsl[ot2])
                w("tensor", "ve", v_h_done)
                if ot2 == 0:
                    wnow("tensor", "ve")
                w("tensor", "ve", v_y2b_prev)
                v_dmm = mm_seq([(DS[par][:, 128 * k:128 * k + 128],
                                 h_of(k), pbd[:, :], k == 0, k == 42)
                                for k in range(43)])
                v_pe_d[ot2] = v_dmm
                if ot2 + 2 <= 15:
                    plan_b2_load(ot2 + 2)
                w("vector", "pe", v_dmm)
                w("vector", "sq", v_x1r)
                vec(lambda e, ot2=ot2, pbd=pbd: e.tensor_add(
                    big1[:, 512 * ot2:512 * ot2 + 512],
                    big1[:, 512 * ot2:512 * ot2 + 512], pbd[:, :]))
                v_y2b = vec(lambda e, ot2=ot2: e.tensor_copy(
                    y1b[:], big1[:, 512 * ot2:512 * ot2 + 512]), inc=True)
                v_y2b_prev = v_y2b
                w("scalar", "ve", v_y2b)
                v_sqa = act(lambda e: e.activation(
                    y1sq[:], y1b[:], AF.Square), inc=True)
                w("tensor", "ve", v_y2b)
                mm(ones_c[:], y1b[:], B[6][0:1, :],
                   start=(ot2 == 0), stop=(ot2 == 15))
                w("tensor", "ac", v_sqa)
                v_stmm = mm_inc(ones_c[:], y1sq[:], B[7][0:1, :],
                                start=(ot2 == 0), stop=(ot2 == 15))
                w("vector", "pe", v_stmm)
            v_stats = P.now("pe")
            w("vector", "pe", v_stats)
            vec(lambda e: e.tensor_scalar(s_mean[:], B[6][0:1, :],
                                          1.0 / D, None, OP.mult))
            vec(lambda e: e.tensor_scalar(s_sq[:], B[7][0:1, :],
                                          1.0 / D, None, OP.mult))
            vec(lambda e: e.tensor_mul(s_var[:], s_mean[:], s_mean[:]))
            v_var = vec(lambda e: e.tensor_sub(s_var[:], s_sq[:], s_var[:]),
                        inc=True)
            w("scalar", "ve", v_var)
            act(lambda e: e.activation(s_rstd[:], s_var[:], AF.Ln,
                                       bias=eps1[:]))
            v_rstd = act(lambda e: e.activation(s_rstd[:], s_rstd[:],
                                                AF.Exp, scale=-0.5),
                         inc=True)
            w("tensor", "ac", v_rstd)
            w("tensor", "ve", v_var)
            mm(ones_r32[:], s_mean[:], B[2][:, :], start=True, stop=True)
            v_bc = mm_inc(ones_r32[:], s_rstd[:], B[3][:, :],
                          start=True, stop=True)
            w("vector", "pe", v_bc)
            last = None
            for g in range(NKT):
                vec(lambda e, g=g: e.tensor_sub(
                    hsil[:], big1[:, 512 * g:512 * g + 512], B[2][:, :]))
                last = vec(lambda e, g=g: e.tensor_mul(
                    big1[:, 512 * g:512 * g + 512], hsil[:], B[3][:, :]),
                    inc=True)
            v_out_ve = last
            w("gpsimd", "ve", v_out_ve)
            v_out_store = dma(
                "gpsimd",
                out_d.rearrange("(k p) t -> p k t", p=128)[
                    :, :, 512 * sh:512 * sh + 512],
                big1[:].rearrange("p (k t) -> p k t", t=512))

    if STAGE >= 4 and SUB_B < 2:
        wnow("gpsimd", "ve")
        dma("gpsimd", out_d.rearrange("(k p) t -> p k t", p=128)[:, :, 0:512],
            big1[:].rearrange("p (k t) -> p k t", t=512))
    P.em("gpsimd", lambda e: e.wait_ge(GQ, P.now("gq")))
    P.em("gpsimd", lambda e: e.wait_ge(SQ, P.now("sq")))
    P.em("gpsimd", lambda e: e.wait_ge(VE, P.now("ve")))
    P.em("gpsimd", lambda e: e.wait_ge(VG, P.now("vg")))
    P.em("gpsimd", lambda e: e.wait_ge(AC, P.now("ac")))
    P.em("gpsimd", lambda e: e.wait_ge(PE_, P.now("pe")))

    with es:
        with nc.Block() as block:
            @block.sync
            def _(eng):
                for f in P.ops["sync"]:
                    f(eng)

            @block.gpsimd
            def _(eng):
                for f in P.ops["gpsimd"]:
                    f(eng)

            @block.vector
            def _(eng):
                for f in P.ops["vector"]:
                    f(eng)

            @block.scalar
            def _(eng):
                for f in P.ops["scalar"]:
                    f(eng)

            @block.tensor
            def _(eng):
                for f in P.ops["tensor"]:
                    f(eng)

    return nc


def make_inputs(L0_attn_hidden_states, L0_attn_Wq, L0_attn_Wk, L0_attn_Wv,
                L0_attn_Wo, L0_attn_bo, L0_ffn_lut, L0_ffn_gate_idx,
                L0_ffn_up_idx, L0_ffn_down_idx):
    x = np.asarray(L0_attn_hidden_states)[0]
    xT = np.ascontiguousarray(x.T)
    lut_bc = np.tile(np.asarray(L0_ffn_lut, np.float32)[None, :], (128, 1))
    bo = np.ascontiguousarray(np.asarray(L0_attn_bo, np.float32).reshape(16, 128).T)
    sel = np.zeros((32, D), ml_dtypes.bfloat16)
    for t in range(16):
        sel[2 * t, 128 * t:128 * t + 64] = 1
        sel[2 * t + 1, 128 * t + 64:128 * t + 128] = 1
    gT = np.ascontiguousarray(np.asarray(L0_ffn_gate_idx).T).astype(np.uint8)
    uT = np.ascontiguousarray(np.asarray(L0_ffn_up_idx).T).astype(np.uint8)
    dT = np.ascontiguousarray(np.asarray(L0_ffn_down_idx).T).astype(np.uint8)
    dT_pad = np.zeros((2 * 2816, D), np.uint8)
    dT_pad[0:INTER, :] = dT
    runid = np.random.randint(1, 2**30, dtype=np.int32)
    wq = np.ascontiguousarray(np.asarray(L0_attn_Wq, np.float32))
    wk = np.ascontiguousarray(np.asarray(L0_attn_Wk, np.float32))
    wv = np.ascontiguousarray(np.asarray(L0_attn_Wv, np.float32))
    wo = np.ascontiguousarray(np.asarray(L0_attn_Wo, np.float32))
    in_maps = []
    for c in range(NCORE):
        p, e = c // 2, c % 2
        own = slice(1024 * c, 1024 * c + 1024)
        pair_lo, pair_hi = 2048 * p, 2048 * p + 2048
        comp = np.r_[0:pair_lo, pair_hi:8192]
        extra = comp[:3072] if e == 0 else comp[3072:]
        in_maps.append({
            "x_own": np.ascontiguousarray(xT[:, own]),
            "x_ext": np.ascontiguousarray(xT[:, extra]),
            "wq": wq, "wk": wk, "wv": wv, "wo": wo,
            "bo": bo, "lut": lut_bc, "sel": sel,
            "gidx": np.ascontiguousarray(gT[1024 * e:1024 * e + 1024, :]),
            "uidx": np.ascontiguousarray(uT[1024 * e:1024 * e + 1024, :]),
            "didx": np.ascontiguousarray(dT_pad[2816 * e:2816 * e + 2816, :]),
            "runid": np.full((1, 1), runid, np.int32),
            "eid": np.full((1, 1), e, np.int32),
        })
    return in_maps


def kernel(**inputs):
    in_maps = make_inputs(**inputs)
    nc = build_program()
    res = run_bass_kernel_spmd(nc, in_maps, core_ids=list(range(NCORE)))
    out = np.empty((1, 8192, D), np.float32)
    for c in range(NCORE):
        out[0, 1024 * c:1024 * c + 1024, :] = \
            np.asarray(res.results[c]["out"], np.float32).T
    return out


# version 02
# speedup vs baseline: 1.4663x; 1.4663x over previous
import numpy as np
import ml_dtypes
import contextlib
import concourse.bass as bass
import concourse.mybir as mybir
from concourse.bass_utils import run_bass_kernel_spmd
from concourse.expressions import RuntimeValue

F32 = mybir.dt.float32
BF16 = mybir.dt.bfloat16
U8 = mybir.dt.uint8
I32 = mybir.dt.int32
AF = mybir.ActivationFunctionType
OP = mybir.AluOpType

import os
STAGE = int(os.environ.get("MK_STAGE", "4"))
SUB_OQ = int(os.environ.get("MK_OQ", "4"))
SUB_CH = int(os.environ.get("MK_CH", "32"))
SUB_KV = int(os.environ.get("MK_KV", "1"))
SUB_EXC = int(os.environ.get("MK_EXC", "3"))
SUB_VEV = int(os.environ.get("MK_VEV", "1"))
SUB_A3 = int(os.environ.get("MK_A3", "5"))
SUB_QT = int(os.environ.get("MK_QT", "16"))
SUB_NJ = int(os.environ.get("MK_NJ", "999"))
SUB_GST = int(os.environ.get("MK_GST", "1"))
SUB_GDV = int(os.environ.get("MK_GDV", "1"))
SUB_B = int(os.environ.get("MK_B", "9"))
SUB_JLATE = int(os.environ.get("MK_JLATE", "0"))
GSEM = os.environ.get("MK_GSEM", "vg")
SUB_SMODE = int(os.environ.get("MK_SMODE", "0"))
DEBUG = bool(int(os.environ.get("MK_DBG", "1")))
NCORE = 8

D = 2048
S_OWN = 1024
NCH = 32
INTER = 5504
NKT = 16
H = 32
JOBW = 2048


class Plan:
    def __init__(self):
        self.ops = {"sync": [], "gpsimd": [], "vector": [], "scalar": [],
                    "tensor": []}
        self.cnt = {}

    def em(self, eng, fn):
        self.ops[eng].append(fn)

    def inc(self, sem, n=1):
        self.cnt[sem] = self.cnt.get(sem, 0) + n
        return self.cnt[sem]

    def now(self, sem):
        return self.cnt.get(sem, 0)


def build_program():
    nc = bass.Bass("TRN2", target_bir_lowering=False, debug=False,
                   detect_race_conditions=False)
    P = Plan()

    dt = lambda n, s, d, k="ExternalInput", **kw: nc.dram_tensor(
        n, s, d, kind=k, **kw).ap()
    x_own = dt("x_own", [D, S_OWN], F32)
    x_ext = dt("x_ext", [D, 3072], F32)
    wq_d = dt("wq", [D, D], F32)
    wk_d = dt("wk", [D, D], F32)
    wv_d = dt("wv", [D, D], F32)
    wo_d = dt("wo", [D, D], F32)
    bo_d = dt("bo", [128, 16], F32)
    lut_d = dt("lut", [128, 16], F32)
    sel_d = dt("sel", [32, D], BF16)
    gidx_d = dt("gidx", [1024, INTER], U8)
    uidx_d = dt("uidx", [1024, INTER], U8)
    didx_d = dt("didx", [2816, D], U8)
    runid_d = dt("runid", [1, 1], I32)
    eid_d = dt("eid", [1, 1], I32)
    out_d = dt("out", [D, S_OWN], F32, k="ExternalOutput")
    if DEBUG and STAGE == 1:
        dbg_kv = dt("dbg_kv", [128, 1040], F32, k="ExternalOutput")
    if DEBUG and STAGE == 2:
        dbg_x1 = dt("dbg_x1", [128, 16 * S_OWN], F32, k="ExternalOutput")

    x1_spill = dt("x1_spill", [2, D, 512], F32, k="Internal")
    Wloc_g = dt("Wloc_g", [1024, INTER], BF16, k="Internal")
    Wloc_u = dt("Wloc_u", [1024, INTER], BF16, k="Internal")
    Wloc_d = dt("Wloc_d", [2816, D], BF16, k="Internal")
    shd = lambda n, s, d: nc.dram_tensor(n, s, d, kind="Internal",
                                         addr_space="Shared").ap()
    Wg_n = shd("Wg_n", [2048, INTER], BF16)
    Wu_n = shd("Wu_n", [2048, INTER], BF16)
    Wd_n = shd("Wd_n", [5632, 2048], BF16)
    Wd4 = shd("Wd4", [16, 128, 44, 128], BF16)
    kvx = shd("kvx", [256, 1040], F32)
    flags = shd("flags", [4, 1, 2], I32)

    es = contextlib.ExitStack()
    sb = lambda n, s, d: es.enter_context(nc.sbuf_tensor(n, s, d))

    xb = sb("xb", [128, NKT * S_OWN], BF16)
    h_hi = sb("h_hi", [128, 16384], BF16)
    wsl = sb("wsl", [128, 16384], BF16)
    xck = sb("xck", [128, 2 * 2048], BF16)
    ktm = sb("ktm", [128, 2 * 512], BF16)
    vpad = sb("vpad", [128, 2 * 520], BF16)
    kv_b = sb("kv_b", [128, 1040], BF16)
    gsc_i = sb("gsc_i", [128, 2 * JOBW], U8)
    gsc_b = sb("gsc_b", [128, JOBW], BF16)
    gsc_w = sb("gsc_w", [128, 2 * JOBW], BF16)
    gsc_t = sb("gsc_t", [128, JOBW], BF16)
    sel_s = sb("sel_s", [32, D], BF16)
    y1b = sb("y1b", [128, 512], BF16)
    y1sq = sb("y1sq", [128, 512], BF16)
    rnb = sb("rnb", [32, 512], BF16)
    ones_c = sb("ones_c", [128, 1], BF16)
    ones_r32 = sb("ones_r32", [1, 128], F32)

    big1 = sb("big1", [128, 8192], F32)
    qsc = sb("qsc", [128, 512], F32)
    hsil = sb("hsil", [128, 512], F32)
    xres = sb("xres", [128, 2 * 512], F32)
    nrow0 = sb("nrow0", [1, 512], F32)
    nrow1 = sb("nrow1", [1, 512], F32)
    norms = sb("norms", [32, 512], F32)
    s_mean = sb("s_mean", [1, 512], F32)
    s_sq = sb("s_sq", [1, 512], F32)
    s_var = sb("s_var", [1, 512], F32)
    s_rstd = sb("s_rstd", [1, 512], F32)
    eps1 = sb("eps1", [1, 1], F32)
    lutbc = sb("lutbc", [128, 16], F32)
    bo_sb = sb("bo_sb", [128, 16], F32)
    rid_s = sb("rid_s", [1, 1], I32)
    eid_s = sb("eid_s", [1, 1], I32)
    flg_s = sb("flg_s", [1, 2], I32)

    kv_mine = big1[:, 0:1040]
    kv_peer = big1[:, 1040:2080]

    B = [es.enter_context(nc.psum_tensor(f"pb{i}", [128, 512], F32))
         for i in range(8)]

    sems = {}
    for name in ("sq", "gq", "pe", "ac", "ve", "vg", "gc", "po"):
        sems[name] = es.enter_context(nc.semaphore(name))
    SQ, GQ, PE_, AC, VE, VG, GC, PO = (sems[k] for k in
                                       ("sq", "gq", "pe", "ac", "ve", "vg",
                                        "gc", "po"))

    def w(eng, sem, val):
        if val <= 0:
            return
        s = sems[sem]
        P.em(eng, lambda e, s=s, v=val: e.wait_ge(s, v))

    def wnow(eng, sem):
        w(eng, sem, P.now(sem))

    def dma(q, out, in_):
        sem = "sq" if q == "sync" else "gq"
        v = P.inc(sem, 16)
        s = sems[sem]
        P.em(q, lambda e, o=out, i=in_, s=s: e.dma_start(out=o, in_=i
                                                         ).then_inc(s, 16))
        return v

    def mm(lhsT, rhs, out, start, stop):
        P.em("tensor", lambda e, l=lhsT, r=rhs, o=out, st=start, sp=stop:
             e.matmul(o, l, r, start=st, stop=sp, skip_group_check=True))

    def mm_inc(lhsT, rhs, out, start, stop):
        v = P.inc("pe", 1)
        P.em("tensor", lambda e, l=lhsT, r=rhs, o=out, st=start, sp=stop:
             e.matmul(o, l, r, start=st, stop=sp, skip_group_check=True
                      ).then_inc(PE_, 1))
        return v

    def mm_seq(ops_list):
        for (l, r, o, st, sp) in ops_list[:-1]:
            mm(l, r, o, st, sp)
        l, r, o, st, sp = ops_list[-1]
        return mm_inc(l, r, o, st, sp)

    def act(fn, inc=False):
        if inc:
            v = P.inc("ac", 1)
            P.em("scalar", lambda e, f=fn: f(e).then_inc(AC, 1))
            return v
        P.em("scalar", fn)

    def vec(fn, inc=False, sem="ve"):
        if inc:
            v = P.inc(sem, 1)
            s = sems[sem]
            P.em("vector", lambda e, f=fn, s=s: f(e).then_inc(s, 1))
            return v
        P.em("vector", fn)

    dma("sync", lutbc[:], lut_d[:])
    dma("sync", bo_sb[:], bo_d[:])
    dma("sync", sel_s[:], sel_d[:])
    dma("sync", rid_s[:], runid_d[:])
    dma("sync", eid_s[:], eid_d[:])
    v_consts = P.now("sq")
    v_xb = dma("gpsimd", xb[:].rearrange("p (k t) -> p k t", k=NKT),
               x_own.rearrange("(k p) t -> p k t", p=128))

    def _memsets(e):
        if DEBUG:
            e.memset(big1[:], 0.0)
        e.memset(ones_c[:], 1.0)
        e.memset(ones_r32[:], 1.0)
        e.memset(eps1[:], 1e-5)
        va = vpad[:].rearrange("p (b h c) -> p b h c", b=2, c=65)
        return e.memset(va[:, :, :, 64:65], 1.0)
    v_memset = vec(_memsets, inc=True)

    greg = {}

    def gp_regs(e):
        cid = e.partition_id()
        greg["e"] = cid % 2
        greg["peer"] = 1 - cid % 2
    P.em("gpsimd", gp_regs)

    def _lazy_regs(e):
        if "ebgu" in greg:
            return
        ev = e.value_load(eid_s[0:1, 0:1], min_val=0, max_val=1)
        er = e.to_reg(ev)
        ebgu = e.to_reg(0)
        e.reg_mul(ebgu, er, 1024)
        greg["ebgu"] = ebgu
        ebd = e.to_reg(0)
        e.reg_mul(ebd, er, 2816)
        greg["ebd"] = ebd
        greg["rowoff"] = e.to_reg(0)

    def dyn_rows(e, base_reg, const_rows, max_start):
        _lazy_regs(e)
        e.reg_alu(greg["rowoff"], greg[base_reg], const_rows, OP.add)
        return bass.DynSlice(
            RuntimeValue(val=greg["rowoff"], min_val=0, max_val=max_start),
            128)

    sreg = {}

    def sy_regs(e):
        cid = e.partition_id()
        sreg["e"] = cid % 2
    P.em("sync", sy_regs)

    gstate = {"n": 0, "done": 0}
    v_job_g = {}

    def emit_gather_job(mat, kt, c0, width):
        j = gstate["n"]
        gstate["n"] += 1
        par = j % 2
        idx_t = gsc_i[:, JOBW * par:JOBW * par + width]
        wac = gsc_w[:, JOBW * par:JOBW * par + width]
        src = {"g": gidx_d, "u": uidx_d, "d": didx_d}[mat]
        w("sync", GSEM, v_job_g.get(j - 2, 0))
        v_idx = dma("sync", idx_t,
                    src[128 * kt:128 * kt + 128, c0:c0 + width])
        w("vector", "sq", v_idx)
        w("vector", "sq", gstate.get(f"st{j - 2}", 0))

        def _gather(e, it=idx_t, wa=wac, wd=width):
            e.tensor_copy(gsc_b[:, 0:wd], it)
            if not SUB_GDV:
                return e.tensor_copy(wa, gsc_b[:, 0:wd])
            e.tensor_scalar(wa, gsc_b[:, 0:wd], 0.0, lutbc[:, 0:1],
                            OP.is_equal, OP.mult)
            for jj in range(1, 15):
                e.tensor_scalar(gsc_t[:, 0:wd], gsc_b[:, 0:wd], float(jj),
                                lutbc[:, jj:jj + 1], OP.is_equal, OP.mult)
                e.tensor_add(wa, wa, gsc_t[:, 0:wd])
            e.tensor_scalar(gsc_t[:, 0:wd], gsc_b[:, 0:wd], 15.0,
                            lutbc[:, 15:16], OP.is_equal, OP.mult)
            return e.tensor_add(wa, wa, gsc_t[:, 0:wd])
        v_job_g[j] = vec(_gather, inc=True, sem=GSEM)
        w("sync", GSEM, v_job_g[j])
        Wloc = {"g": Wloc_g, "u": Wloc_u, "d": Wloc_d}[mat]

        def _store(e, Wloc=Wloc, kt=kt, c0=c0, width=width, wa=wac):
            return e.dma_start(out=Wloc[128 * kt:128 * kt + 128,
                                        c0:c0 + width],
                               in_=wa).then_inc(SQ, 16)
        if SUB_GST:
            gstate[f"st{j}"] = P.inc("sq", 16)
            P.em("sync", _store)

    gate_jobs = [("g", kt, c0, min(JOBW, INTER - c0))
                 for kt in range(8) for c0 in range(0, INTER, JOBW)]
    up_jobs = [("u", kt, c0, min(JOBW, INTER - c0))
               for kt in range(8) for c0 in range(0, INTER, JOBW)]
    down_jobs = [("d", kt, 0, D) for kt in range(22)]
    job_q = gate_jobs + up_jobs + down_jobs
    NG, NU = len(gate_jobs), len(up_jobs)
    flag_after = {NG: 1, NG + NU: 2, NG + NU + len(down_jobs): 3}

    def emit_jobs(n):
        if STAGE < 3:
            return
        for _ in range(n):
            if not job_q or gstate["done"] >= SUB_NJ:
                return
            emit_gather_job(*job_q.pop(0))
            gstate["done"] += 1
            if gstate["done"] in flag_after:
                fi = flag_after[gstate["done"]]
                wnow("gpsimd", "sq")
                w("gpsimd", "sq", v_consts)
                if fi == 1:
                    def _pub(e):
                        return e.dma_start(
                            out=Wg_n[bass.ts(greg["e"], 1024), :],
                            in_=Wloc_g[:, :]).then_inc(GQ, 16)
                elif fi == 2:
                    def _pub(e):
                        return e.dma_start(
                            out=Wu_n[bass.ts(greg["e"], 1024), :],
                            in_=Wloc_u[:, :]).then_inc(GQ, 16)
                else:
                    def _pub(e):
                        return e.dma_start(
                            out=Wd_n[bass.ts(greg["e"], 2816), :],
                            in_=Wloc_d[:, :]).then_inc(GQ, 16)
                P.inc("gq", 16)
                P.em("gpsimd", _pub)
                wnow("gpsimd", "gq")

                def _flag(e, fi=fi):
                    return e.dma_start(
                        out=flags[fi, 0:1, bass.ts(greg["e"], 1)],
                        in_=rid_s[:]).then_inc(GQ, 16)
                P.inc("gq", 16)
                P.em("gpsimd", _flag)

    def emit_all_jobs():
        emit_jobs(len(job_q))

    def emit_poll(fi):
        def _poll(e, fi=fi):
            if "po_polls" not in greg:
                greg["po_polls"] = e.to_reg(0)
                greg["po_tgt"] = e.to_reg(0)
                greg["po_nd"] = e.to_reg(1)
                greg["po_msc"] = e.to_reg(0)
                rr = e.value_load(rid_s[0:1, 0:1])
                greg["po_rid"] = e.to_reg(rr)
            polls, tgt = greg["po_polls"], greg["po_tgt"]
            nd, msc, rreg = greg["po_nd"], greg["po_msc"], greg["po_rid"]

            def cond():
                e.reg_add(polls, polls, 1)
                e.reg_mul(tgt, polls, 16)
                e.dma_start(out=flg_s[:], in_=flags[fi, :, :]).then_inc(PO, 16)
                e.wait_ge(PO, tgt)
                e.reg_mov(nd, 0)
                for cc in range(2):
                    vc = e.value_load(flg_s[0:1, cc:cc + 1])
                    e.reg_alu(msc, vc, rreg, OP.not_equal)
                    e.reg_alu(nd, nd, msc, OP.add)
                return nd
            with e.While(cond):
                pass
            return e.sem_inc(GC, 1)
        v = P.inc("gc", 1)
        P.em("gpsimd", _poll)
        return v

    emit_jobs(4)

    v_evac_v = {}
    v_kvmm = {}
    gch = -1

    def head_slice(h):
        oq, hh = h // 8, h % 8
        return 64 * (hh % 2), 260 * oq + 65 * (hh // 2)

    for oq in range(SUB_OQ):
        wnow("gpsimd", "pe")
        o0 = 512 * oq
        v_wk = dma("gpsimd",
                   wsl[:, 0:8192].rearrange("p (k o) -> p k o", o=512),
                   wk_d[:, o0:o0 + 512].rearrange("(k p) o -> p k o", p=128))
        v_wv = dma("gpsimd",
                   wsl[:, 8192:16384].rearrange("p (k o) -> p k o", o=512),
                   wv_d[:, o0:o0 + 512].rearrange("(k p) o -> p k o", p=128))
        for ch in range(SUB_CH):
            gch += 1
            par = gch % 2
            own = ch < 8
            if own:
                xti = lambda k, ch=ch: xb[:, S_OWN * k + 128 * ch:
                                          S_OWN * k + 128 * ch + 128]
            else:
                fc = ch - 8
                w("gpsimd", "pe", v_kvmm.get(gch - 2, 0))
                v_ck = dma("gpsimd",
                           xck[:, 2048 * par:2048 * par + 2048].rearrange(
                               "p (k t) -> p k t", t=128),
                           x_ext[:, 128 * fc:128 * fc + 128].rearrange(
                               "(k p) t -> p k t", p=128))
                xti = lambda k, par=par: xck[:, 2048 * par + 128 * k:
                                             2048 * par + 128 * k + 128]
            pb_k = B[par]
            pb_v = B[2 + par]
            if ch == 0:
                w("tensor", "gq", max(v_wk, v_wv, v_xb))
            elif not own:
                w("tensor", "gq", v_ck)
            w("tensor", "ac", v_evac_v.get(gch - 2, 0))
            v_pk = mm_seq([(xti(k), wsl[:, 512 * k:512 * k + 512],
                            pb_k[:, :], k == 0, k == NKT - 1)
                           for k in range(NKT)])
            v_pv = mm_seq([(xti(k), wsl[:, 8192 + 512 * k:8192 + 512 * k + 512],
                            pb_v[:, :], k == 0, k == NKT - 1)
                           for k in range(NKT)])
            w("scalar", "pe", v_kvmm.get(gch - 2, 0))
            w("scalar", "pe", v_pk)
            if gch == 0:
                w("scalar", "ve", v_memset)
            act(lambda e, par=par, pb=pb_k: e.activation(
                ktm[:, 512 * par:512 * par + 512], pb[:, :], AF.Relu))
            w("scalar", "pe", v_pv)
            va = vpad[:].rearrange("p (b h c) -> p b h c", b=2, c=65)
            v_evac_v[gch] = act(
                lambda e, par=par, pb=pb_v, va=va: e.activation(
                    va[:, par, :, 0:64],
                    pb[:, :].rearrange("p (h c) -> p h c", c=64),
                    AF.Identity),
                inc=True)

            def emit_kv(cch):
                ppar = (4 * NCH * 0 + cch) % 2
                w("tensor", "ac", v_evac_v[cch])
                kvops = []
                for hh in range(8):
                    r0 = 64 * (hh % 2)
                    kvops.append((
                        ktm[:, 512 * (cch % 2) + 64 * hh:
                            512 * (cch % 2) + 64 * hh + 64],
                        vpad[:, 520 * (cch % 2) + 65 * hh:
                             520 * (cch % 2) + 65 * hh + 65],
                        B[4 + hh // 2][r0:r0 + 64, 0:65],
                        (cch % NCH) == 0, (cch % NCH) == NCH - 1))
                v_kvmm[cch] = mm_seq(kvops)

            if SUB_KV:
                if ch > 0:
                    emit_kv(gch - 1)
                if ch == NCH - 1:
                    emit_kv(gch)
            else:
                v_kvmm[gch] = P.now("pe")
        if SUB_KV and SUB_CH == NCH:
            w("vector", "pe", v_kvmm[gch])
            for b4 in range(3):
                vec(lambda e, oq=oq, b4=b4: e.tensor_copy(
                    kv_mine[:, 260 * oq + 65 * b4:260 * oq + 65 * b4 + 65],
                    B[4 + b4][:, 0:65]))
            v_kvev = vec(lambda e, oq=oq: e.tensor_copy(
                kv_mine[:, 260 * oq + 195:260 * oq + 260],
                B[7][:, 0:65]), inc=True)
            if oq < 3:
                w("tensor", "ve", v_kvev)
        if not SUB_JLATE:
            emit_jobs(12)

    if SUB_EXC >= 1:
        wnow("gpsimd", "ve")

        def _kv_out(e):
            return e.dma_start(out=kvx[bass.ts(greg["e"], 128), :],
                               in_=kv_mine).then_inc(GQ, 16)
        P.inc("gq", 16)
        P.em("gpsimd", _kv_out)
        wnow("gpsimd", "gq")
        w("gpsimd", "sq", v_consts)

        def _kv_flag(e):
            return e.dma_start(out=flags[0, 0:1, bass.ts(greg["e"], 1)],
                               in_=rid_s[:]).then_inc(GQ, 16)
        P.inc("gq", 16)
        P.em("gpsimd", _kv_flag)
    if SUB_EXC >= 2:
        v_kvpoll = emit_poll(0)
    if SUB_EXC >= 3:
        w("gpsimd", "gc", v_kvpoll)

        def _kv_in(e):
            return e.dma_start(out=kv_peer,
                               in_=kvx[bass.ts(greg["peer"], 128), :]
                               ).then_inc(GQ, 16)
        v_peer = P.inc("gq", 16)
        P.em("gpsimd", _kv_in)
        w("vector", "gq", v_peer)
        vec(lambda e: e.tensor_add(kv_mine, kv_mine, kv_peer))
    v_kvfull = vec(lambda e: e.tensor_copy(kv_b[:], kv_mine), inc=True)

    if DEBUG and STAGE == 1:
        wnow("gpsimd", "ve")
        dma("gpsimd", dbg_kv[:], kv_mine)

    qb = h_hi[:, 0:16384]
    if STAGE >= 2:
        v_qev = {}
        wnow("tensor", "ac")
        wnow("tensor", "ve")
        for oh in range(2):
            wnow("gpsimd", "pe")
            v_wq = dma("gpsimd",
                       wsl[:, 0:16384].rearrange("p (k o) -> p k o", o=1024),
                       wq_d[:, 1024 * oh:1024 * oh + 1024].rearrange(
                           "(k p) o -> p k o", p=128))
            w("tensor", "gq", v_wq)
            for t8 in range(8):
                t = 8 * oh + t8
                if t >= SUB_QT:
                    continue
                w("tensor", "ac", v_qev.get(t - 2, 0))
                for sc in range(2):
                    pb = B[2 * (t8 % 2) + sc]
                    v_qmm = mm_seq([(wsl[:, 1024 * k + 128 * t8:
                                         1024 * k + 128 * t8 + 128],
                                     xb[:, S_OWN * k + 512 * sc:
                                        S_OWN * k + 512 * sc + 512],
                                     pb[:, :], k == 0, k == NKT - 1)
                                    for k in range(NKT)])
                    w("scalar", "pe", v_qmm)
                    v_qev[t] = act(lambda e, pb=pb, t=t, sc=sc: e.activation(
                        qb[:, 1024 * t + 512 * sc:1024 * t + 512 * sc + 512],
                        pb[:, :], AF.Relu), inc=True)
        v_q_done = P.now("ac")
        if not SUB_JLATE:
            emit_jobs(8)

    v_spill = {}
    v_apply = {}
    if STAGE >= 2 and SUB_A3 >= 2:
        for sh in range(2):
            qkv_b = xb[:, 0:8192]
            v_nev = {}
            v_ndma = {}
            for h in range(H):
                r0, c0 = head_slice(h)
                pb = B[2 + h % 2]
                if h < 2:
                    w("tensor", "ve", v_kvfull)
                    w("tensor", "ac", v_q_done)
                    if sh == 1:
                        w("tensor", "ve", v_apply.get(0, 0))
                if h >= 2:
                    w("tensor", "ac", v_nev[h - 2])
                mm_n = mm_inc(
                    kv_b[r0:r0 + 64, c0 + 64:c0 + 65],
                    qb[64 * (h % 2):64 * (h % 2) + 64,
                       1024 * (h // 2) + 512 * sh:
                       1024 * (h // 2) + 512 * sh + 512],
                    pb[0:1, :], start=True, stop=True)
                w("scalar", "pe", mm_n)
                w("scalar", "gq", v_ndma.get(h - 2, 0))
                nr = nrow0 if h % 2 == 0 else nrow1
                v_nev[h] = act(lambda e, pb=pb, nr=nr: e.activation(
                    nr[:], pb[0:1, :], AF.Identity), inc=True)
                w("gpsimd", "ac", v_nev[h])
                v_ndma[h] = dma("gpsimd", norms[h:h + 1, :], nr[:])
            wnow("vector", "gq")
            vec(lambda e: e.tensor_scalar(norms[:], norms[:], 1e-6, None,
                                          OP.max))
            vec(lambda e: e.reciprocal(norms[:], norms[:]))
            v_rnb = vec(lambda e: e.tensor_copy(rnb[:], norms[:]), inc=True)
            if SUB_A3 < 3:
                continue

            v_qkvb = {}
            for t in range(NKT):
                pb = B[t % 2]
                pbr = B[5]
                wv_prev = v_qkvb.get(t - 2, 0)
                w("tensor", "ve", wv_prev)
                qkops = []
                for hh in range(2):
                    h = 2 * t + hh
                    r0, c0 = head_slice(h)
                    qkops.append((kv_b[r0:r0 + 64, c0:c0 + 64],
                                  qb[64 * hh:64 * hh + 64,
                                     1024 * t + 512 * sh:
                                     1024 * t + 512 * sh + 512],
                                  pb[64 * hh:64 * hh + 64, :], True, True))
                v_qkvmm = mm_seq(qkops)
                w("tensor", "ve", v_rnb)
                if t >= 1:
                    w("tensor", "ve", v_qkvb.get(t - 1, 0))
                v_bcmm = mm_inc(sel_s[:, 128 * t:128 * t + 128], rnb[:, :],
                                pbr[:, :], start=True, stop=True)
                w("scalar", "pe", v_qkvmm)
                if t >= 1:
                    w("scalar", "ve", v_qkvb.get(t - 1, 0))
                v_qe = act(lambda e, pb=pb: e.activation(
                    qsc[:], pb[:, :], AF.Identity), inc=True)
                w("vector", "ac", v_qe)
                w("vector", "pe", v_bcmm)
                v_qkvb[t] = vec(lambda e, t=t, pbr=pbr: e.tensor_tensor(
                    qkv_b[:, 512 * t:512 * t + 512], qsc[:], pbr[:, :],
                    OP.mult), inc=True)

            if SUB_A3 < 4:
                continue
            wnow("tensor", "ac")
            if sh == 1:
                wnow("vector", "gq")
            v_y1ev = {}
            v_add = {}
            for oh in range(2):
                wnow("gpsimd", "pe")
                v_wo = dma("gpsimd",
                           wsl[:, 0:16384].rearrange("p (k o) -> p k o",
                                                     o=1024),
                           wo_d[:, 1024 * oh:1024 * oh + 1024].rearrange(
                               "(k p) o -> p k o", p=128))
                w("tensor", "gq", v_wo)
                for t8 in range(8):
                    g = 8 * oh + t8
                    pb = B[2 + t8 % 2]
                    w("gpsimd", "ve", v_add.get(g - 2, 0))
                    v_xr = dma("gpsimd",
                               xres[:, 512 * (g % 2):512 * (g % 2) + 512],
                               x_own[128 * g:128 * g + 128,
                                     512 * sh:512 * sh + 512])
                    w("tensor", "ve", v_qkvb[NKT - 1])
                    w("tensor", "ac", v_y1ev.get(g - 2, 0))
                    v_womm = mm_seq([(wsl[:, 1024 * k + 128 * t8:
                                           1024 * k + 128 * t8 + 128],
                                       qkv_b[:, 512 * k:512 * k + 512],
                                       pb[:, :], k == 0, k == NKT - 1)
                                      for k in range(NKT)])
                    w("scalar", "pe", v_womm)
                    w("scalar", "ve", v_add.get(g - 1, 0))
                    v_y1ev[g] = act(lambda e, pb=pb, g=g: e.activation(
                        hsil[:], pb[:, :], AF.Identity,
                        bias=bo_sb[:, g:g + 1]), inc=True)
                    w("vector", "ac", v_y1ev[g])
                    w("vector", "gq", v_xr)
                    v_add[g] = vec(lambda e, g=g: e.tensor_add(
                        big1[:, 512 * g:512 * g + 512], hsil[:],
                        xres[:, 512 * (g % 2):512 * (g % 2) + 512]),
                        inc=True)
                    v_y1b = vec(lambda e, g=g: e.tensor_copy(
                        y1b[:], big1[:, 512 * g:512 * g + 512]), inc=True)
                    w("scalar", "ve", v_y1b)
                    v_sqa = act(lambda e: e.activation(
                        y1sq[:], y1b[:], AF.Square), inc=True)
                    w("tensor", "ve", v_y1b)
                    mm(ones_c[:], y1b[:], B[6][0:1, :],
                       start=(g == 0), stop=(g == 15))
                    w("tensor", "ac", v_sqa)
                    v_stmm = mm_inc(ones_c[:], y1sq[:], B[7][0:1, :],
                                    start=(g == 0), stop=(g == 15))
                    w("vector", "pe", v_stmm)
            if SUB_A3 < 5:
                v_apply[sh] = P.now("ve")
                v_spill[sh] = P.now("gq")
                continue
            v_stats = P.now("pe")
            w("vector", "pe", v_stats)
            vec(lambda e: e.tensor_scalar(s_mean[:], B[6][0:1, :],
                                          1.0 / D, None, OP.mult))
            vec(lambda e: e.tensor_scalar(s_sq[:], B[7][0:1, :],
                                          1.0 / D, None, OP.mult))
            vec(lambda e: e.tensor_mul(s_var[:], s_mean[:], s_mean[:]))
            v_var = vec(lambda e: e.tensor_sub(s_var[:], s_sq[:], s_var[:]),
                        inc=True)
            w("scalar", "ve", v_var)
            act(lambda e: e.activation(s_rstd[:], s_var[:], AF.Ln,
                                       bias=eps1[:]))
            v_rstd = act(lambda e: e.activation(s_rstd[:], s_rstd[:],
                                                AF.Exp, scale=-0.5),
                         inc=True)
            w("tensor", "ac", v_rstd)
            w("tensor", "ve", v_var)
            mm(ones_r32[:], s_mean[:], B[2][:, :], start=True, stop=True)
            v_bc = mm_inc(ones_r32[:], s_rstd[:], B[3][:, :],
                          start=True, stop=True)
            w("vector", "pe", v_bc)
            last = None
            for g in range(NKT):
                vec(lambda e, g=g: e.tensor_sub(
                    hsil[:], big1[:, 512 * g:512 * g + 512], B[2][:, :]))
                last = vec(lambda e, g=g: e.tensor_mul(
                    big1[:, 512 * g:512 * g + 512], hsil[:], B[3][:, :]),
                    inc=True)
            v_apply[sh] = last
            w("gpsimd", "ve", v_apply[sh])
            v_spill[sh] = dma("gpsimd",
                              x1_spill[sh, :, :].rearrange(
                                  "(k p) t -> p k t", p=128),
                              big1[:].rearrange("p (k t) -> p k t", t=512))
            if DEBUG and STAGE == 2:
                dma("gpsimd", dbg_x1[:, 8192 * sh:8192 * sh + 8192],
                    big1[:])
            if not SUB_JLATE:
                emit_jobs(7)

    emit_all_jobs()

    if STAGE >= 4 and SUB_B >= 1:
        v_pollg = emit_poll(1)
        v_pollu = emit_poll(2)
        v_polld = emit_poll(3)

        def h_of(k):
            if k < 16:
                return xb[:, 8192 + 512 * k:8192 + 512 * k + 512]
            return h_hi[:, 512 * (k - 16):512 * (k - 16) + 512]

        v_out_store = 0
        for sh in (range(2) if SUB_B >= 2 else []):
            if sh == 1:
                w("sync", "gq", v_out_store)
            w("sync", "gq", v_spill[sh])
            v_x1r = dma("sync", big1[:].rearrange("p (k t) -> p k t", t=512),
                        x1_spill[sh, :, :].rearrange("(k p) t -> p k t",
                                                     p=128))
            x1b_sh = xb[:, 0:8192]
            w("gpsimd", "gq", v_spill[1])
            wnow("gpsimd", "pe")
            v_x1bc = dma("gpsimd",
                         x1b_sh.rearrange("p (k t) -> p k t", t=512),
                         x1_spill[sh, :, :].rearrange("(k p) t -> p k t",
                                                      p=128))
            if SUB_B < 3:
                w("gpsimd", "gq", v_x1bc)
                v_out_store = dma(
                    "gpsimd",
                    out_d.rearrange("(k p) t -> p k t", p=128)[
                        :, :, 512 * sh:512 * sh + 512],
                    big1[:].rearrange("p (k t) -> p k t", t=512))
                v_out_ve = P.now("ve")
                continue
            w("sync", "gc", v_pollu)
            w("sync", "gc", v_pollg)
            NM = 22
            GSL = [wsl[:, 0:4096], wsl[:, 4096:8192]]
            USL = [wsl[:, 8192:12288], wsl[:, 12288:16384]]
            v_ht = {}
            v_pe_after = {}
            MACROS = [(m, min(256, INTER - 256 * m)) for m in range(NM)]

            def plan_b1_load(m):
                mw = MACROS[m][1]
                mpar = m % 2
                w("sync", "pe", v_pe_after.get(m - 2, P.now("pe")
                                               if m < 2 else 0))
                vg_ = 0
                vu_ = 0
                for k in range(NKT):
                    vg_ = dma("sync", GSL[mpar][:, 256 * k:256 * k + mw],
                              Wg_n[128 * k:128 * k + 128,
                                   256 * m:256 * m + mw])
                    vu_ = dma("sync", USL[mpar][:, 256 * k:256 * k + mw],
                              Wu_n[128 * k:128 * k + 128,
                                   256 * m:256 * m + mw])
                return max(vg_, vu_)

            v_msl = {0: plan_b1_load(0), 1: plan_b1_load(1)}
            for m, mw in MACROS:
                mpar = m % 2
                nt = (mw + 127) // 128
                for otl in range(nt):
                    ot = 2 * m + otl
                    par = ot % 2
                    pbg = B[par]
                    pbu = B[2 + par]
                    w("tensor", "sq", v_msl[m])
                    w("tensor", "gq", v_x1bc)
                    if sh == 1 or ot >= 2:
                        w("tensor", "ve", v_ht.get(ot - 2, 0))
                    if ot == 0:
                        w("tensor", "ve", v_apply[1] if sh == 0
                          else v_out_ve)
                    v_gmm = mm_seq([(GSL[mpar][:, 256 * k + 128 * otl:
                                               256 * k + 128 * otl + 128],
                                     x1b_sh[:, 512 * k:512 * k + 512],
                                     pbg[:, :], k == 0, k == NKT - 1)
                                    for k in range(NKT)])
                    v_umm = mm_seq([(USL[mpar][:, 256 * k + 128 * otl:
                                               256 * k + 128 * otl + 128],
                                     x1b_sh[:, 512 * k:512 * k + 512],
                                     pbu[:, :], k == 0, k == NKT - 1)
                                    for k in range(NKT)])
                    w("scalar", "pe", v_gmm)
                    w("scalar", "ve", v_ht.get(ot - 1, 0))
                    v_sil = act(lambda e, pbg=pbg: e.activation(
                        hsil[:], pbg[:, :], AF.Silu), inc=True)
                    w("vector", "ac", v_sil)
                    w("vector", "pe", v_umm)
                    v_ht[ot] = vec(lambda e, ot=ot, pbu=pbu: e.tensor_tensor(
                        h_of(ot), hsil[:], pbu[:, :], OP.mult), inc=True)
                v_pe_after[m] = P.now("pe")
                if m + 2 < NM:
                    v_msl[m + 2] = plan_b1_load(m + 2)
            v_h_done = P.now("ve")
            if SUB_B < 4:
                w("gpsimd", "ve", v_h_done)
                v_out_store = dma(
                    "gpsimd",
                    out_d.rearrange("(k p) t -> p k t", p=128)[
                        :, :, 512 * sh:512 * sh + 512],
                    big1[:].rearrange("p (k t) -> p k t", t=512))
                v_out_ve = P.now("ve")
                continue
            if sh == 0:
                w("sync", "gc", v_polld)
                for ot2 in range(16):
                    v_rp = dma("sync", Wd4[ot2, :, :, :],
                               Wd_n[:, 128 * ot2:128 * ot2 + 128].rearrange(
                                   "(k p) o -> p k o", p=128))
                v_repack = v_rp
            DS = [wsl[:, 0:5632], wsl[:, 5632:11264]]
            v_dsl = {}
            v_pe_d = {}

            def plan_b2_load(ot2):
                par = ot2 % 2
                if ot2 < 2:
                    w("sync", "pe", v_pe_after[21])
                    w("sync", "sq", v_repack)
                else:
                    w("sync", "pe", v_pe_d[ot2 - 2])
                v_dsl[ot2] = dma("sync", DS[par].rearrange(
                    "p (k o) -> p k o", o=128), Wd4[ot2, :, :, :])

            plan_b2_load(0)
            plan_b2_load(1)
            v_y2b_prev = 0
            for ot2 in range(16):
                par = ot2 % 2
                pbd = B[4 + par]
                w("tensor", "sq", v_dsl[ot2])
                w("tensor", "ve", v_h_done)
                if ot2 == 0:
                    wnow("tensor", "ve")
                w("tensor", "ve", v_y2b_prev)
                v_dmm = mm_seq([(DS[par][:, 128 * k:128 * k + 128],
                                 h_of(k), pbd[:, :], k == 0, k == 42)
                                for k in range(43)])
                v_pe_d[ot2] = v_dmm
                if ot2 + 2 <= 15:
                    plan_b2_load(ot2 + 2)
                w("vector", "pe", v_dmm)
                w("vector", "sq", v_x1r)
                vec(lambda e, ot2=ot2, pbd=pbd: e.tensor_add(
                    big1[:, 512 * ot2:512 * ot2 + 512],
                    big1[:, 512 * ot2:512 * ot2 + 512], pbd[:, :]))
                v_y2b = vec(lambda e, ot2=ot2: e.tensor_copy(
                    y1b[:], big1[:, 512 * ot2:512 * ot2 + 512]), inc=True)
                v_y2b_prev = v_y2b
                w("scalar", "ve", v_y2b)
                v_sqa = act(lambda e: e.activation(
                    y1sq[:], y1b[:], AF.Square), inc=True)
                w("tensor", "ve", v_y2b)
                mm(ones_c[:], y1b[:], B[6][0:1, :],
                   start=(ot2 == 0), stop=(ot2 == 15))
                w("tensor", "ac", v_sqa)
                v_stmm = mm_inc(ones_c[:], y1sq[:], B[7][0:1, :],
                                start=(ot2 == 0), stop=(ot2 == 15))
                w("vector", "pe", v_stmm)
            v_stats = P.now("pe")
            w("vector", "pe", v_stats)
            vec(lambda e: e.tensor_scalar(s_mean[:], B[6][0:1, :],
                                          1.0 / D, None, OP.mult))
            vec(lambda e: e.tensor_scalar(s_sq[:], B[7][0:1, :],
                                          1.0 / D, None, OP.mult))
            vec(lambda e: e.tensor_mul(s_var[:], s_mean[:], s_mean[:]))
            v_var = vec(lambda e: e.tensor_sub(s_var[:], s_sq[:], s_var[:]),
                        inc=True)
            w("scalar", "ve", v_var)
            act(lambda e: e.activation(s_rstd[:], s_var[:], AF.Ln,
                                       bias=eps1[:]))
            v_rstd = act(lambda e: e.activation(s_rstd[:], s_rstd[:],
                                                AF.Exp, scale=-0.5),
                         inc=True)
            w("tensor", "ac", v_rstd)
            w("tensor", "ve", v_var)
            mm(ones_r32[:], s_mean[:], B[2][:, :], start=True, stop=True)
            v_bc = mm_inc(ones_r32[:], s_rstd[:], B[3][:, :],
                          start=True, stop=True)
            w("vector", "pe", v_bc)
            last = None
            for g in range(NKT):
                vec(lambda e, g=g: e.tensor_sub(
                    hsil[:], big1[:, 512 * g:512 * g + 512], B[2][:, :]))
                last = vec(lambda e, g=g: e.tensor_mul(
                    big1[:, 512 * g:512 * g + 512], hsil[:], B[3][:, :]),
                    inc=True)
            v_out_ve = last
            w("gpsimd", "ve", v_out_ve)
            v_out_store = dma(
                "gpsimd",
                out_d.rearrange("(k p) t -> p k t", p=128)[
                    :, :, 512 * sh:512 * sh + 512],
                big1[:].rearrange("p (k t) -> p k t", t=512))

    if STAGE >= 4 and SUB_B < 2:
        wnow("gpsimd", "ve")
        dma("gpsimd", out_d.rearrange("(k p) t -> p k t", p=128)[:, :, 0:512],
            big1[:].rearrange("p (k t) -> p k t", t=512))
    P.em("gpsimd", lambda e: e.wait_ge(GQ, P.now("gq")))
    P.em("gpsimd", lambda e: e.wait_ge(SQ, P.now("sq")))
    P.em("gpsimd", lambda e: e.wait_ge(VE, P.now("ve")))
    P.em("gpsimd", lambda e: e.wait_ge(VG, P.now("vg")))
    P.em("gpsimd", lambda e: e.wait_ge(AC, P.now("ac")))
    P.em("gpsimd", lambda e: e.wait_ge(PE_, P.now("pe")))

    with es:
        with nc.Block() as block:
            @block.sync
            def _(eng):
                for f in P.ops["sync"]:
                    f(eng)

            @block.gpsimd
            def _(eng):
                for f in P.ops["gpsimd"]:
                    f(eng)

            @block.vector
            def _(eng):
                for f in P.ops["vector"]:
                    f(eng)

            @block.scalar
            def _(eng):
                for f in P.ops["scalar"]:
                    f(eng)

            @block.tensor
            def _(eng):
                for f in P.ops["tensor"]:
                    f(eng)

    return nc


def make_inputs(L0_attn_hidden_states, L0_attn_Wq, L0_attn_Wk, L0_attn_Wv,
                L0_attn_Wo, L0_attn_bo, L0_ffn_lut, L0_ffn_gate_idx,
                L0_ffn_up_idx, L0_ffn_down_idx):
    x = np.asarray(L0_attn_hidden_states)[0]
    xT = np.ascontiguousarray(x.T)
    lut_bc = np.tile(np.asarray(L0_ffn_lut, np.float32)[None, :], (128, 1))
    bo = np.ascontiguousarray(np.asarray(L0_attn_bo, np.float32).reshape(16, 128).T)
    sel = np.zeros((32, D), ml_dtypes.bfloat16)
    for t in range(16):
        sel[2 * t, 128 * t:128 * t + 64] = 1
        sel[2 * t + 1, 128 * t + 64:128 * t + 128] = 1
    gT = np.ascontiguousarray(np.asarray(L0_ffn_gate_idx).T).astype(np.uint8)
    uT = np.ascontiguousarray(np.asarray(L0_ffn_up_idx).T).astype(np.uint8)
    dT = np.ascontiguousarray(np.asarray(L0_ffn_down_idx).T).astype(np.uint8)
    dT_pad = np.zeros((2 * 2816, D), np.uint8)
    dT_pad[0:INTER, :] = dT
    runid = np.random.randint(1, 2**30, dtype=np.int32)
    wq = np.ascontiguousarray(np.asarray(L0_attn_Wq, np.float32))
    wk = np.ascontiguousarray(np.asarray(L0_attn_Wk, np.float32))
    wv = np.ascontiguousarray(np.asarray(L0_attn_Wv, np.float32))
    wo = np.ascontiguousarray(np.asarray(L0_attn_Wo, np.float32))
    in_maps = []
    for c in range(NCORE):
        p, e = c // 2, c % 2
        own = slice(1024 * c, 1024 * c + 1024)
        pair_lo, pair_hi = 2048 * p, 2048 * p + 2048
        comp = np.r_[0:pair_lo, pair_hi:8192]
        extra = comp[:3072] if e == 0 else comp[3072:]
        in_maps.append({
            "x_own": np.ascontiguousarray(xT[:, own]),
            "x_ext": np.ascontiguousarray(xT[:, extra]),
            "wq": wq, "wk": wk, "wv": wv, "wo": wo,
            "bo": bo, "lut": lut_bc, "sel": sel,
            "gidx": np.ascontiguousarray(gT[1024 * e:1024 * e + 1024, :]),
            "uidx": np.ascontiguousarray(uT[1024 * e:1024 * e + 1024, :]),
            "didx": np.ascontiguousarray(dT_pad[2816 * e:2816 * e + 2816, :]),
            "runid": np.full((1, 1), runid, np.int32),
            "eid": np.full((1, 1), e, np.int32),
        })
    return in_maps


def kernel(**inputs):
    in_maps = make_inputs(**inputs)
    nc = build_program()
    res = run_bass_kernel_spmd(nc, in_maps, core_ids=list(range(NCORE)))
    out = np.empty((1, 8192, D), np.float32)
    for c in range(NCORE):
        out[0, 1024 * c:1024 * c + 1024, :] = \
            np.asarray(res.results[c]["out"], np.float32).T
    return out


# version 03
# speedup vs baseline: 1.5192x; 1.0361x over previous
import numpy as np
import ml_dtypes
import contextlib
import concourse.bass as bass
import concourse.mybir as mybir
from concourse.bass_utils import run_bass_kernel_spmd
from concourse.expressions import RuntimeValue

F32 = mybir.dt.float32
BF16 = mybir.dt.bfloat16
U8 = mybir.dt.uint8
I32 = mybir.dt.int32
AF = mybir.ActivationFunctionType
OP = mybir.AluOpType

import os
STAGE = int(os.environ.get("MK_STAGE", "4"))
SUB_OQ = int(os.environ.get("MK_OQ", "4"))
SUB_CH = int(os.environ.get("MK_CH", "32"))
SUB_KV = int(os.environ.get("MK_KV", "1"))
SUB_EXC = int(os.environ.get("MK_EXC", "3"))
SUB_VEV = int(os.environ.get("MK_VEV", "1"))
SUB_A3 = int(os.environ.get("MK_A3", "5"))
SUB_QT = int(os.environ.get("MK_QT", "16"))
SUB_NJ = int(os.environ.get("MK_NJ", "999"))
SUB_GST = int(os.environ.get("MK_GST", "1"))
SUB_GDV = int(os.environ.get("MK_GDV", "1"))
SUB_B = int(os.environ.get("MK_B", "9"))
SUB_JLATE = int(os.environ.get("MK_JLATE", "0"))
SITE_A1 = int(os.environ.get("MK_SA1", "1"))
SITE_QK = int(os.environ.get("MK_SQK", "1"))
SITE_WO = int(os.environ.get("MK_SWO", "1"))
GSEM = os.environ.get("MK_GSEM", "vg")
SUB_SMODE = int(os.environ.get("MK_SMODE", "0"))
DEBUG = bool(int(os.environ.get("MK_DBG", "1")))
NCORE = 8

D = 2048
S_OWN = 1024
NCH = 32
INTER = 5504
NKT = 16
H = 32
JOBW = 2048


class Plan:
    def __init__(self):
        self.ops = {"sync": [], "gpsimd": [], "vector": [], "scalar": [],
                    "tensor": []}
        self.cnt = {}

    def em(self, eng, fn):
        self.ops[eng].append(fn)

    def inc(self, sem, n=1):
        self.cnt[sem] = self.cnt.get(sem, 0) + n
        return self.cnt[sem]

    def now(self, sem):
        return self.cnt.get(sem, 0)


def build_program():
    nc = bass.Bass("TRN2", target_bir_lowering=False, debug=False,
                   detect_race_conditions=bool(int(os.environ.get("MK_RACE", "0"))))
    P = Plan()

    dt = lambda n, s, d, k="ExternalInput", **kw: nc.dram_tensor(
        n, s, d, kind=k, **kw).ap()
    x_own = dt("x_own", [D, S_OWN], F32)
    x_ext = dt("x_ext", [D, 3072], F32)
    wq_d = dt("wq", [D, D], F32)
    wk_d = dt("wk", [D, D], F32)
    wv_d = dt("wv", [D, D], F32)
    wo_d = dt("wo", [D, D], F32)
    bo_d = dt("bo", [128, 16], F32)
    lut_d = dt("lut", [128, 16], F32)
    sel_d = dt("sel", [32, D], BF16)
    gidx_d = dt("gidx", [1024, INTER], U8)
    uidx_d = dt("uidx", [1024, INTER], U8)
    didx_d = dt("didx", [2816, D], U8)
    runid_d = dt("runid", [1, 1], I32)
    eid_d = dt("eid", [1, 1], I32)
    out_d = dt("out", [D, S_OWN], F32, k="ExternalOutput")
    if DEBUG and STAGE == 1:
        dbg_kv = dt("dbg_kv", [128, 1040], F32, k="ExternalOutput")
    if DEBUG and STAGE == 2:
        dbg_x1 = dt("dbg_x1", [128, 16 * S_OWN], F32, k="ExternalOutput")

    x1_spill = dt("x1_spill", [2, D, 512], F32, k="Internal")
    Wloc_g = dt("Wloc_g", [1024, INTER], BF16, k="Internal")
    Wloc_u = dt("Wloc_u", [1024, INTER], BF16, k="Internal")
    Wloc_d = dt("Wloc_d", [2816, D], BF16, k="Internal")
    shd = lambda n, s, d: nc.dram_tensor(n, s, d, kind="Internal",
                                         addr_space="Shared").ap()
    Wg_n = shd("Wg_n", [2048, INTER], BF16)
    Wu_n = shd("Wu_n", [2048, INTER], BF16)
    Wd_n = shd("Wd_n", [5632, 2048], BF16)
    Wd4 = shd("Wd4", [16, 128, 44, 128], BF16)
    kvx = shd("kvx", [256, 1040], F32)
    flags = shd("flags", [4, 1, 2], I32)

    es = contextlib.ExitStack()
    sb = lambda n, s, d: es.enter_context(nc.sbuf_tensor(n, s, d))

    xb = sb("xb", [128, NKT * S_OWN], BF16)
    h_hi = sb("h_hi", [128, 16384], BF16)
    wsl = sb("wsl", [128, 16384], BF16)
    xck = sb("xck", [128, 2 * 2048], BF16)
    ktm = sb("ktm", [128, 2 * 512], BF16)
    vpad = sb("vpad", [128, 2 * 520], BF16)
    kv_b = sb("kv_b", [128, 1040], BF16)
    gsc_i = sb("gsc_i", [128, 2 * JOBW], U8)
    gsc_b = sb("gsc_b", [128, JOBW], BF16)
    gsc_w = sb("gsc_w", [128, 2 * JOBW], BF16)
    gsc_t = sb("gsc_t", [128, JOBW], BF16)
    sel_s = sb("sel_s", [32, D], BF16)
    y1b = sb("y1b", [128, 512], BF16)
    y1sq = sb("y1sq", [128, 512], BF16)
    rnb = sb("rnb", [32, 512], BF16)
    ones_c = sb("ones_c", [128, 1], BF16)
    ones_r32 = sb("ones_r32", [1, 128], F32)

    big1 = sb("big1", [128, 8192], F32)
    qsc = sb("qsc", [128, 512], F32)
    hsil = sb("hsil", [128, 512], F32)
    xres = sb("xres", [128, 2 * 512], F32)
    nrow0 = sb("nrow0", [1, 512], F32)
    nrow1 = sb("nrow1", [1, 512], F32)
    norms = sb("norms", [32, 512], F32)
    s_mean = sb("s_mean", [1, 512], F32)
    s_sq = sb("s_sq", [1, 512], F32)
    s_var = sb("s_var", [1, 512], F32)
    s_rstd = sb("s_rstd", [1, 512], F32)
    eps1 = sb("eps1", [1, 1], F32)
    lutbc = sb("lutbc", [128, 16], F32)
    bo_sb = sb("bo_sb", [128, 16], F32)
    rid_s = sb("rid_s", [1, 1], I32)
    eid_s = sb("eid_s", [1, 1], I32)
    flg_s = sb("flg_s", [1, 2], I32)

    kv_mine = big1[:, 0:1040]
    kv_peer = big1[:, 1040:2080]

    B = [es.enter_context(nc.psum_tensor(f"pb{i}", [128, 512], F32))
         for i in range(8)]

    sems = {}
    for name in ("sq", "gq", "pe", "ac", "ve", "vg", "gc", "po"):
        sems[name] = es.enter_context(nc.semaphore(name))
    SQ, GQ, PE_, AC, VE, VG, GC, PO = (sems[k] for k in
                                       ("sq", "gq", "pe", "ac", "ve", "vg",
                                        "gc", "po"))

    def w(eng, sem, val):
        if val <= 0:
            return
        s = sems[sem]
        P.em(eng, lambda e, s=s, v=val: e.wait_ge(s, v))

    def wnow(eng, sem):
        w(eng, sem, P.now(sem))

    def dma(q, out, in_):
        sem = "sq" if q == "sync" else "gq"
        v = P.inc(sem, 16)
        s = sems[sem]
        P.em(q, lambda e, o=out, i=in_, s=s: e.dma_start(out=o, in_=i
                                                         ).then_inc(s, 16))
        return v

    def mm(lhsT, rhs, out, start, stop):
        P.em("tensor", lambda e, l=lhsT, r=rhs, o=out, st=start, sp=stop:
             e.matmul(o, l, r, start=st, stop=sp, skip_group_check=True))

    def mm_inc(lhsT, rhs, out, start, stop):
        v = P.inc("pe", 1)
        P.em("tensor", lambda e, l=lhsT, r=rhs, o=out, st=start, sp=stop:
             e.matmul(o, l, r, start=st, stop=sp, skip_group_check=True
                      ).then_inc(PE_, 1))
        return v

    def mm_seq(ops_list):
        for (l, r, o, st, sp) in ops_list[:-1]:
            mm(l, r, o, st, sp)
        l, r, o, st, sp = ops_list[-1]
        return mm_inc(l, r, o, st, sp)

    def act(fn, inc=False):
        if inc:
            v = P.inc("ac", 1)
            P.em("scalar", lambda e, f=fn: f(e).then_inc(AC, 1))
            return v
        P.em("scalar", fn)

    def vec(fn, inc=False, sem="ve"):
        if inc:
            v = P.inc(sem, 1)
            s = sems[sem]
            P.em("vector", lambda e, f=fn, s=s: f(e).then_inc(s, 1))
            return v
        P.em("vector", fn)

    dma("sync", lutbc[:], lut_d[:])
    dma("sync", bo_sb[:], bo_d[:])
    dma("sync", sel_s[:], sel_d[:])
    dma("sync", rid_s[:], runid_d[:])
    dma("sync", eid_s[:], eid_d[:])
    v_consts = P.now("sq")
    v_xb = dma("gpsimd", xb[:].rearrange("p (k t) -> p k t", k=NKT),
               x_own.rearrange("(k p) t -> p k t", p=128))

    def _memsets(e):
        if DEBUG:
            e.memset(big1[:], 0.0)
        e.memset(ones_c[:], 1.0)
        e.memset(ones_r32[:], 1.0)
        e.memset(eps1[:], 1e-5)
        va = vpad[:].rearrange("p (b h c) -> p b h c", b=2, c=65)
        return e.memset(va[:, :, :, 64:65], 1.0)
    v_memset = vec(_memsets, inc=True)

    greg = {}

    def gp_regs(e):
        cid = e.partition_id()
        greg["e"] = cid % 2
        greg["peer"] = 1 - cid % 2
    P.em("gpsimd", gp_regs)

    def _lazy_regs(e):
        if "ebgu" in greg:
            return
        ev = e.value_load(eid_s[0:1, 0:1], min_val=0, max_val=1)
        er = e.to_reg(ev)
        ebgu = e.to_reg(0)
        e.reg_mul(ebgu, er, 1024)
        greg["ebgu"] = ebgu
        ebd = e.to_reg(0)
        e.reg_mul(ebd, er, 2816)
        greg["ebd"] = ebd
        greg["rowoff"] = e.to_reg(0)

    def dyn_rows(e, base_reg, const_rows, max_start):
        _lazy_regs(e)
        e.reg_alu(greg["rowoff"], greg[base_reg], const_rows, OP.add)
        return bass.DynSlice(
            RuntimeValue(val=greg["rowoff"], min_val=0, max_val=max_start),
            128)

    sreg = {}

    def sy_regs(e):
        cid = e.partition_id()
        sreg["e"] = cid % 2
    P.em("sync", sy_regs)

    gstate = {"n": 0, "done": 0}
    v_job_g = {}

    def emit_gather_job(mat, kt, c0, width):
        j = gstate["n"]
        gstate["n"] += 1
        par = j % 2
        idx_t = gsc_i[:, JOBW * par:JOBW * par + width]
        wac = gsc_w[:, JOBW * par:JOBW * par + width]
        src = {"g": gidx_d, "u": uidx_d, "d": didx_d}[mat]
        w("sync", GSEM, v_job_g.get(j - 2, 0))
        v_idx = dma("sync", idx_t,
                    src[128 * kt:128 * kt + 128, c0:c0 + width])
        w("vector", "sq", v_idx)
        w("vector", "sq", gstate.get(f"st{j - 2}", 0))

        def _gather(e, it=idx_t, wa=wac, wd=width):
            e.tensor_copy(gsc_b[:, 0:wd], it)
            if not SUB_GDV:
                return e.tensor_copy(wa, gsc_b[:, 0:wd])
            e.tensor_scalar(wa, gsc_b[:, 0:wd], 0.0, lutbc[:, 0:1],
                            OP.is_equal, OP.mult)
            for jj in range(1, 15):
                e.tensor_scalar(gsc_t[:, 0:wd], gsc_b[:, 0:wd], float(jj),
                                lutbc[:, jj:jj + 1], OP.is_equal, OP.mult)
                e.tensor_add(wa, wa, gsc_t[:, 0:wd])
            e.tensor_scalar(gsc_t[:, 0:wd], gsc_b[:, 0:wd], 15.0,
                            lutbc[:, 15:16], OP.is_equal, OP.mult)
            return e.tensor_add(wa, wa, gsc_t[:, 0:wd])
        v_job_g[j] = vec(_gather, inc=True, sem=GSEM)
        w("sync", GSEM, v_job_g[j])
        Wloc = {"g": Wloc_g, "u": Wloc_u, "d": Wloc_d}[mat]

        def _store(e, Wloc=Wloc, kt=kt, c0=c0, width=width, wa=wac):
            return e.dma_start(out=Wloc[128 * kt:128 * kt + 128,
                                        c0:c0 + width],
                               in_=wa).then_inc(SQ, 16)
        if SUB_GST:
            gstate[f"st{j}"] = P.inc("sq", 16)
            P.em("sync", _store)

    gate_jobs = [("g", kt, c0, min(JOBW, INTER - c0))
                 for kt in range(8) for c0 in range(0, INTER, JOBW)]
    up_jobs = [("u", kt, c0, min(JOBW, INTER - c0))
               for kt in range(8) for c0 in range(0, INTER, JOBW)]
    down_jobs = [("d", kt, 0, D) for kt in range(22)]
    job_q = gate_jobs + up_jobs + down_jobs
    NG, NU = len(gate_jobs), len(up_jobs)
    flag_after = {NG: 1, NG + NU: 2, NG + NU + len(down_jobs): 3}

    def emit_jobs(n):
        if STAGE < 3:
            return
        for _ in range(n):
            if not job_q or gstate["done"] >= SUB_NJ:
                return
            emit_gather_job(*job_q.pop(0))
            gstate["done"] += 1
            if gstate["done"] in flag_after:
                fi = flag_after[gstate["done"]]
                wnow("gpsimd", "sq")
                w("gpsimd", "sq", v_consts)
                if fi == 1:
                    def _pub(e):
                        return e.dma_start(
                            out=Wg_n[bass.ts(greg["e"], 1024), :],
                            in_=Wloc_g[:, :]).then_inc(GQ, 16)
                elif fi == 2:
                    def _pub(e):
                        return e.dma_start(
                            out=Wu_n[bass.ts(greg["e"], 1024), :],
                            in_=Wloc_u[:, :]).then_inc(GQ, 16)
                else:
                    def _pub(e):
                        return e.dma_start(
                            out=Wd_n[bass.ts(greg["e"], 2816), :],
                            in_=Wloc_d[:, :]).then_inc(GQ, 16)
                P.inc("gq", 16)
                P.em("gpsimd", _pub)
                wnow("gpsimd", "gq")

                def _flag(e, fi=fi):
                    return e.dma_start(
                        out=flags[fi, 0:1, bass.ts(greg["e"], 1)],
                        in_=rid_s[:]).then_inc(GQ, 16)
                P.inc("gq", 16)
                P.em("gpsimd", _flag)

    def emit_all_jobs():
        emit_jobs(len(job_q))

    def emit_poll(fi):
        def _poll(e, fi=fi):
            if "po_polls" not in greg:
                greg["po_polls"] = e.to_reg(0)
                greg["po_tgt"] = e.to_reg(0)
                greg["po_nd"] = e.to_reg(1)
                greg["po_msc"] = e.to_reg(0)
                rr = e.value_load(rid_s[0:1, 0:1])
                greg["po_rid"] = e.to_reg(rr)
            polls, tgt = greg["po_polls"], greg["po_tgt"]
            nd, msc, rreg = greg["po_nd"], greg["po_msc"], greg["po_rid"]

            def cond():
                e.reg_add(polls, polls, 1)
                e.reg_mul(tgt, polls, 16)
                e.dma_start(out=flg_s[:], in_=flags[fi, :, :]).then_inc(PO, 16)
                e.wait_ge(PO, tgt)
                e.reg_mov(nd, 0)
                for cc in range(2):
                    vc = e.value_load(flg_s[0:1, cc:cc + 1])
                    e.reg_alu(msc, vc, rreg, OP.not_equal)
                    e.reg_alu(nd, nd, msc, OP.add)
                return nd
            with e.While(cond):
                pass
            return e.sem_inc(GC, 1)
        v = P.inc("gc", 1)
        P.em("gpsimd", _poll)
        return v

    emit_jobs(4)

    if not SUB_JLATE:
        emit_jobs(4)

    v_evac_v = {}
    v_kvmm = {}
    gch = -1

    def head_slice(h):
        oq, hh = h // 8, h % 8
        return 64 * (hh % 2), 260 * oq + 65 * (hh // 2)

    for oq in range(SUB_OQ):
        wnow("gpsimd", "pe")
        o0 = 512 * oq
        v_wk = dma("gpsimd",
                   wsl[:, 0:8192].rearrange("p (k o) -> p k o", o=512),
                   wk_d[:, o0:o0 + 512].rearrange("(k p) o -> p k o", p=128))
        v_wv = dma("gpsimd",
                   wsl[:, 8192:16384].rearrange("p (k o) -> p k o", o=512),
                   wv_d[:, o0:o0 + 512].rearrange("(k p) o -> p k o", p=128))
        for ch in range(SUB_CH):
            gch += 1
            par = gch % 2
            own = ch < 8
            if own:
                xti = lambda k, ch=ch: xb[:, S_OWN * k + 128 * ch:
                                          S_OWN * k + 128 * ch + 128]
            else:
                fc = ch - 8
                w("gpsimd", "pe", v_kvmm.get(gch - 2, 0))
                v_ck = dma("gpsimd",
                           xck[:, 2048 * par:2048 * par + 2048].rearrange(
                               "p (k t) -> p k t", t=128),
                           x_ext[:, 128 * fc:128 * fc + 128].rearrange(
                               "(k p) t -> p k t", p=128))
                xti = lambda k, par=par: xck[:, 2048 * par + 128 * k:
                                             2048 * par + 128 * k + 128]
            pb_k = B[par]
            pb_v = B[2 + par]
            if ch == 0:
                w("tensor", "gq", max(v_wk, v_wv, v_xb))
            elif not own:
                w("tensor", "gq", v_ck)
            w("tensor", "ac", v_evac_v.get(gch - 2, 0))
            v_pk = mm_seq([(xti(k), wsl[:, 512 * k:512 * k + 512],
                            pb_k[:, :], k == 0, k == NKT - 1)
                           for k in range(NKT)])
            v_pv = mm_seq([(xti(k), wsl[:, 8192 + 512 * k:8192 + 512 * k + 512],
                            pb_v[:, :], k == 0, k == NKT - 1)
                           for k in range(NKT)])
            w("scalar", "pe", v_kvmm.get(gch - 2, 0))
            w("scalar", "pe", v_pk)
            if gch == 0:
                w("scalar", "ve", v_memset)
            act(lambda e, par=par, pb=pb_k: e.activation(
                ktm[:, 512 * par:512 * par + 512], pb[:, :], AF.Relu))
            w("scalar", "pe", v_pv)
            va = vpad[:].rearrange("p (b h c) -> p b h c", b=2, c=65)
            v_evac_v[gch] = act(
                lambda e, par=par, pb=pb_v, va=va: e.activation(
                    va[:, par, :, 0:64],
                    pb[:, :].rearrange("p (h c) -> p h c", c=64),
                    AF.Identity),
                inc=True)

            def emit_kv(cch):
                ppar = (4 * NCH * 0 + cch) % 2
                w("tensor", "ac", v_evac_v[cch])
                kvops = []
                for hh in range(8):
                    r0 = 64 * (hh % 2)
                    kvops.append((
                        ktm[:, 512 * (cch % 2) + 64 * hh:
                            512 * (cch % 2) + 64 * hh + 64],
                        vpad[:, 520 * (cch % 2) + 65 * hh:
                             520 * (cch % 2) + 65 * hh + 65],
                        B[4 + hh // 2][r0:r0 + 64, 0:65],
                        (cch % NCH) == 0, (cch % NCH) == NCH - 1))
                v_kvmm[cch] = mm_seq(kvops)

            if SUB_KV:
                if ch > 0:
                    emit_kv(gch - 1)
                if ch == NCH - 1:
                    emit_kv(gch)
            else:
                v_kvmm[gch] = P.now("pe")
            if ch % 4 == 3 and not SUB_JLATE and SITE_A1:
                emit_jobs(1)
        if SUB_KV and SUB_CH == NCH:
            w("scalar", "pe", v_kvmm[gch])
            for b4 in range(3):
                act(lambda e, oq=oq, b4=b4: e.copy(
                    kv_mine[:, 260 * oq + 65 * b4:260 * oq + 65 * b4 + 65],
                    B[4 + b4][:, 0:65]))
            v_kvev = act(lambda e, oq=oq: e.copy(
                kv_mine[:, 260 * oq + 195:260 * oq + 260],
                B[7][:, 0:65]), inc=True)
            if oq < 3:
                w("tensor", "ac", v_kvev)
        pass

    if SUB_EXC >= 1:
        wnow("gpsimd", "ac")

        def _kv_out(e):
            return e.dma_start(out=kvx[bass.ts(greg["e"], 128), :],
                               in_=kv_mine).then_inc(GQ, 16)
        P.inc("gq", 16)
        P.em("gpsimd", _kv_out)
        wnow("gpsimd", "gq")
        w("gpsimd", "sq", v_consts)

        def _kv_flag(e):
            return e.dma_start(out=flags[0, 0:1, bass.ts(greg["e"], 1)],
                               in_=rid_s[:]).then_inc(GQ, 16)
        P.inc("gq", 16)
        P.em("gpsimd", _kv_flag)
    if SUB_EXC >= 2:
        v_kvpoll = emit_poll(0)
    if SUB_EXC >= 3:
        w("gpsimd", "gc", v_kvpoll)

        def _kv_in(e):
            return e.dma_start(out=kv_peer,
                               in_=kvx[bass.ts(greg["peer"], 128), :]
                               ).then_inc(GQ, 16)
        v_peer = P.inc("gq", 16)
        P.em("gpsimd", _kv_in)
        w("gpsimd", "gq", v_peer)
        P.em("gpsimd", lambda e: e.tensor_add(kv_mine, kv_mine, kv_peer))
    v_kvfull = P.inc("gc", 1)
    P.em("gpsimd", lambda e: e.tensor_copy(kv_b[:], kv_mine
                                           ).then_inc(GC, 1))

    if DEBUG and STAGE == 1:
        wnow("gpsimd", "ve")
        dma("gpsimd", dbg_kv[:], kv_mine)

    qb = h_hi[:, 0:16384]
    if STAGE >= 2:
        v_qev = {}
        wnow("tensor", "ac")
        wnow("tensor", "ve")
        for oh in range(2):
            wnow("gpsimd", "pe")
            v_wq = dma("gpsimd",
                       wsl[:, 0:16384].rearrange("p (k o) -> p k o", o=1024),
                       wq_d[:, 1024 * oh:1024 * oh + 1024].rearrange(
                           "(k p) o -> p k o", p=128))
            w("tensor", "gq", v_wq)
            for t8 in range(8):
                t = 8 * oh + t8
                if t >= SUB_QT:
                    continue
                w("tensor", "ac", v_qev.get(t - 2, 0))
                for sc in range(2):
                    pb = B[2 * (t8 % 2) + sc]
                    v_qmm = mm_seq([(wsl[:, 1024 * k + 128 * t8:
                                         1024 * k + 128 * t8 + 128],
                                     xb[:, S_OWN * k + 512 * sc:
                                        S_OWN * k + 512 * sc + 512],
                                     pb[:, :], k == 0, k == NKT - 1)
                                    for k in range(NKT)])
                    w("scalar", "pe", v_qmm)
                    v_qev[t] = act(lambda e, pb=pb, t=t, sc=sc: e.activation(
                        qb[:, 1024 * t + 512 * sc:1024 * t + 512 * sc + 512],
                        pb[:, :], AF.Relu), inc=True)
        v_q_done = P.now("ac")

    v_spill = {}
    v_apply = {}
    if STAGE >= 2 and SUB_A3 >= 2:
        for sh in range(2):
            qkv_b = xb[:, 0:8192]
            v_nev = {}
            v_ndma = {}
            for h in range(H):
                r0, c0 = head_slice(h)
                pb = B[2 + h % 2]
                if h < 2:
                    w("tensor", "gc", v_kvfull)
                    w("tensor", "ac", v_q_done)
                    if sh == 1:
                        w("tensor", "ve", v_apply.get(0, 0))
                if h >= 2:
                    w("tensor", "ac", v_nev[h - 2])
                mm_n = mm_inc(
                    kv_b[r0:r0 + 64, c0 + 64:c0 + 65],
                    qb[64 * (h % 2):64 * (h % 2) + 64,
                       1024 * (h // 2) + 512 * sh:
                       1024 * (h // 2) + 512 * sh + 512],
                    pb[0:1, :], start=True, stop=True)
                w("scalar", "pe", mm_n)
                w("scalar", "gq", v_ndma.get(h - 2, 0))
                nr = nrow0 if h % 2 == 0 else nrow1
                v_nev[h] = act(lambda e, pb=pb, nr=nr: e.activation(
                    nr[:], pb[0:1, :], AF.Identity), inc=True)
                w("gpsimd", "ac", v_nev[h])
                v_ndma[h] = dma("gpsimd", norms[h:h + 1, :], nr[:])
            wnow("vector", "gq")
            vec(lambda e: e.tensor_scalar(norms[:], norms[:], 1e-6, None,
                                          OP.max))
            vec(lambda e: e.reciprocal(norms[:], norms[:]))
            v_rnb = vec(lambda e: e.tensor_copy(rnb[:], norms[:]), inc=True)
            if SUB_A3 < 3:
                continue

            v_qkvb = {}
            for t in range(NKT):
                if SITE_QK:
                    emit_jobs(1)
                pb = B[t % 2]
                pbr = B[5]
                wv_prev = v_qkvb.get(t - 2, 0)
                w("tensor", "ve", wv_prev)
                qkops = []
                for hh in range(2):
                    h = 2 * t + hh
                    r0, c0 = head_slice(h)
                    qkops.append((kv_b[r0:r0 + 64, c0:c0 + 64],
                                  qb[64 * hh:64 * hh + 64,
                                     1024 * t + 512 * sh:
                                     1024 * t + 512 * sh + 512],
                                  pb[64 * hh:64 * hh + 64, :], True, True))
                v_qkvmm = mm_seq(qkops)
                w("tensor", "ve", v_rnb)
                if t >= 1:
                    w("tensor", "ve", v_qkvb.get(t - 1, 0))
                v_bcmm = mm_inc(sel_s[:, 128 * t:128 * t + 128], rnb[:, :],
                                pbr[:, :], start=True, stop=True)
                w("scalar", "pe", v_qkvmm)
                if t >= 1:
                    w("scalar", "ve", v_qkvb.get(t - 1, 0))
                v_qe = act(lambda e, pb=pb: e.activation(
                    qsc[:], pb[:, :], AF.Identity), inc=True)
                w("vector", "ac", v_qe)
                w("vector", "pe", v_bcmm)
                v_qkvb[t] = vec(lambda e, t=t, pbr=pbr: e.tensor_tensor(
                    qkv_b[:, 512 * t:512 * t + 512], qsc[:], pbr[:, :],
                    OP.mult), inc=True)

            if SUB_A3 < 4:
                continue
            wnow("tensor", "ac")
            if sh == 1:
                wnow("vector", "gq")
            v_y1ev = {}
            v_add = {}
            for oh in range(2):
                wnow("gpsimd", "pe")
                v_wo = dma("gpsimd",
                           wsl[:, 0:16384].rearrange("p (k o) -> p k o",
                                                     o=1024),
                           wo_d[:, 1024 * oh:1024 * oh + 1024].rearrange(
                               "(k p) o -> p k o", p=128))
                w("tensor", "gq", v_wo)
                for t8 in range(8):
                    if SITE_WO:
                        emit_jobs(1)
                    g = 8 * oh + t8
                    pb = B[2 + t8 % 2]
                    w("gpsimd", "ve", v_add.get(g - 2, 0))
                    v_xr = dma("gpsimd",
                               xres[:, 512 * (g % 2):512 * (g % 2) + 512],
                               x_own[128 * g:128 * g + 128,
                                     512 * sh:512 * sh + 512])
                    w("tensor", "ve", v_qkvb[NKT - 1])
                    w("tensor", "ac", v_y1ev.get(g - 2, 0))
                    v_womm = mm_seq([(wsl[:, 1024 * k + 128 * t8:
                                           1024 * k + 128 * t8 + 128],
                                       qkv_b[:, 512 * k:512 * k + 512],
                                       pb[:, :], k == 0, k == NKT - 1)
                                      for k in range(NKT)])
                    w("scalar", "pe", v_womm)
                    w("scalar", "ve", v_add.get(g - 1, 0))
                    v_y1ev[g] = act(lambda e, pb=pb, g=g: e.activation(
                        hsil[:], pb[:, :], AF.Identity,
                        bias=bo_sb[:, g:g + 1]), inc=True)
                    w("vector", "ac", v_y1ev[g])
                    w("vector", "gq", v_xr)
                    v_add[g] = vec(lambda e, g=g: e.tensor_add(
                        big1[:, 512 * g:512 * g + 512], hsil[:],
                        xres[:, 512 * (g % 2):512 * (g % 2) + 512]),
                        inc=True)
                    v_y1b = vec(lambda e, g=g: e.tensor_copy(
                        y1b[:], big1[:, 512 * g:512 * g + 512]), inc=True)
                    w("scalar", "ve", v_y1b)
                    v_sqa = act(lambda e: e.activation(
                        y1sq[:], y1b[:], AF.Square), inc=True)
                    w("tensor", "ve", v_y1b)
                    mm(ones_c[:], y1b[:], B[6][0:1, :],
                       start=(g == 0), stop=(g == 15))
                    w("tensor", "ac", v_sqa)
                    v_stmm = mm_inc(ones_c[:], y1sq[:], B[7][0:1, :],
                                    start=(g == 0), stop=(g == 15))
                    w("vector", "pe", v_stmm)
            if SUB_A3 < 5:
                v_apply[sh] = P.now("ve")
                v_spill[sh] = P.now("gq")
                continue
            v_stats = P.now("pe")
            w("vector", "pe", v_stats)
            vec(lambda e: e.tensor_scalar(s_mean[:], B[6][0:1, :],
                                          1.0 / D, None, OP.mult))
            vec(lambda e: e.tensor_scalar(s_sq[:], B[7][0:1, :],
                                          1.0 / D, None, OP.mult))
            vec(lambda e: e.tensor_mul(s_var[:], s_mean[:], s_mean[:]))
            v_var = vec(lambda e: e.tensor_sub(s_var[:], s_sq[:], s_var[:]),
                        inc=True)
            w("scalar", "ve", v_var)
            act(lambda e: e.activation(s_rstd[:], s_var[:], AF.Ln,
                                       bias=eps1[:]))
            v_rstd = act(lambda e: e.activation(s_rstd[:], s_rstd[:],
                                                AF.Exp, scale=-0.5),
                         inc=True)
            w("tensor", "ac", v_rstd)
            w("tensor", "ve", v_var)
            mm(ones_r32[:], s_mean[:], B[2][:, :], start=True, stop=True)
            v_bc = mm_inc(ones_r32[:], s_rstd[:], B[3][:, :],
                          start=True, stop=True)
            w("vector", "pe", v_bc)
            last = None
            for g in range(NKT):
                vec(lambda e, g=g: e.tensor_sub(
                    hsil[:], big1[:, 512 * g:512 * g + 512], B[2][:, :]))
                last = vec(lambda e, g=g: e.tensor_mul(
                    big1[:, 512 * g:512 * g + 512], hsil[:], B[3][:, :]),
                    inc=True)
            v_apply[sh] = last
            w("gpsimd", "ve", v_apply[sh])
            v_spill[sh] = dma("gpsimd",
                              x1_spill[sh, :, :].rearrange(
                                  "(k p) t -> p k t", p=128),
                              big1[:].rearrange("p (k t) -> p k t", t=512))
            if DEBUG and STAGE == 2:
                dma("gpsimd", dbg_x1[:, 8192 * sh:8192 * sh + 8192],
                    big1[:])
            pass

    emit_all_jobs()

    if STAGE >= 4 and SUB_B >= 1:
        v_pollg = emit_poll(1)
        v_pollu = emit_poll(2)
        v_polld = emit_poll(3)

        def h_of(k):
            if k < 16:
                return xb[:, 8192 + 512 * k:8192 + 512 * k + 512]
            return h_hi[:, 512 * (k - 16):512 * (k - 16) + 512]

        v_out_store = 0
        for sh in (range(2) if SUB_B >= 2 else []):
            if sh == 1:
                w("sync", "gq", v_out_store)
            w("sync", "gq", v_spill[1])
            v_x1r = dma("sync", big1[:].rearrange("p (k t) -> p k t", t=512),
                        x1_spill[sh, :, :].rearrange("(k p) t -> p k t",
                                                     p=128))
            x1b_sh = xb[:, 0:8192]
            w("gpsimd", "gq", v_spill[1])
            wnow("gpsimd", "pe")
            v_x1bc = dma("gpsimd",
                         x1b_sh.rearrange("p (k t) -> p k t", t=512),
                         x1_spill[sh, :, :].rearrange("(k p) t -> p k t",
                                                      p=128))
            if SUB_B < 3:
                w("gpsimd", "gq", v_x1bc)
                v_out_store = dma(
                    "gpsimd",
                    out_d.rearrange("(k p) t -> p k t", p=128)[
                        :, :, 512 * sh:512 * sh + 512],
                    big1[:].rearrange("p (k t) -> p k t", t=512))
                v_out_ve = P.now("ve")
                continue
            w("sync", "gc", v_pollu)
            w("sync", "gc", v_pollg)
            NM = 22
            GSL = [wsl[:, 0:4096], wsl[:, 4096:8192]]
            USL = [wsl[:, 8192:12288], wsl[:, 12288:16384]]
            v_ht = {}
            v_pe_after = {}
            MACROS = [(m, min(256, INTER - 256 * m)) for m in range(NM)]

            def plan_b1_load(m):
                mw = MACROS[m][1]
                mpar = m % 2
                w("sync", "pe", v_pe_after.get(m - 2, P.now("pe")
                                               if m < 2 else 0))
                vg_ = 0
                vu_ = 0
                for k in range(NKT):
                    vg_ = dma("sync", GSL[mpar][:, 256 * k:256 * k + mw],
                              Wg_n[128 * k:128 * k + 128,
                                   256 * m:256 * m + mw])
                    vu_ = dma("sync", USL[mpar][:, 256 * k:256 * k + mw],
                              Wu_n[128 * k:128 * k + 128,
                                   256 * m:256 * m + mw])
                return max(vg_, vu_)

            v_msl = {0: plan_b1_load(0), 1: plan_b1_load(1)}
            for m, mw in MACROS:
                mpar = m % 2
                nt = (mw + 127) // 128
                for otl in range(nt):
                    ot = 2 * m + otl
                    par = ot % 2
                    pbg = B[par]
                    pbu = B[2 + par]
                    w("tensor", "sq", v_msl[m])
                    w("tensor", "gq", v_x1bc)
                    if sh == 1 or ot >= 2:
                        w("tensor", "ve", v_ht.get(ot - 2, 0))
                    if ot == 0:
                        w("tensor", "ve", v_apply[1] if sh == 0
                          else v_out_ve)
                    v_gmm = mm_seq([(GSL[mpar][:, 256 * k + 128 * otl:
                                               256 * k + 128 * otl + 128],
                                     x1b_sh[:, 512 * k:512 * k + 512],
                                     pbg[:, :], k == 0, k == NKT - 1)
                                    for k in range(NKT)])
                    v_umm = mm_seq([(USL[mpar][:, 256 * k + 128 * otl:
                                               256 * k + 128 * otl + 128],
                                     x1b_sh[:, 512 * k:512 * k + 512],
                                     pbu[:, :], k == 0, k == NKT - 1)
                                    for k in range(NKT)])
                    w("scalar", "pe", v_gmm)
                    w("scalar", "ve", v_ht.get(ot - 1, 0))
                    v_sil = act(lambda e, pbg=pbg: e.activation(
                        hsil[:], pbg[:, :], AF.Silu), inc=True)
                    w("vector", "ac", v_sil)
                    w("vector", "pe", v_umm)
                    v_ht[ot] = vec(lambda e, ot=ot, pbu=pbu: e.tensor_tensor(
                        h_of(ot), hsil[:], pbu[:, :], OP.mult), inc=True)
                v_pe_after[m] = P.now("pe")
                if m + 2 < NM:
                    v_msl[m + 2] = plan_b1_load(m + 2)
            v_h_done = P.now("ve")
            if SUB_B < 4:
                w("gpsimd", "ve", v_h_done)
                v_out_store = dma(
                    "gpsimd",
                    out_d.rearrange("(k p) t -> p k t", p=128)[
                        :, :, 512 * sh:512 * sh + 512],
                    big1[:].rearrange("p (k t) -> p k t", t=512))
                v_out_ve = P.now("ve")
                continue
            if sh == 0:
                w("sync", "gc", v_polld)
                for ot2 in range(16):
                    v_rp = dma("sync", Wd4[ot2, :, :, :],
                               Wd_n[:, 128 * ot2:128 * ot2 + 128].rearrange(
                                   "(k p) o -> p k o", p=128))
                v_repack = v_rp
            DS = [wsl[:, 0:5632], wsl[:, 5632:11264]]
            v_dsl = {}
            v_pe_d = {}

            def plan_b2_load(ot2):
                par = ot2 % 2
                if ot2 < 2:
                    w("sync", "pe", v_pe_after[21])
                    w("sync", "sq", v_repack)
                else:
                    w("sync", "pe", v_pe_d[ot2 - 2])
                v_dsl[ot2] = dma("sync", DS[par].rearrange(
                    "p (k o) -> p k o", o=128), Wd4[ot2, :, :, :])

            plan_b2_load(0)
            plan_b2_load(1)
            v_y2b_prev = 0
            for ot2 in range(16):
                par = ot2 % 2
                pbd = B[4 + par]
                w("tensor", "sq", v_dsl[ot2])
                w("tensor", "ve", v_h_done)
                if ot2 == 0:
                    wnow("tensor", "ve")
                w("tensor", "ve", v_y2b_prev)
                for k in range(43):
                    mm(DS[par][:, 128 * k:128 * k + 128], h_of(k),
                       pbd[:, :], k == 0, k == 42)
                mm(ones_c[0:1, 0:1], ones_c[0:1, 0:1],
                   B[par][0:1, 0:1], True, True)
                v_dmm = mm_inc(ones_c[0:1, 0:1], ones_c[0:1, 0:1],
                               B[par][0:1, 0:1], True, True)
                v_pe_d[ot2] = v_dmm
                if ot2 + 2 <= 15:
                    plan_b2_load(ot2 + 2)
                w("vector", "pe", v_dmm)
                w("vector", "sq", v_x1r)
                vec(lambda e, ot2=ot2, pbd=pbd: e.tensor_add(
                    big1[:, 512 * ot2:512 * ot2 + 512],
                    big1[:, 512 * ot2:512 * ot2 + 512], pbd[:, :]))
                v_y2b = vec(lambda e, ot2=ot2: e.tensor_copy(
                    y1b[:], big1[:, 512 * ot2:512 * ot2 + 512]), inc=True)
                v_y2b_prev = v_y2b
                w("scalar", "ve", v_y2b)
                v_sqa = act(lambda e: e.activation(
                    y1sq[:], y1b[:], AF.Square), inc=True)
                w("tensor", "ve", v_y2b)
                mm(ones_c[:], y1b[:], B[6][0:1, :],
                   start=(ot2 == 0), stop=(ot2 == 15))
                w("tensor", "ac", v_sqa)
                v_stmm = mm_inc(ones_c[:], y1sq[:], B[7][0:1, :],
                                start=(ot2 == 0), stop=(ot2 == 15))
                w("vector", "pe", v_stmm)
            v_stats = P.now("pe")
            w("vector", "pe", v_stats)
            vec(lambda e: e.tensor_scalar(s_mean[:], B[6][0:1, :],
                                          1.0 / D, None, OP.mult))
            vec(lambda e: e.tensor_scalar(s_sq[:], B[7][0:1, :],
                                          1.0 / D, None, OP.mult))
            vec(lambda e: e.tensor_mul(s_var[:], s_mean[:], s_mean[:]))
            v_var = vec(lambda e: e.tensor_sub(s_var[:], s_sq[:], s_var[:]),
                        inc=True)
            w("scalar", "ve", v_var)
            act(lambda e: e.activation(s_rstd[:], s_var[:], AF.Ln,
                                       bias=eps1[:]))
            v_rstd = act(lambda e: e.activation(s_rstd[:], s_rstd[:],
                                                AF.Exp, scale=-0.5),
                         inc=True)
            w("tensor", "ac", v_rstd)
            w("tensor", "ve", v_var)
            mm(ones_r32[:], s_mean[:], B[2][:, :], start=True, stop=True)
            v_bc = mm_inc(ones_r32[:], s_rstd[:], B[3][:, :],
                          start=True, stop=True)
            w("vector", "pe", v_bc)
            last = None
            for g in range(NKT):
                vec(lambda e, g=g: e.tensor_sub(
                    hsil[:], big1[:, 512 * g:512 * g + 512], B[2][:, :]))
                last = vec(lambda e, g=g: e.tensor_mul(
                    big1[:, 512 * g:512 * g + 512], hsil[:], B[3][:, :]),
                    inc=True)
            v_out_ve = last
            w("gpsimd", "ve", v_out_ve)
            v_out_store = dma(
                "gpsimd",
                out_d.rearrange("(k p) t -> p k t", p=128)[
                    :, :, 512 * sh:512 * sh + 512],
                big1[:].rearrange("p (k t) -> p k t", t=512))

    if STAGE >= 4 and SUB_B < 2:
        wnow("gpsimd", "ve")
        dma("gpsimd", out_d.rearrange("(k p) t -> p k t", p=128)[:, :, 0:512],
            big1[:].rearrange("p (k t) -> p k t", t=512))
    if int(os.environ.get("MK_DBGW2", "0")) and STAGE >= 3:
        dbg_we = dt("dbg_we", [128, 8192], F32, k="ExternalOutput")
        cg, cs = P.now("gq"), P.now("sq")
        P.em("gpsimd", lambda e, cg=cg: e.wait_ge(GQ, cg))
        P.em("gpsimd", lambda e, cs=cs: e.wait_ge(SQ, cs))
        dma("gpsimd", dbg_we[:, 0:2752], Wu_n[0:128, :].bitcast(F32))
        dma("gpsimd", dbg_we[:, 2752:5504], Wu_n[1024:1152, :].bitcast(F32))
        dma("gpsimd", dbg_we[:, 5504:6528], Wd_n[0:128, :].bitcast(F32))
        dma("gpsimd", dbg_we[:, 6528:7552], Wd_n[2816:2944, :].bitcast(F32))
        dma("gpsimd", dbg_we[:, 7552:8192], Wg_n[0:128, 0:1280].bitcast(F32))

    if int(os.environ.get("MK_DBGW3", "0")) and STAGE >= 3:
        dbg_sb = dt("dbg_sb", [128, 16384], F32, k="ExternalOutput")
        cg2, cs2, cv2 = P.now("gq"), P.now("sq"), P.now("ve")
        P.em("gpsimd", lambda e, c=cg2: e.wait_ge(GQ, c))
        P.em("gpsimd", lambda e, c=cs2: e.wait_ge(SQ, c))
        P.em("gpsimd", lambda e, c=cv2: e.wait_ge(VE, c))
        dma("gpsimd", dbg_sb[:, 0:8192], xb[:, 0:16384].bitcast(F32))
        dma("gpsimd", dbg_sb[:, 8192:15104], h_hi[:, 0:13824].bitcast(F32))

    P.em("gpsimd", lambda e: e.wait_ge(GQ, P.now("gq")))
    P.em("gpsimd", lambda e: e.wait_ge(SQ, P.now("sq")))
    P.em("gpsimd", lambda e: e.wait_ge(VE, P.now("ve")))
    P.em("gpsimd", lambda e: e.wait_ge(VG, P.now("vg")))
    P.em("gpsimd", lambda e: e.wait_ge(AC, P.now("ac")))
    P.em("gpsimd", lambda e: e.wait_ge(PE_, P.now("pe")))

    with es:
        with nc.Block() as block:
            @block.sync
            def _(eng):
                for f in P.ops["sync"]:
                    f(eng)

            @block.gpsimd
            def _(eng):
                for f in P.ops["gpsimd"]:
                    f(eng)

            @block.vector
            def _(eng):
                for f in P.ops["vector"]:
                    f(eng)

            @block.scalar
            def _(eng):
                for f in P.ops["scalar"]:
                    f(eng)

            @block.tensor
            def _(eng):
                for f in P.ops["tensor"]:
                    f(eng)

    return nc


def make_inputs(L0_attn_hidden_states, L0_attn_Wq, L0_attn_Wk, L0_attn_Wv,
                L0_attn_Wo, L0_attn_bo, L0_ffn_lut, L0_ffn_gate_idx,
                L0_ffn_up_idx, L0_ffn_down_idx):
    x = np.asarray(L0_attn_hidden_states)[0]
    xT = np.ascontiguousarray(x.T)
    lut_bc = np.tile(np.asarray(L0_ffn_lut, np.float32)[None, :], (128, 1))
    bo = np.ascontiguousarray(np.asarray(L0_attn_bo, np.float32).reshape(16, 128).T)
    sel = np.zeros((32, D), ml_dtypes.bfloat16)
    for t in range(16):
        sel[2 * t, 128 * t:128 * t + 64] = 1
        sel[2 * t + 1, 128 * t + 64:128 * t + 128] = 1
    gT = np.ascontiguousarray(np.asarray(L0_ffn_gate_idx).T).astype(np.uint8)
    uT = np.ascontiguousarray(np.asarray(L0_ffn_up_idx).T).astype(np.uint8)
    dT = np.ascontiguousarray(np.asarray(L0_ffn_down_idx).T).astype(np.uint8)
    dT_pad = np.zeros((2 * 2816, D), np.uint8)
    dT_pad[0:INTER, :] = dT
    runid = np.random.randint(1, 2**30, dtype=np.int32)
    wq = np.ascontiguousarray(np.asarray(L0_attn_Wq, np.float32))
    wk = np.ascontiguousarray(np.asarray(L0_attn_Wk, np.float32))
    wv = np.ascontiguousarray(np.asarray(L0_attn_Wv, np.float32))
    wo = np.ascontiguousarray(np.asarray(L0_attn_Wo, np.float32))
    in_maps = []
    for c in range(NCORE):
        p, e = c // 2, c % 2
        own = slice(1024 * c, 1024 * c + 1024)
        pair_lo, pair_hi = 2048 * p, 2048 * p + 2048
        comp = np.r_[0:pair_lo, pair_hi:8192]
        extra = comp[:3072] if e == 0 else comp[3072:]
        in_maps.append({
            "x_own": np.ascontiguousarray(xT[:, own]),
            "x_ext": np.ascontiguousarray(xT[:, extra]),
            "wq": wq, "wk": wk, "wv": wv, "wo": wo,
            "bo": bo, "lut": lut_bc, "sel": sel,
            "gidx": np.ascontiguousarray(gT[1024 * e:1024 * e + 1024, :]),
            "uidx": np.ascontiguousarray(uT[1024 * e:1024 * e + 1024, :]),
            "didx": np.ascontiguousarray(dT_pad[2816 * e:2816 * e + 2816, :]),
            "runid": np.full((1, 1), runid, np.int32),
            "eid": np.full((1, 1), e, np.int32),
        })
    return in_maps


def kernel(**inputs):
    in_maps = make_inputs(**inputs)
    nc = build_program()
    res = run_bass_kernel_spmd(nc, in_maps, core_ids=list(range(NCORE)))
    out = np.empty((1, 8192, D), np.float32)
    for c in range(NCORE):
        out[0, 1024 * c:1024 * c + 1024, :] = \
            np.asarray(res.results[c]["out"], np.float32).T
    return out


# version 04 (final)
# speedup vs baseline: 1.5499x; 1.0202x over previous
import numpy as np
import ml_dtypes
import contextlib
import concourse.bass as bass
import concourse.mybir as mybir
from concourse.bass_utils import run_bass_kernel_spmd
from concourse.expressions import RuntimeValue

F32 = mybir.dt.float32
BF16 = mybir.dt.bfloat16
U8 = mybir.dt.uint8
I32 = mybir.dt.int32
AF = mybir.ActivationFunctionType
OP = mybir.AluOpType

import os
STAGE = int(os.environ.get("MK_STAGE", "4"))
SUB_OQ = int(os.environ.get("MK_OQ", "4"))
SUB_CH = int(os.environ.get("MK_CH", "32"))
SUB_KV = int(os.environ.get("MK_KV", "1"))
SUB_EXC = int(os.environ.get("MK_EXC", "3"))
SUB_VEV = int(os.environ.get("MK_VEV", "1"))
SUB_A3 = int(os.environ.get("MK_A3", "5"))
SUB_QT = int(os.environ.get("MK_QT", "16"))
SUB_NJ = int(os.environ.get("MK_NJ", "999"))
SUB_GST = int(os.environ.get("MK_GST", "1"))
SUB_GDV = int(os.environ.get("MK_GDV", "1"))
SUB_B = int(os.environ.get("MK_B", "9"))
SUB_JLATE = int(os.environ.get("MK_JLATE", "0"))
SITE_A1 = int(os.environ.get("MK_SA1", "1"))
SITE_QK = int(os.environ.get("MK_SQK", "1"))
SITE_WO = int(os.environ.get("MK_SWO", "1"))
GSEM = os.environ.get("MK_GSEM", "vg")
SUB_SMODE = int(os.environ.get("MK_SMODE", "0"))
DEBUG = bool(int(os.environ.get("MK_DBG", "1")))
NCORE = 8

D = 2048
S_OWN = 1024
NCH = 32
INTER = 5504
NKT = 16
H = 32
JOBW = 2048


class Plan:
    def __init__(self):
        self.ops = {"sync": [], "gpsimd": [], "vector": [], "scalar": [],
                    "tensor": []}
        self.cnt = {}

    def em(self, eng, fn):
        self.ops[eng].append(fn)

    def inc(self, sem, n=1):
        self.cnt[sem] = self.cnt.get(sem, 0) + n
        return self.cnt[sem]

    def now(self, sem):
        return self.cnt.get(sem, 0)


def build_program():
    nc = bass.Bass("TRN2", target_bir_lowering=False, debug=False,
                   detect_race_conditions=bool(int(os.environ.get("MK_RACE", "0"))))
    P = Plan()

    dt = lambda n, s, d, k="ExternalInput", **kw: nc.dram_tensor(
        n, s, d, kind=k, **kw).ap()
    x_own = dt("x_own", [D, S_OWN], F32)
    x_ext = dt("x_ext", [D, 3072], F32)
    wq_d = dt("wq", [D, D], F32)
    wk_d = dt("wk", [D, D], F32)
    wv_d = dt("wv", [D, D], F32)
    wo_d = dt("wo", [D, D], F32)
    bo_d = dt("bo", [128, 16], F32)
    lut_d = dt("lut", [128, 16], F32)
    sel_d = dt("sel", [32, D], BF16)
    gidx_d = dt("gidx", [1024, INTER], U8)
    uidx_d = dt("uidx", [1024, INTER], U8)
    didx_d = dt("didx", [2816, D], U8)
    runid_d = dt("runid", [1, 1], I32)
    eid_d = dt("eid", [1, 1], I32)
    out_d = dt("out", [D, S_OWN], F32, k="ExternalOutput")
    if DEBUG and STAGE == 1:
        dbg_kv = dt("dbg_kv", [128, 1040], F32, k="ExternalOutput")
    if DEBUG and STAGE == 2:
        dbg_x1 = dt("dbg_x1", [128, 16 * S_OWN], F32, k="ExternalOutput")

    x1_spill = dt("x1_spill", [2, D, 512], F32, k="Internal")
    Wloc_g = dt("Wloc_g", [1024, INTER], BF16, k="Internal")
    Wloc_u = dt("Wloc_u", [1024, INTER], BF16, k="Internal")
    Wloc_d = dt("Wloc_d", [2816, D], BF16, k="Internal")
    shd = lambda n, s, d: nc.dram_tensor(n, s, d, kind="Internal",
                                         addr_space="Shared").ap()
    Wg_n = shd("Wg_n", [2048, INTER], BF16)
    Wu_n = shd("Wu_n", [2048, INTER], BF16)
    Wd_n = shd("Wd_n", [5632, 2048], BF16)
    Wd4 = shd("Wd4", [16, 128, 44, 128], BF16)
    kvx = shd("kvx", [256, 1040], F32)
    flags = shd("flags", [4, 1, 2], I32)

    es = contextlib.ExitStack()
    sb = lambda n, s, d: es.enter_context(nc.sbuf_tensor(n, s, d))

    xb = sb("xb", [128, NKT * S_OWN], BF16)
    h_hi = sb("h_hi", [128, 16384], BF16)
    wsl = sb("wsl", [128, 16384], BF16)
    xck = sb("xck", [128, 2 * 2048], BF16)
    ktm = sb("ktm", [128, 2 * 512], BF16)
    vpad = sb("vpad", [128, 2 * 520], BF16)
    kv_b = sb("kv_b", [128, 1040], BF16)
    gsc_i = sb("gsc_i", [128, 2 * JOBW], U8)
    gsc_b = sb("gsc_b", [128, JOBW], BF16)
    gsc_w = sb("gsc_w", [128, 2 * JOBW], BF16)
    gsc_t = sb("gsc_t", [128, JOBW], BF16)
    sel_s = sb("sel_s", [32, D], BF16)
    y1b = sb("y1b", [128, 512], BF16)
    y1sq = sb("y1sq", [128, 512], BF16)
    rnb = sb("rnb", [32, 512], BF16)
    ones_c = sb("ones_c", [128, 1], BF16)
    ones_r32 = sb("ones_r32", [1, 128], F32)

    big1 = sb("big1", [128, 8192], F32)
    qsc = sb("qsc", [128, 512], F32)
    hsil = sb("hsil", [128, 512], F32)
    xres = sb("xres", [128, 2 * 512], F32)
    nrow0 = sb("nrow0", [1, 512], F32)
    nrow1 = sb("nrow1", [1, 512], F32)
    norms = sb("norms", [32, 512], F32)
    s_mean = sb("s_mean", [1, 512], F32)
    s_sq = sb("s_sq", [1, 512], F32)
    s_var = sb("s_var", [1, 512], F32)
    s_rstd = sb("s_rstd", [1, 512], F32)
    eps1 = sb("eps1", [1, 1], F32)
    lutbc = sb("lutbc", [128, 16], F32)
    bo_sb = sb("bo_sb", [128, 16], F32)
    rid_s = sb("rid_s", [1, 1], I32)
    eid_s = sb("eid_s", [1, 1], I32)
    flg_s = sb("flg_s", [1, 2], I32)

    kv_mine = big1[:, 0:1040]
    kv_peer = big1[:, 1040:2080]

    B = [es.enter_context(nc.psum_tensor(f"pb{i}", [128, 512], F32))
         for i in range(8)]

    sems = {}
    for name in ("sq", "gq", "pe", "ac", "ve", "vg", "gc", "po"):
        sems[name] = es.enter_context(nc.semaphore(name))
    SQ, GQ, PE_, AC, VE, VG, GC, PO = (sems[k] for k in
                                       ("sq", "gq", "pe", "ac", "ve", "vg",
                                        "gc", "po"))

    def w(eng, sem, val):
        if val <= 0:
            return
        s = sems[sem]
        P.em(eng, lambda e, s=s, v=val: e.wait_ge(s, v))

    def wnow(eng, sem):
        w(eng, sem, P.now(sem))

    def dma(q, out, in_):
        sem = "sq" if q == "sync" else "gq"
        v = P.inc(sem, 16)
        s = sems[sem]
        P.em(q, lambda e, o=out, i=in_, s=s: e.dma_start(out=o, in_=i
                                                         ).then_inc(s, 16))
        return v

    def mm(lhsT, rhs, out, start, stop):
        P.em("tensor", lambda e, l=lhsT, r=rhs, o=out, st=start, sp=stop:
             e.matmul(o, l, r, start=st, stop=sp, skip_group_check=True))

    def mm_inc(lhsT, rhs, out, start, stop):
        v = P.inc("pe", 1)
        P.em("tensor", lambda e, l=lhsT, r=rhs, o=out, st=start, sp=stop:
             e.matmul(o, l, r, start=st, stop=sp, skip_group_check=True
                      ).then_inc(PE_, 1))
        return v

    def mm_seq(ops_list):
        for (l, r, o, st, sp) in ops_list[:-1]:
            mm(l, r, o, st, sp)
        l, r, o, st, sp = ops_list[-1]
        return mm_inc(l, r, o, st, sp)

    def act(fn, inc=False):
        if inc:
            v = P.inc("ac", 1)
            P.em("scalar", lambda e, f=fn: f(e).then_inc(AC, 1))
            return v
        P.em("scalar", fn)

    def vec(fn, inc=False, sem="ve"):
        if inc:
            v = P.inc(sem, 1)
            s = sems[sem]
            P.em("vector", lambda e, f=fn, s=s: f(e).then_inc(s, 1))
            return v
        P.em("vector", fn)

    dma("sync", lutbc[:], lut_d[:])
    dma("sync", bo_sb[:], bo_d[:])
    dma("sync", sel_s[:], sel_d[:])
    dma("sync", rid_s[:], runid_d[:])
    dma("sync", eid_s[:], eid_d[:])
    v_consts = P.now("sq")
    v_xb = dma("gpsimd", xb[:].rearrange("p (k t) -> p k t", k=NKT),
               x_own.rearrange("(k p) t -> p k t", p=128))

    def _memsets(e):
        if DEBUG:
            e.memset(big1[:], 0.0)
        e.memset(ones_c[:], 1.0)
        e.memset(ones_r32[:], 1.0)
        e.memset(eps1[:], 1e-5)
        va = vpad[:].rearrange("p (b h c) -> p b h c", b=2, c=65)
        return e.memset(va[:, :, :, 64:65], 1.0)
    v_memset = vec(_memsets, inc=True)

    greg = {}

    def gp_regs(e):
        cid = e.partition_id()
        greg["e"] = cid % 2
        greg["peer"] = 1 - cid % 2
    P.em("gpsimd", gp_regs)

    def _lazy_regs(e):
        if "ebgu" in greg:
            return
        ev = e.value_load(eid_s[0:1, 0:1], min_val=0, max_val=1)
        er = e.to_reg(ev)
        ebgu = e.to_reg(0)
        e.reg_mul(ebgu, er, 1024)
        greg["ebgu"] = ebgu
        ebd = e.to_reg(0)
        e.reg_mul(ebd, er, 2816)
        greg["ebd"] = ebd
        greg["rowoff"] = e.to_reg(0)

    def dyn_rows(e, base_reg, const_rows, max_start):
        _lazy_regs(e)
        e.reg_alu(greg["rowoff"], greg[base_reg], const_rows, OP.add)
        return bass.DynSlice(
            RuntimeValue(val=greg["rowoff"], min_val=0, max_val=max_start),
            128)

    sreg = {}

    def sy_regs(e):
        cid = e.partition_id()
        sreg["e"] = cid % 2
    P.em("sync", sy_regs)

    gstate = {"n": 0, "done": 0}
    v_job_g = {}

    def emit_gather_job(mat, kt, c0, width):
        j = gstate["n"]
        gstate["n"] += 1
        par = j % 2
        idx_t = gsc_i[:, JOBW * par:JOBW * par + width]
        wac = gsc_w[:, JOBW * par:JOBW * par + width]
        src = {"g": gidx_d, "u": uidx_d, "d": didx_d}[mat]
        w("sync", GSEM, v_job_g.get(j - 2, 0))
        v_idx = dma("sync", idx_t,
                    src[128 * kt:128 * kt + 128, c0:c0 + width])
        w("vector", "sq", v_idx)
        w("vector", "sq", gstate.get(f"st{j - 2}", 0))

        def _gather(e, it=idx_t, wa=wac, wd=width):
            e.tensor_copy(gsc_b[:, 0:wd], it)
            if not SUB_GDV:
                return e.tensor_copy(wa, gsc_b[:, 0:wd])
            e.tensor_scalar(wa, gsc_b[:, 0:wd], 0.0, lutbc[:, 0:1],
                            OP.is_equal, OP.mult)
            for jj in range(1, 15):
                e.tensor_scalar(gsc_t[:, 0:wd], gsc_b[:, 0:wd], float(jj),
                                lutbc[:, jj:jj + 1], OP.is_equal, OP.mult)
                e.tensor_add(wa, wa, gsc_t[:, 0:wd])
            e.tensor_scalar(gsc_t[:, 0:wd], gsc_b[:, 0:wd], 15.0,
                            lutbc[:, 15:16], OP.is_equal, OP.mult)
            return e.tensor_add(wa, wa, gsc_t[:, 0:wd])
        v_job_g[j] = vec(_gather, inc=True, sem=GSEM)
        w("sync", GSEM, v_job_g[j])
        Wloc = {"g": Wloc_g, "u": Wloc_u, "d": Wloc_d}[mat]

        def _store(e, Wloc=Wloc, kt=kt, c0=c0, width=width, wa=wac):
            return e.dma_start(out=Wloc[128 * kt:128 * kt + 128,
                                        c0:c0 + width],
                               in_=wa).then_inc(SQ, 16)
        if SUB_GST:
            gstate[f"st{j}"] = P.inc("sq", 16)
            P.em("sync", _store)

    gate_jobs = [("g", kt, c0, min(JOBW, INTER - c0))
                 for kt in range(8) for c0 in range(0, INTER, JOBW)]
    up_jobs = [("u", kt, c0, min(JOBW, INTER - c0))
               for kt in range(8) for c0 in range(0, INTER, JOBW)]
    down_jobs = [("d", kt, 0, D) for kt in range(22)]
    job_q = gate_jobs + up_jobs + down_jobs
    NG, NU = len(gate_jobs), len(up_jobs)
    flag_after = {NG: 1, NG + NU: 2, NG + NU + len(down_jobs): 3}

    def emit_jobs(n):
        if STAGE < 3:
            return
        for _ in range(n):
            if not job_q or gstate["done"] >= SUB_NJ:
                return
            emit_gather_job(*job_q.pop(0))
            gstate["done"] += 1
            if gstate["done"] in flag_after:
                fi = flag_after[gstate["done"]]
                wnow("gpsimd", "sq")
                w("gpsimd", "sq", v_consts)
                if fi == 1:
                    def _pub(e):
                        return e.dma_start(
                            out=Wg_n[bass.ts(greg["e"], 1024), :],
                            in_=Wloc_g[:, :]).then_inc(GQ, 16)
                elif fi == 2:
                    def _pub(e):
                        return e.dma_start(
                            out=Wu_n[bass.ts(greg["e"], 1024), :],
                            in_=Wloc_u[:, :]).then_inc(GQ, 16)
                else:
                    def _pub(e):
                        return e.dma_start(
                            out=Wd_n[bass.ts(greg["e"], 2816), :],
                            in_=Wloc_d[:, :]).then_inc(GQ, 16)
                P.inc("gq", 16)
                P.em("gpsimd", _pub)
                wnow("gpsimd", "gq")

                def _flag(e, fi=fi):
                    return e.dma_start(
                        out=flags[fi, 0:1, bass.ts(greg["e"], 1)],
                        in_=rid_s[:]).then_inc(GQ, 16)
                P.inc("gq", 16)
                P.em("gpsimd", _flag)

    def emit_all_jobs():
        emit_jobs(len(job_q))

    def emit_jobs_site(n):
        if gstate["done"] < NG + NU + len(down_jobs) - 1:
            emit_jobs(n)

    def emit_poll(fi):
        def _poll(e, fi=fi):
            if "po_polls" not in greg:
                greg["po_polls"] = e.to_reg(0)
                greg["po_tgt"] = e.to_reg(0)
                greg["po_nd"] = e.to_reg(1)
                greg["po_msc"] = e.to_reg(0)
                rr = e.value_load(rid_s[0:1, 0:1])
                greg["po_rid"] = e.to_reg(rr)
            polls, tgt = greg["po_polls"], greg["po_tgt"]
            nd, msc, rreg = greg["po_nd"], greg["po_msc"], greg["po_rid"]

            def cond():
                e.reg_add(polls, polls, 1)
                e.reg_mul(tgt, polls, 16)
                e.dma_start(out=flg_s[:], in_=flags[fi, :, :]).then_inc(PO, 16)
                e.wait_ge(PO, tgt)
                e.reg_mov(nd, 0)
                for cc in range(2):
                    vc = e.value_load(flg_s[0:1, cc:cc + 1])
                    e.reg_alu(msc, vc, rreg, OP.not_equal)
                    e.reg_alu(nd, nd, msc, OP.add)
                return nd
            with e.While(cond):
                pass
            return e.sem_inc(GC, 1)
        v = P.inc("gc", 1)
        P.em("gpsimd", _poll)
        return v

    emit_jobs(4)

    if not SUB_JLATE:
        emit_jobs(4)

    v_evac_v = {}
    v_kvmm = {}
    gch = -1

    def head_slice(h):
        oq, hh = h // 8, h % 8
        return 64 * (hh % 2), 260 * oq + 65 * (hh // 2)

    for oq in range(SUB_OQ):
        wnow("gpsimd", "pe")
        o0 = 512 * oq
        v_wk = dma("gpsimd",
                   wsl[:, 0:8192].rearrange("p (k o) -> p k o", o=512),
                   wk_d[:, o0:o0 + 512].rearrange("(k p) o -> p k o", p=128))
        v_wv = dma("gpsimd",
                   wsl[:, 8192:16384].rearrange("p (k o) -> p k o", o=512),
                   wv_d[:, o0:o0 + 512].rearrange("(k p) o -> p k o", p=128))
        for ch in range(SUB_CH):
            gch += 1
            par = gch % 2
            own = ch < 8
            if own:
                xti = lambda k, ch=ch: xb[:, S_OWN * k + 128 * ch:
                                          S_OWN * k + 128 * ch + 128]
            else:
                fc = ch - 8
                w("gpsimd", "pe", v_kvmm.get(gch - 2, 0))
                v_ck = dma("gpsimd",
                           xck[:, 2048 * par:2048 * par + 2048].rearrange(
                               "p (k t) -> p k t", t=128),
                           x_ext[:, 128 * fc:128 * fc + 128].rearrange(
                               "(k p) t -> p k t", p=128))
                xti = lambda k, par=par: xck[:, 2048 * par + 128 * k:
                                             2048 * par + 128 * k + 128]
            pb_k = B[par]
            pb_v = B[2 + par]
            if ch == 0:
                w("tensor", "gq", max(v_wk, v_wv, v_xb))
            elif not own:
                w("tensor", "gq", v_ck)
            w("tensor", "ac", v_evac_v.get(gch - 2, 0))
            v_pk = mm_seq([(xti(k), wsl[:, 512 * k:512 * k + 512],
                            pb_k[:, :], k == 0, k == NKT - 1)
                           for k in range(NKT)])
            v_pv = mm_seq([(xti(k), wsl[:, 8192 + 512 * k:8192 + 512 * k + 512],
                            pb_v[:, :], k == 0, k == NKT - 1)
                           for k in range(NKT)])
            w("scalar", "pe", v_kvmm.get(gch - 2, 0))
            w("scalar", "pe", v_pk)
            if gch == 0:
                w("scalar", "ve", v_memset)
            act(lambda e, par=par, pb=pb_k: e.activation(
                ktm[:, 512 * par:512 * par + 512], pb[:, :], AF.Relu))
            w("scalar", "pe", v_pv)
            va = vpad[:].rearrange("p (b h c) -> p b h c", b=2, c=65)
            v_evac_v[gch] = act(
                lambda e, par=par, pb=pb_v, va=va: e.activation(
                    va[:, par, :, 0:64],
                    pb[:, :].rearrange("p (h c) -> p h c", c=64),
                    AF.Identity),
                inc=True)

            def emit_kv(cch):
                ppar = (4 * NCH * 0 + cch) % 2
                w("tensor", "ac", v_evac_v[cch])
                kvops = []
                for hh in range(8):
                    r0 = 64 * (hh % 2)
                    kvops.append((
                        ktm[:, 512 * (cch % 2) + 64 * hh:
                            512 * (cch % 2) + 64 * hh + 64],
                        vpad[:, 520 * (cch % 2) + 65 * hh:
                             520 * (cch % 2) + 65 * hh + 65],
                        B[4 + hh // 2][r0:r0 + 64, 0:65],
                        (cch % NCH) == 0, (cch % NCH) == NCH - 1))
                v_kvmm[cch] = mm_seq(kvops)

            if SUB_KV:
                if ch > 0:
                    emit_kv(gch - 1)
                if ch == NCH - 1:
                    emit_kv(gch)
            else:
                v_kvmm[gch] = P.now("pe")
            if ch % 4 == 3 and not SUB_JLATE and SITE_A1:
                emit_jobs_site(1)
        if SUB_KV and SUB_CH == NCH:
            w("scalar", "pe", v_kvmm[gch])
            for b4 in range(3):
                act(lambda e, oq=oq, b4=b4: e.copy(
                    kv_mine[:, 260 * oq + 65 * b4:260 * oq + 65 * b4 + 65],
                    B[4 + b4][:, 0:65]))
            v_kvev = act(lambda e, oq=oq: e.copy(
                kv_mine[:, 260 * oq + 195:260 * oq + 260],
                B[7][:, 0:65]), inc=True)
            if oq < 3:
                w("tensor", "ac", v_kvev)
        pass

    if SUB_EXC >= 1:
        wnow("gpsimd", "ac")

        def _kv_out(e):
            return e.dma_start(out=kvx[bass.ts(greg["e"], 128), :],
                               in_=kv_mine).then_inc(GQ, 16)
        P.inc("gq", 16)
        P.em("gpsimd", _kv_out)
        wnow("gpsimd", "gq")
        w("gpsimd", "sq", v_consts)

        def _kv_flag(e):
            return e.dma_start(out=flags[0, 0:1, bass.ts(greg["e"], 1)],
                               in_=rid_s[:]).then_inc(GQ, 16)
        P.inc("gq", 16)
        P.em("gpsimd", _kv_flag)
    if SUB_EXC >= 2:
        v_kvpoll = emit_poll(0)
    if SUB_EXC >= 3:
        w("gpsimd", "gc", v_kvpoll)

        def _kv_in(e):
            return e.dma_start(out=kv_peer,
                               in_=kvx[bass.ts(greg["peer"], 128), :]
                               ).then_inc(GQ, 16)
        v_peer = P.inc("gq", 16)
        P.em("gpsimd", _kv_in)
        w("gpsimd", "gq", v_peer)
        P.em("gpsimd", lambda e: e.tensor_add(kv_mine, kv_mine, kv_peer))
    v_kvfull = P.inc("gc", 1)
    P.em("gpsimd", lambda e: e.tensor_copy(kv_b[:], kv_mine
                                           ).then_inc(GC, 1))

    if DEBUG and STAGE == 1:
        wnow("gpsimd", "ve")
        dma("gpsimd", dbg_kv[:], kv_mine)

    qb = h_hi[:, 0:16384]
    if STAGE >= 2:
        v_qev = {}
        wnow("tensor", "ac")
        wnow("tensor", "ve")
        for oh in range(2):
            wnow("gpsimd", "pe")
            v_wq = dma("gpsimd",
                       wsl[:, 0:16384].rearrange("p (k o) -> p k o", o=1024),
                       wq_d[:, 1024 * oh:1024 * oh + 1024].rearrange(
                           "(k p) o -> p k o", p=128))
            w("tensor", "gq", v_wq)
            for t8 in range(8):
                t = 8 * oh + t8
                if t >= SUB_QT:
                    continue
                w("tensor", "ac", v_qev.get(t - 2, 0))
                for sc in range(2):
                    pb = B[2 * (t8 % 2) + sc]
                    v_qmm = mm_seq([(wsl[:, 1024 * k + 128 * t8:
                                         1024 * k + 128 * t8 + 128],
                                     xb[:, S_OWN * k + 512 * sc:
                                        S_OWN * k + 512 * sc + 512],
                                     pb[:, :], k == 0, k == NKT - 1)
                                    for k in range(NKT)])
                    w("scalar", "pe", v_qmm)
                    v_qev[t] = act(lambda e, pb=pb, t=t, sc=sc: e.activation(
                        qb[:, 1024 * t + 512 * sc:1024 * t + 512 * sc + 512],
                        pb[:, :], AF.Relu), inc=True)
        v_q_done = P.now("ac")

    v_spill = {}
    v_apply = {}
    if STAGE >= 2 and SUB_A3 >= 2:
        for sh in range(2):
            qkv_b = xb[:, 0:8192]
            v_nev = {}
            v_ndma = {}
            for h in range(H):
                r0, c0 = head_slice(h)
                pb = B[2 + h % 2]
                if h < 2:
                    w("tensor", "gc", v_kvfull)
                    w("tensor", "ac", v_q_done)
                    if sh == 1:
                        w("tensor", "ve", v_apply.get(0, 0))
                if h >= 2:
                    w("tensor", "ac", v_nev[h - 2])
                mm_n = mm_inc(
                    kv_b[r0:r0 + 64, c0 + 64:c0 + 65],
                    qb[64 * (h % 2):64 * (h % 2) + 64,
                       1024 * (h // 2) + 512 * sh:
                       1024 * (h // 2) + 512 * sh + 512],
                    pb[0:1, :], start=True, stop=True)
                w("scalar", "pe", mm_n)
                w("scalar", "gq", v_ndma.get(h - 2, 0))
                nr = nrow0 if h % 2 == 0 else nrow1
                v_nev[h] = act(lambda e, pb=pb, nr=nr: e.activation(
                    nr[:], pb[0:1, :], AF.Identity), inc=True)
                w("gpsimd", "ac", v_nev[h])
                v_ndma[h] = dma("gpsimd", norms[h:h + 1, :], nr[:])
            wnow("vector", "gq")
            vec(lambda e: e.tensor_scalar(norms[:], norms[:], 1e-6, None,
                                          OP.max))
            vec(lambda e: e.reciprocal(norms[:], norms[:]))
            v_rnb = vec(lambda e: e.tensor_copy(rnb[:], norms[:]), inc=True)
            if SUB_A3 < 3:
                continue

            v_qkvb = {}
            for t in range(NKT):
                if SITE_QK:
                    emit_jobs_site(1)
                pb = B[t % 2]
                pbr = B[5]
                wv_prev = v_qkvb.get(t - 2, 0)
                w("tensor", "ve", wv_prev)
                qkops = []
                for hh in range(2):
                    h = 2 * t + hh
                    r0, c0 = head_slice(h)
                    qkops.append((kv_b[r0:r0 + 64, c0:c0 + 64],
                                  qb[64 * hh:64 * hh + 64,
                                     1024 * t + 512 * sh:
                                     1024 * t + 512 * sh + 512],
                                  pb[64 * hh:64 * hh + 64, :], True, True))
                v_qkvmm = mm_seq(qkops)
                w("tensor", "ve", v_rnb)
                if t >= 1:
                    w("tensor", "ve", v_qkvb.get(t - 1, 0))
                v_bcmm = mm_inc(sel_s[:, 128 * t:128 * t + 128], rnb[:, :],
                                pbr[:, :], start=True, stop=True)
                w("scalar", "pe", v_qkvmm)
                if t >= 1:
                    w("scalar", "ve", v_qkvb.get(t - 1, 0))
                v_qe = act(lambda e, pb=pb: e.activation(
                    qsc[:], pb[:, :], AF.Identity), inc=True)
                w("vector", "ac", v_qe)
                w("vector", "pe", v_bcmm)
                v_qkvb[t] = vec(lambda e, t=t, pbr=pbr: e.tensor_tensor(
                    qkv_b[:, 512 * t:512 * t + 512], qsc[:], pbr[:, :],
                    OP.mult), inc=True)

            if SUB_A3 < 4:
                continue
            wnow("tensor", "ac")
            if sh == 1:
                wnow("vector", "gq")
            v_y1ev = {}
            v_add = {}
            for oh in range(2):
                wnow("gpsimd", "pe")
                v_wo = dma("gpsimd",
                           wsl[:, 0:16384].rearrange("p (k o) -> p k o",
                                                     o=1024),
                           wo_d[:, 1024 * oh:1024 * oh + 1024].rearrange(
                               "(k p) o -> p k o", p=128))
                w("tensor", "gq", v_wo)
                for t8 in range(8):
                    if SITE_WO:
                        emit_jobs_site(1)
                    g = 8 * oh + t8
                    pb = B[2 + t8 % 2]
                    w("gpsimd", "ve", v_add.get(g - 2, 0))
                    v_xr = dma("gpsimd",
                               xres[:, 512 * (g % 2):512 * (g % 2) + 512],
                               x_own[128 * g:128 * g + 128,
                                     512 * sh:512 * sh + 512])
                    w("tensor", "ve", v_qkvb[NKT - 1])
                    w("tensor", "ac", v_y1ev.get(g - 2, 0))
                    v_womm = mm_seq([(wsl[:, 1024 * k + 128 * t8:
                                           1024 * k + 128 * t8 + 128],
                                       qkv_b[:, 512 * k:512 * k + 512],
                                       pb[:, :], k == 0, k == NKT - 1)
                                      for k in range(NKT)])
                    w("scalar", "pe", v_womm)
                    w("scalar", "ve", v_add.get(g - 1, 0))
                    v_y1ev[g] = act(lambda e, pb=pb, g=g: e.activation(
                        hsil[:], pb[:, :], AF.Identity,
                        bias=bo_sb[:, g:g + 1]), inc=True)
                    w("vector", "ac", v_y1ev[g])
                    w("vector", "gq", v_xr)
                    v_add[g] = vec(lambda e, g=g: e.tensor_add(
                        big1[:, 512 * g:512 * g + 512], hsil[:],
                        xres[:, 512 * (g % 2):512 * (g % 2) + 512]),
                        inc=True)
                    v_y1b = vec(lambda e, g=g: e.tensor_copy(
                        y1b[:], big1[:, 512 * g:512 * g + 512]), inc=True)
                    w("scalar", "ve", v_y1b)
                    v_sqa = act(lambda e: e.activation(
                        y1sq[:], y1b[:], AF.Square), inc=True)
                    w("tensor", "ve", v_y1b)
                    mm(ones_c[:], y1b[:], B[6][0:1, :],
                       start=(g == 0), stop=(g == 15))
                    w("tensor", "ac", v_sqa)
                    v_stmm = mm_inc(ones_c[:], y1sq[:], B[7][0:1, :],
                                    start=(g == 0), stop=(g == 15))
                    w("vector", "pe", v_stmm)
            if SUB_A3 < 5:
                v_apply[sh] = P.now("ve")
                v_spill[sh] = P.now("gq")
                continue
            v_stats = P.now("pe")
            w("vector", "pe", v_stats)
            vec(lambda e: e.tensor_scalar(s_mean[:], B[6][0:1, :],
                                          1.0 / D, None, OP.mult))
            vec(lambda e: e.tensor_scalar(s_sq[:], B[7][0:1, :],
                                          1.0 / D, None, OP.mult))
            vec(lambda e: e.tensor_mul(s_var[:], s_mean[:], s_mean[:]))
            v_var = vec(lambda e: e.tensor_sub(s_var[:], s_sq[:], s_var[:]),
                        inc=True)
            w("scalar", "ve", v_var)
            act(lambda e: e.activation(s_rstd[:], s_var[:], AF.Ln,
                                       bias=eps1[:]))
            v_rstd = act(lambda e: e.activation(s_rstd[:], s_rstd[:],
                                                AF.Exp, scale=-0.5),
                         inc=True)
            w("tensor", "ac", v_rstd)
            w("tensor", "ve", v_var)
            mm(ones_r32[:], s_mean[:], B[2][:, :], start=True, stop=True)
            v_bc = mm_inc(ones_r32[:], s_rstd[:], B[3][:, :],
                          start=True, stop=True)
            w("vector", "pe", v_bc)
            last = None
            for g in range(NKT):
                vec(lambda e, g=g: e.tensor_sub(
                    hsil[:], big1[:, 512 * g:512 * g + 512], B[2][:, :]))
                last = vec(lambda e, g=g: e.tensor_mul(
                    big1[:, 512 * g:512 * g + 512], hsil[:], B[3][:, :]),
                    inc=True)
            v_apply[sh] = last
            w("gpsimd", "ve", v_apply[sh])
            v_spill[sh] = dma("gpsimd",
                              x1_spill[sh, :, :].rearrange(
                                  "(k p) t -> p k t", p=128),
                              big1[:].rearrange("p (k t) -> p k t", t=512))
            if DEBUG and STAGE == 2:
                dma("gpsimd", dbg_x1[:, 8192 * sh:8192 * sh + 8192],
                    big1[:])
            pass

    if gstate["done"] < NG + NU:
        emit_all_jobs()

    if STAGE >= 4 and SUB_B >= 1:
        v_pollg = emit_poll(1)
        v_pollu = emit_poll(2)
        v_polld = None

        def h_of(k):
            if k < 16:
                return xb[:, 8192 + 512 * k:8192 + 512 * k + 512]
            return h_hi[:, 512 * (k - 16):512 * (k - 16) + 512]

        v_out_store = 0
        for sh in (range(2) if SUB_B >= 2 else []):
            if sh == 1:
                w("sync", "gq", v_out_store)
            w("sync", "gq", v_spill[1])
            v_x1r = dma("sync", big1[:].rearrange("p (k t) -> p k t", t=512),
                        x1_spill[sh, :, :].rearrange("(k p) t -> p k t",
                                                     p=128))
            x1b_sh = xb[:, 0:8192]
            w("gpsimd", "gq", v_spill[1])
            wnow("gpsimd", "pe")
            v_x1bc = dma("gpsimd",
                         x1b_sh.rearrange("p (k t) -> p k t", t=512),
                         x1_spill[sh, :, :].rearrange("(k p) t -> p k t",
                                                      p=128))
            if sh == 0:
                emit_all_jobs()
                v_polld = emit_poll(3)
            if SUB_B < 3:
                w("gpsimd", "gq", v_x1bc)
                v_out_store = dma(
                    "gpsimd",
                    out_d.rearrange("(k p) t -> p k t", p=128)[
                        :, :, 512 * sh:512 * sh + 512],
                    big1[:].rearrange("p (k t) -> p k t", t=512))
                v_out_ve = P.now("ve")
                continue
            w("sync", "gc", v_pollu)
            w("sync", "gc", v_pollg)
            NM = 22
            GSL = [wsl[:, 0:4096], wsl[:, 4096:8192]]
            USL = [wsl[:, 8192:12288], wsl[:, 12288:16384]]
            v_ht = {}
            v_pe_after = {}
            MACROS = [(m, min(256, INTER - 256 * m)) for m in range(NM)]

            def plan_b1_load(m):
                mw = MACROS[m][1]
                mpar = m % 2
                w("sync", "pe", v_pe_after.get(m - 2, P.now("pe")
                                               if m < 2 else 0))
                vg_ = 0
                vu_ = 0
                for k in range(NKT):
                    vg_ = dma("sync", GSL[mpar][:, 256 * k:256 * k + mw],
                              Wg_n[128 * k:128 * k + 128,
                                   256 * m:256 * m + mw])
                    vu_ = dma("sync", USL[mpar][:, 256 * k:256 * k + mw],
                              Wu_n[128 * k:128 * k + 128,
                                   256 * m:256 * m + mw])
                return max(vg_, vu_)

            v_msl = {0: plan_b1_load(0), 1: plan_b1_load(1)}
            for m, mw in MACROS:
                mpar = m % 2
                nt = (mw + 127) // 128
                for otl in range(nt):
                    ot = 2 * m + otl
                    par = ot % 2
                    pbg = B[par]
                    pbu = B[2 + par]
                    w("tensor", "sq", v_msl[m])
                    w("tensor", "gq", v_x1bc)
                    if sh == 1 or ot >= 2:
                        w("tensor", "ve", v_ht.get(ot - 2, 0))
                    if ot == 0:
                        w("tensor", "ve", v_apply[1] if sh == 0
                          else v_out_ve)
                    v_gmm = mm_seq([(GSL[mpar][:, 256 * k + 128 * otl:
                                               256 * k + 128 * otl + 128],
                                     x1b_sh[:, 512 * k:512 * k + 512],
                                     pbg[:, :], k == 0, k == NKT - 1)
                                    for k in range(NKT)])
                    v_umm = mm_seq([(USL[mpar][:, 256 * k + 128 * otl:
                                               256 * k + 128 * otl + 128],
                                     x1b_sh[:, 512 * k:512 * k + 512],
                                     pbu[:, :], k == 0, k == NKT - 1)
                                    for k in range(NKT)])
                    w("scalar", "pe", v_gmm)
                    w("scalar", "ve", v_ht.get(ot - 1, 0))
                    v_sil = act(lambda e, pbg=pbg: e.activation(
                        hsil[:], pbg[:, :], AF.Silu), inc=True)
                    w("vector", "ac", v_sil)
                    w("vector", "pe", v_umm)
                    v_ht[ot] = vec(lambda e, ot=ot, pbu=pbu: e.tensor_tensor(
                        h_of(ot), hsil[:], pbu[:, :], OP.mult), inc=True)
                v_pe_after[m] = P.now("pe")
                if m + 2 < NM:
                    v_msl[m + 2] = plan_b1_load(m + 2)
            v_h_done = P.now("ve")
            if SUB_B < 4:
                w("gpsimd", "ve", v_h_done)
                v_out_store = dma(
                    "gpsimd",
                    out_d.rearrange("(k p) t -> p k t", p=128)[
                        :, :, 512 * sh:512 * sh + 512],
                    big1[:].rearrange("p (k t) -> p k t", t=512))
                v_out_ve = P.now("ve")
                continue
            if sh == 0:
                w("gpsimd", "gc", v_polld)
                for ot2 in range(16):
                    v_rp = dma("gpsimd", Wd4[ot2, :, :, :],
                               Wd_n[:, 128 * ot2:128 * ot2 + 128].rearrange(
                                   "(k p) o -> p k o", p=128))
                v_repack = v_rp
            DS = [wsl[:, 0:5632], wsl[:, 5632:11264]]
            v_dsl = {}
            v_pe_d = {}

            def plan_b2_load(ot2):
                par = ot2 % 2
                if ot2 < 2:
                    w("sync", "pe", v_pe_after[21])
                    w("sync", "gq", v_repack)
                else:
                    w("sync", "pe", v_pe_d[ot2 - 2])
                v_dsl[ot2] = dma("sync", DS[par].rearrange(
                    "p (k o) -> p k o", o=128), Wd4[ot2, :, :, :])

            plan_b2_load(0)
            plan_b2_load(1)
            v_y2b_prev = 0
            for ot2 in range(16):
                par = ot2 % 2
                pbd = B[4 + par]
                w("tensor", "sq", v_dsl[ot2])
                w("tensor", "ve", v_h_done)
                if ot2 == 0:
                    wnow("tensor", "ve")
                w("tensor", "ve", v_y2b_prev)
                for k in range(43):
                    mm(DS[par][:, 128 * k:128 * k + 128], h_of(k),
                       pbd[:, :], k == 0, k == 42)
                mm(ones_c[0:1, 0:1], ones_c[0:1, 0:1],
                   B[par][0:1, 0:1], True, True)
                v_dmm = mm_inc(ones_c[0:1, 0:1], ones_c[0:1, 0:1],
                               B[par][0:1, 0:1], True, True)
                v_pe_d[ot2] = v_dmm
                if ot2 + 2 <= 15:
                    plan_b2_load(ot2 + 2)
                w("vector", "pe", v_dmm)
                w("vector", "sq", v_x1r)
                vec(lambda e, ot2=ot2, pbd=pbd: e.tensor_add(
                    big1[:, 512 * ot2:512 * ot2 + 512],
                    big1[:, 512 * ot2:512 * ot2 + 512], pbd[:, :]))
                v_y2b = vec(lambda e, ot2=ot2: e.tensor_copy(
                    y1b[:], big1[:, 512 * ot2:512 * ot2 + 512]), inc=True)
                v_y2b_prev = v_y2b
                w("scalar", "ve", v_y2b)
                v_sqa = act(lambda e: e.activation(
                    y1sq[:], y1b[:], AF.Square), inc=True)
                w("tensor", "ve", v_y2b)
                mm(ones_c[:], y1b[:], B[6][0:1, :],
                   start=(ot2 == 0), stop=(ot2 == 15))
                w("tensor", "ac", v_sqa)
                v_stmm = mm_inc(ones_c[:], y1sq[:], B[7][0:1, :],
                                start=(ot2 == 0), stop=(ot2 == 15))
                w("vector", "pe", v_stmm)
            v_stats = P.now("pe")
            w("vector", "pe", v_stats)
            vec(lambda e: e.tensor_scalar(s_mean[:], B[6][0:1, :],
                                          1.0 / D, None, OP.mult))
            vec(lambda e: e.tensor_scalar(s_sq[:], B[7][0:1, :],
                                          1.0 / D, None, OP.mult))
            vec(lambda e: e.tensor_mul(s_var[:], s_mean[:], s_mean[:]))
            v_var = vec(lambda e: e.tensor_sub(s_var[:], s_sq[:], s_var[:]),
                        inc=True)
            w("scalar", "ve", v_var)
            act(lambda e: e.activation(s_rstd[:], s_var[:], AF.Ln,
                                       bias=eps1[:]))
            v_rstd = act(lambda e: e.activation(s_rstd[:], s_rstd[:],
                                                AF.Exp, scale=-0.5),
                         inc=True)
            w("tensor", "ac", v_rstd)
            w("tensor", "ve", v_var)
            mm(ones_r32[:], s_mean[:], B[2][:, :], start=True, stop=True)
            v_bc = mm_inc(ones_r32[:], s_rstd[:], B[3][:, :],
                          start=True, stop=True)
            w("vector", "pe", v_bc)
            last = None
            for g in range(NKT):
                vec(lambda e, g=g: e.tensor_sub(
                    hsil[:], big1[:, 512 * g:512 * g + 512], B[2][:, :]))
                last = vec(lambda e, g=g: e.tensor_mul(
                    big1[:, 512 * g:512 * g + 512], hsil[:], B[3][:, :]),
                    inc=True)
            v_out_ve = last
            w("gpsimd", "ve", v_out_ve)
            v_out_store = dma(
                "gpsimd",
                out_d.rearrange("(k p) t -> p k t", p=128)[
                    :, :, 512 * sh:512 * sh + 512],
                big1[:].rearrange("p (k t) -> p k t", t=512))

    if STAGE >= 4 and SUB_B < 2:
        wnow("gpsimd", "ve")
        dma("gpsimd", out_d.rearrange("(k p) t -> p k t", p=128)[:, :, 0:512],
            big1[:].rearrange("p (k t) -> p k t", t=512))
    if int(os.environ.get("MK_DBGW2", "0")) and STAGE >= 3:
        dbg_we = dt("dbg_we", [128, 8192], F32, k="ExternalOutput")
        cg, cs = P.now("gq"), P.now("sq")
        P.em("gpsimd", lambda e, cg=cg: e.wait_ge(GQ, cg))
        P.em("gpsimd", lambda e, cs=cs: e.wait_ge(SQ, cs))
        dma("gpsimd", dbg_we[:, 0:2752], Wu_n[0:128, :].bitcast(F32))
        dma("gpsimd", dbg_we[:, 2752:5504], Wu_n[1024:1152, :].bitcast(F32))
        dma("gpsimd", dbg_we[:, 5504:6528], Wd_n[0:128, :].bitcast(F32))
        dma("gpsimd", dbg_we[:, 6528:7552], Wd_n[2816:2944, :].bitcast(F32))
        dma("gpsimd", dbg_we[:, 7552:8192], Wg_n[0:128, 0:1280].bitcast(F32))

    if int(os.environ.get("MK_DBGW3", "0")) and STAGE >= 3:
        dbg_sb = dt("dbg_sb", [128, 16384], F32, k="ExternalOutput")
        cg2, cs2, cv2 = P.now("gq"), P.now("sq"), P.now("ve")
        P.em("gpsimd", lambda e, c=cg2: e.wait_ge(GQ, c))
        P.em("gpsimd", lambda e, c=cs2: e.wait_ge(SQ, c))
        P.em("gpsimd", lambda e, c=cv2: e.wait_ge(VE, c))
        dma("gpsimd", dbg_sb[:, 0:8192], xb[:, 0:16384].bitcast(F32))
        dma("gpsimd", dbg_sb[:, 8192:15104], h_hi[:, 0:13824].bitcast(F32))

    P.em("gpsimd", lambda e: e.wait_ge(GQ, P.now("gq")))
    P.em("gpsimd", lambda e: e.wait_ge(SQ, P.now("sq")))
    P.em("gpsimd", lambda e: e.wait_ge(VE, P.now("ve")))
    P.em("gpsimd", lambda e: e.wait_ge(VG, P.now("vg")))
    P.em("gpsimd", lambda e: e.wait_ge(AC, P.now("ac")))
    P.em("gpsimd", lambda e: e.wait_ge(PE_, P.now("pe")))

    with es:
        with nc.Block() as block:
            @block.sync
            def _(eng):
                for f in P.ops["sync"]:
                    f(eng)

            @block.gpsimd
            def _(eng):
                for f in P.ops["gpsimd"]:
                    f(eng)

            @block.vector
            def _(eng):
                for f in P.ops["vector"]:
                    f(eng)

            @block.scalar
            def _(eng):
                for f in P.ops["scalar"]:
                    f(eng)

            @block.tensor
            def _(eng):
                for f in P.ops["tensor"]:
                    f(eng)

    return nc


def make_inputs(L0_attn_hidden_states, L0_attn_Wq, L0_attn_Wk, L0_attn_Wv,
                L0_attn_Wo, L0_attn_bo, L0_ffn_lut, L0_ffn_gate_idx,
                L0_ffn_up_idx, L0_ffn_down_idx):
    x = np.asarray(L0_attn_hidden_states)[0]
    xT = np.ascontiguousarray(x.T)
    lut_bc = np.tile(np.asarray(L0_ffn_lut, np.float32)[None, :], (128, 1))
    bo = np.ascontiguousarray(np.asarray(L0_attn_bo, np.float32).reshape(16, 128).T)
    sel = np.zeros((32, D), ml_dtypes.bfloat16)
    for t in range(16):
        sel[2 * t, 128 * t:128 * t + 64] = 1
        sel[2 * t + 1, 128 * t + 64:128 * t + 128] = 1
    gT = np.ascontiguousarray(np.asarray(L0_ffn_gate_idx).T).astype(np.uint8)
    uT = np.ascontiguousarray(np.asarray(L0_ffn_up_idx).T).astype(np.uint8)
    dT = np.ascontiguousarray(np.asarray(L0_ffn_down_idx).T).astype(np.uint8)
    dT_pad = np.zeros((2 * 2816, D), np.uint8)
    dT_pad[0:INTER, :] = dT
    runid = np.random.randint(1, 2**30, dtype=np.int32)
    wq = np.ascontiguousarray(np.asarray(L0_attn_Wq, np.float32))
    wk = np.ascontiguousarray(np.asarray(L0_attn_Wk, np.float32))
    wv = np.ascontiguousarray(np.asarray(L0_attn_Wv, np.float32))
    wo = np.ascontiguousarray(np.asarray(L0_attn_Wo, np.float32))
    in_maps = []
    for c in range(NCORE):
        p, e = c // 2, c % 2
        own = slice(1024 * c, 1024 * c + 1024)
        pair_lo, pair_hi = 2048 * p, 2048 * p + 2048
        comp = np.r_[0:pair_lo, pair_hi:8192]
        extra = comp[:3072] if e == 0 else comp[3072:]
        in_maps.append({
            "x_own": np.ascontiguousarray(xT[:, own]),
            "x_ext": np.ascontiguousarray(xT[:, extra]),
            "wq": wq, "wk": wk, "wv": wv, "wo": wo,
            "bo": bo, "lut": lut_bc, "sel": sel,
            "gidx": np.ascontiguousarray(gT[1024 * e:1024 * e + 1024, :]),
            "uidx": np.ascontiguousarray(uT[1024 * e:1024 * e + 1024, :]),
            "didx": np.ascontiguousarray(dT_pad[2816 * e:2816 * e + 2816, :]),
            "runid": np.full((1, 1), runid, np.int32),
            "eid": np.full((1, 1), e, np.int32),
        })
    return in_maps


def kernel(**inputs):
    in_maps = make_inputs(**inputs)
    nc = build_program()
    res = run_bass_kernel_spmd(nc, in_maps, core_ids=list(range(NCORE)))
    out = np.empty((1, 8192, D), np.float32)
    for c in range(NCORE):
        out[0, 1024 * c:1024 * c + 1024, :] = \
            np.asarray(res.results[c]["out"], np.float32).T
    return out


# version 05
# speedup vs baseline: 1.6125x; 1.0404x over previous
import numpy as np
import ml_dtypes
import contextlib
import concourse.bass as bass
import concourse.mybir as mybir
from concourse.bass_utils import run_bass_kernel_spmd
from concourse.expressions import RuntimeValue

F32 = mybir.dt.float32
BF16 = mybir.dt.bfloat16
U8 = mybir.dt.uint8
I32 = mybir.dt.int32
AF = mybir.ActivationFunctionType
OP = mybir.AluOpType

import os
STAGE = int(os.environ.get("MK_STAGE", "4"))
SUB_OQ = int(os.environ.get("MK_OQ", "4"))
SUB_CH = int(os.environ.get("MK_CH", "32"))
SUB_KV = int(os.environ.get("MK_KV", "1"))
SUB_EXC = int(os.environ.get("MK_EXC", "3"))
SUB_VEV = int(os.environ.get("MK_VEV", "1"))
SUB_A3 = int(os.environ.get("MK_A3", "5"))
SUB_QT = int(os.environ.get("MK_QT", "16"))
SUB_NJ = int(os.environ.get("MK_NJ", "999"))
SUB_GST = int(os.environ.get("MK_GST", "1"))
SUB_GDV = int(os.environ.get("MK_GDV", "1"))
SUB_B = int(os.environ.get("MK_B", "9"))
SUB_JLATE = int(os.environ.get("MK_JLATE", "0"))
SITE_A1 = int(os.environ.get("MK_SA1", "1"))
SITE_QK = int(os.environ.get("MK_SQK", "1"))
SITE_WO = int(os.environ.get("MK_SWO", "1"))
GSEM = os.environ.get("MK_GSEM", "vg")
SUB_SMODE = int(os.environ.get("MK_SMODE", "0"))
DEBUG = bool(int(os.environ.get("MK_DBG", "1")))
NCORE = 8

D = 2048
S_OWN = 1024
NCH = 32
INTER = 5504
NKT = 16
H = 32
JOBW = 2048


class Plan:
    def __init__(self):
        self.ops = {"sync": [], "gpsimd": [], "vector": [], "scalar": [],
                    "tensor": []}
        self.cnt = {}

    def em(self, eng, fn):
        self.ops[eng].append(fn)

    def inc(self, sem, n=1):
        self.cnt[sem] = self.cnt.get(sem, 0) + n
        return self.cnt[sem]

    def now(self, sem):
        return self.cnt.get(sem, 0)


def build_program():
    nc = bass.Bass("TRN2", target_bir_lowering=False, debug=False,
                   detect_race_conditions=bool(int(os.environ.get("MK_RACE", "0"))))
    P = Plan()

    dt = lambda n, s, d, k="ExternalInput", **kw: nc.dram_tensor(
        n, s, d, kind=k, **kw).ap()
    x_own = dt("x_own", [D, S_OWN], F32)
    x_ext = dt("x_ext", [D, 3072], F32)
    wq_d = dt("wq", [D, D], F32)
    wk_d = dt("wk", [D, D], F32)
    wv_d = dt("wv", [D, D], F32)
    wo_d = dt("wo", [D, D], F32)
    bo_d = dt("bo", [128, 16], F32)
    lut_d = dt("lut", [128, 16], F32)
    sel_d = dt("sel", [32, D], BF16)
    gidx_d = dt("gidx", [1024, INTER], U8)
    uidx_d = dt("uidx", [1024, INTER], U8)
    didx_d = dt("didx", [2816, D], U8)
    runid_d = dt("runid", [1, 1], I32)
    eid_d = dt("eid", [1, 1], I32)
    out_d = dt("out", [D, S_OWN], F32, k="ExternalOutput")
    if DEBUG and STAGE == 1:
        dbg_kv = dt("dbg_kv", [128, 1040], F32, k="ExternalOutput")
    if DEBUG and STAGE == 2:
        dbg_x1 = dt("dbg_x1", [128, 16 * S_OWN], F32, k="ExternalOutput")

    x1_spill = dt("x1_spill", [2, D, 512], F32, k="Internal")
    Wloc_g = dt("Wloc_g", [1024, INTER], BF16, k="Internal")
    Wloc_u = dt("Wloc_u", [1024, INTER], BF16, k="Internal")
    Wloc_d = dt("Wloc_d", [2816, D], BF16, k="Internal")
    shd = lambda n, s, d: nc.dram_tensor(n, s, d, kind="Internal",
                                         addr_space="Shared").ap()
    Wg_n = shd("Wg_n", [2048, INTER], BF16)
    Wu_n = shd("Wu_n", [2048, INTER], BF16)
    Wd_n = shd("Wd_n", [5632, 2048], BF16)
    Wd4 = shd("Wd4", [16, 128, 44, 128], BF16)
    kvx = shd("kvx", [256, 1040], F32)
    flags = shd("flags", [4, 1, 2], I32)

    es = contextlib.ExitStack()
    sb = lambda n, s, d: es.enter_context(nc.sbuf_tensor(n, s, d))

    xb = sb("xb", [128, NKT * S_OWN], BF16)
    h_hi = sb("h_hi", [128, 16384], BF16)
    wsl = sb("wsl", [128, 16384], BF16)
    xck = sb("xck", [128, 2 * 2048], BF16)
    ktm = sb("ktm", [128, 2 * 512], BF16)
    vpad = sb("vpad", [128, 2 * 520], BF16)
    kv_b = sb("kv_b", [128, 1040], BF16)
    gsc_i = sb("gsc_i", [128, 2 * JOBW], U8)
    gsc_b = sb("gsc_b", [128, JOBW], BF16)
    gsc_w = sb("gsc_w", [128, 2 * JOBW], BF16)
    gsc_t = sb("gsc_t", [128, JOBW], BF16)
    sel_s = sb("sel_s", [32, D], BF16)
    y1b = sb("y1b", [128, 512], BF16)
    y1sq = sb("y1sq", [128, 512], BF16)
    rnb = sb("rnb", [32, 512], BF16)
    ones_c = sb("ones_c", [128, 1], BF16)
    ones_r32 = sb("ones_r32", [1, 128], F32)

    big1 = sb("big1", [128, 8192], F32)
    qsc = sb("qsc", [128, 512], F32)
    hsil = sb("hsil", [128, 512], F32)
    xres = sb("xres", [128, 2 * 512], F32)
    nrow0 = sb("nrow0", [1, 512], F32)
    nrow1 = sb("nrow1", [1, 512], F32)
    norms = sb("norms", [32, 512], F32)
    s_mean = sb("s_mean", [1, 512], F32)
    s_sq = sb("s_sq", [1, 512], F32)
    s_var = sb("s_var", [1, 512], F32)
    s_rstd = sb("s_rstd", [1, 512], F32)
    eps1 = sb("eps1", [1, 1], F32)
    lutbc = sb("lutbc", [128, 16], F32)
    bo_sb = sb("bo_sb", [128, 16], F32)
    rid_s = sb("rid_s", [1, 1], I32)
    eid_s = sb("eid_s", [1, 1], I32)
    flg_s = sb("flg_s", [1, 2], I32)

    kv_mine = big1[:, 0:1040]
    kv_peer = big1[:, 1040:2080]

    B = [es.enter_context(nc.psum_tensor(f"pb{i}", [128, 512], F32))
         for i in range(8)]

    sems = {}
    for name in ("sq", "gq", "pe", "ac", "ve", "vg", "gc", "po"):
        sems[name] = es.enter_context(nc.semaphore(name))
    SQ, GQ, PE_, AC, VE, VG, GC, PO = (sems[k] for k in
                                       ("sq", "gq", "pe", "ac", "ve", "vg",
                                        "gc", "po"))

    def w(eng, sem, val):
        if val <= 0:
            return
        s = sems[sem]
        P.em(eng, lambda e, s=s, v=val: e.wait_ge(s, v))

    def wnow(eng, sem):
        w(eng, sem, P.now(sem))

    def dma(q, out, in_):
        sem = "sq" if q == "sync" else "gq"
        v = P.inc(sem, 16)
        s = sems[sem]
        P.em(q, lambda e, o=out, i=in_, s=s: e.dma_start(out=o, in_=i
                                                         ).then_inc(s, 16))
        return v

    def mm(lhsT, rhs, out, start, stop):
        P.em("tensor", lambda e, l=lhsT, r=rhs, o=out, st=start, sp=stop:
             e.matmul(o, l, r, start=st, stop=sp, skip_group_check=True))

    def mm_inc(lhsT, rhs, out, start, stop):
        v = P.inc("pe", 1)
        P.em("tensor", lambda e, l=lhsT, r=rhs, o=out, st=start, sp=stop:
             e.matmul(o, l, r, start=st, stop=sp, skip_group_check=True
                      ).then_inc(PE_, 1))
        return v

    def mm_seq(ops_list):
        for (l, r, o, st, sp) in ops_list[:-1]:
            mm(l, r, o, st, sp)
        l, r, o, st, sp = ops_list[-1]
        return mm_inc(l, r, o, st, sp)

    def act(fn, inc=False):
        if inc:
            v = P.inc("ac", 1)
            P.em("scalar", lambda e, f=fn: f(e).then_inc(AC, 1))
            return v
        P.em("scalar", fn)

    def vec(fn, inc=False, sem="ve"):
        if inc:
            v = P.inc(sem, 1)
            s = sems[sem]
            P.em("vector", lambda e, f=fn, s=s: f(e).then_inc(s, 1))
            return v
        P.em("vector", fn)

    dma("sync", lutbc[:], lut_d[:])
    dma("sync", bo_sb[:], bo_d[:])
    dma("sync", sel_s[:], sel_d[:])
    dma("sync", rid_s[:], runid_d[:])
    dma("sync", eid_s[:], eid_d[:])
    v_consts = P.now("sq")
    v_xb = dma("gpsimd", xb[:].rearrange("p (k t) -> p k t", k=NKT),
               x_own.rearrange("(k p) t -> p k t", p=128))

    def _memsets(e):
        if DEBUG:
            e.memset(big1[:], 0.0)
        e.memset(ones_c[:], 1.0)
        e.memset(ones_r32[:], 1.0)
        e.memset(eps1[:], 1e-5)
        va = vpad[:].rearrange("p (b h c) -> p b h c", b=2, c=65)
        return e.memset(va[:, :, :, 64:65], 1.0)
    v_memset = vec(_memsets, inc=True)

    greg = {}

    def gp_regs(e):
        cid = e.partition_id()
        greg["e"] = cid % 2
        greg["peer"] = 1 - cid % 2
    P.em("gpsimd", gp_regs)

    def _lazy_regs(e):
        if "ebgu" in greg:
            return
        ev = e.value_load(eid_s[0:1, 0:1], min_val=0, max_val=1)
        er = e.to_reg(ev)
        ebgu = e.to_reg(0)
        e.reg_mul(ebgu, er, 1024)
        greg["ebgu"] = ebgu
        ebd = e.to_reg(0)
        e.reg_mul(ebd, er, 2816)
        greg["ebd"] = ebd
        greg["rowoff"] = e.to_reg(0)

    def dyn_rows(e, base_reg, const_rows, max_start):
        _lazy_regs(e)
        e.reg_alu(greg["rowoff"], greg[base_reg], const_rows, OP.add)
        return bass.DynSlice(
            RuntimeValue(val=greg["rowoff"], min_val=0, max_val=max_start),
            128)

    sreg = {}

    def sy_regs(e):
        cid = e.partition_id()
        sreg["e"] = cid % 2
    P.em("sync", sy_regs)

    gstate = {"n": 0, "done": 0}
    v_job_g = {}

    def emit_gather_job(mat, kt, c0, width):
        j = gstate["n"]
        gstate["n"] += 1
        par = j % 2
        idx_t = gsc_i[:, JOBW * par:JOBW * par + width]
        wac = gsc_w[:, JOBW * par:JOBW * par + width]
        src = {"g": gidx_d, "u": uidx_d, "d": didx_d}[mat]
        w("sync", GSEM, v_job_g.get(j - 2, 0))
        v_idx = dma("sync", idx_t,
                    src[128 * kt:128 * kt + 128, c0:c0 + width])
        w("vector", "sq", v_idx)
        w("vector", "sq", gstate.get(f"st{j - 2}", 0))

        def _gather(e, it=idx_t, wa=wac, wd=width):
            e.tensor_copy(gsc_b[:, 0:wd], it)
            if not SUB_GDV:
                return e.tensor_copy(wa, gsc_b[:, 0:wd])
            e.tensor_scalar(wa, gsc_b[:, 0:wd], 0.0, lutbc[:, 0:1],
                            OP.is_equal, OP.mult)
            for jj in range(1, 15):
                e.tensor_scalar(gsc_t[:, 0:wd], gsc_b[:, 0:wd], float(jj),
                                lutbc[:, jj:jj + 1], OP.is_equal, OP.mult)
                e.tensor_add(wa, wa, gsc_t[:, 0:wd])
            e.tensor_scalar(gsc_t[:, 0:wd], gsc_b[:, 0:wd], 15.0,
                            lutbc[:, 15:16], OP.is_equal, OP.mult)
            return e.tensor_add(wa, wa, gsc_t[:, 0:wd])
        v_job_g[j] = vec(_gather, inc=True, sem=GSEM)
        w("sync", GSEM, v_job_g[j])
        Wloc = {"g": Wloc_g, "u": Wloc_u, "d": Wloc_d}[mat]

        def _store(e, Wloc=Wloc, kt=kt, c0=c0, width=width, wa=wac):
            return e.dma_start(out=Wloc[128 * kt:128 * kt + 128,
                                        c0:c0 + width],
                               in_=wa).then_inc(SQ, 16)
        if SUB_GST:
            gstate[f"st{j}"] = P.inc("sq", 16)
            P.em("sync", _store)

    gate_jobs = [("g", kt, c0, min(JOBW, INTER - c0))
                 for kt in range(8) for c0 in range(0, INTER, JOBW)]
    up_jobs = [("u", kt, c0, min(JOBW, INTER - c0))
               for kt in range(8) for c0 in range(0, INTER, JOBW)]
    down_jobs = [("d", kt, 0, D) for kt in range(22)]
    job_q = gate_jobs + up_jobs + down_jobs
    NG, NU = len(gate_jobs), len(up_jobs)
    flag_after = {NG: 1, NG + NU: 2, NG + NU + len(down_jobs): 3}

    def emit_jobs(n):
        if STAGE < 3:
            return
        for _ in range(n):
            if not job_q or gstate["done"] >= SUB_NJ:
                return
            emit_gather_job(*job_q.pop(0))
            gstate["done"] += 1
            if gstate["done"] in flag_after:
                fi = flag_after[gstate["done"]]
                wnow("gpsimd", "sq")
                w("gpsimd", "sq", v_consts)
                if fi == 1:
                    def _pub(e):
                        return e.dma_start(
                            out=Wg_n[bass.ts(greg["e"], 1024), :],
                            in_=Wloc_g[:, :]).then_inc(GQ, 16)
                elif fi == 2:
                    def _pub(e):
                        return e.dma_start(
                            out=Wu_n[bass.ts(greg["e"], 1024), :],
                            in_=Wloc_u[:, :]).then_inc(GQ, 16)
                else:
                    def _pub(e):
                        return e.dma_start(
                            out=Wd_n[bass.ts(greg["e"], 2816), :],
                            in_=Wloc_d[:, :]).then_inc(GQ, 16)
                P.inc("gq", 16)
                P.em("gpsimd", _pub)
                wnow("gpsimd", "gq")

                def _flag(e, fi=fi):
                    return e.dma_start(
                        out=flags[fi, 0:1, bass.ts(greg["e"], 1)],
                        in_=rid_s[:]).then_inc(GQ, 16)
                P.inc("gq", 16)
                P.em("gpsimd", _flag)

    def emit_all_jobs():
        emit_jobs(len(job_q))

    def emit_jobs_site(n):
        if gstate["done"] < NG + NU + len(down_jobs) - 1:
            emit_jobs(n)

    def emit_poll(fi):
        def _poll(e, fi=fi):
            if "po_polls" not in greg:
                greg["po_polls"] = e.to_reg(0)
                greg["po_tgt"] = e.to_reg(0)
                greg["po_nd"] = e.to_reg(1)
                greg["po_msc"] = e.to_reg(0)
                rr = e.value_load(rid_s[0:1, 0:1])
                greg["po_rid"] = e.to_reg(rr)
            polls, tgt = greg["po_polls"], greg["po_tgt"]
            nd, msc, rreg = greg["po_nd"], greg["po_msc"], greg["po_rid"]

            def cond():
                e.reg_add(polls, polls, 1)
                e.reg_mul(tgt, polls, 16)
                e.dma_start(out=flg_s[:], in_=flags[fi, :, :]).then_inc(PO, 16)
                e.wait_ge(PO, tgt)
                e.reg_mov(nd, 0)
                for cc in range(2):
                    vc = e.value_load(flg_s[0:1, cc:cc + 1])
                    e.reg_alu(msc, vc, rreg, OP.not_equal)
                    e.reg_alu(nd, nd, msc, OP.add)
                return nd
            with e.While(cond):
                pass
            return e.sem_inc(GC, 1)
        v = P.inc("gc", 1)
        P.em("gpsimd", _poll)
        return v

    emit_jobs(4)

    if not SUB_JLATE:
        emit_jobs(4)

    v_evac_v = {}
    v_kvmm = {}
    gch = -1

    def head_slice(h):
        oq, hh = h // 8, h % 8
        return 64 * (hh % 2), 260 * oq + 65 * (hh // 2)

    for oq in range(SUB_OQ):
        wnow("gpsimd", "pe")
        o0 = 512 * oq
        v_wk = dma("gpsimd",
                   wsl[:, 0:8192].rearrange("p (k o) -> p k o", o=512),
                   wk_d[:, o0:o0 + 512].rearrange("(k p) o -> p k o", p=128))
        v_wv = dma("gpsimd",
                   wsl[:, 8192:16384].rearrange("p (k o) -> p k o", o=512),
                   wv_d[:, o0:o0 + 512].rearrange("(k p) o -> p k o", p=128))
        for ch in range(SUB_CH):
            gch += 1
            par = gch % 2
            own = ch < 8
            if own:
                xti = lambda k, ch=ch: xb[:, S_OWN * k + 128 * ch:
                                          S_OWN * k + 128 * ch + 128]
            else:
                fc = ch - 8
                w("gpsimd", "pe", v_kvmm.get(gch - 2, 0))
                v_ck = dma("gpsimd",
                           xck[:, 2048 * par:2048 * par + 2048].rearrange(
                               "p (k t) -> p k t", t=128),
                           x_ext[:, 128 * fc:128 * fc + 128].rearrange(
                               "(k p) t -> p k t", p=128))
                xti = lambda k, par=par: xck[:, 2048 * par + 128 * k:
                                             2048 * par + 128 * k + 128]
            pb_k = B[par]
            pb_v = B[2 + par]
            if ch == 0:
                w("tensor", "gq", max(v_wk, v_wv, v_xb))
            elif not own:
                w("tensor", "gq", v_ck)
            w("tensor", "ac", v_evac_v.get(gch - 2, 0))
            v_pk = mm_seq([(xti(k), wsl[:, 512 * k:512 * k + 512],
                            pb_k[:, :], k == 0, k == NKT - 1)
                           for k in range(NKT)])
            v_pv = mm_seq([(xti(k), wsl[:, 8192 + 512 * k:8192 + 512 * k + 512],
                            pb_v[:, :], k == 0, k == NKT - 1)
                           for k in range(NKT)])
            w("scalar", "pe", v_kvmm.get(gch - 2, 0))
            w("scalar", "pe", v_pk)
            if gch == 0:
                w("scalar", "ve", v_memset)
            act(lambda e, par=par, pb=pb_k: e.activation(
                ktm[:, 512 * par:512 * par + 512], pb[:, :], AF.Relu))
            w("scalar", "pe", v_pv)
            va = vpad[:].rearrange("p (b h c) -> p b h c", b=2, c=65)
            v_evac_v[gch] = act(
                lambda e, par=par, pb=pb_v, va=va: e.activation(
                    va[:, par, :, 0:64],
                    pb[:, :].rearrange("p (h c) -> p h c", c=64),
                    AF.Identity),
                inc=True)

            def emit_kv(cch):
                ppar = (4 * NCH * 0 + cch) % 2
                w("tensor", "ac", v_evac_v[cch])
                kvops = []
                for hh in range(8):
                    r0 = 64 * (hh % 2)
                    kvops.append((
                        ktm[:, 512 * (cch % 2) + 64 * hh:
                            512 * (cch % 2) + 64 * hh + 64],
                        vpad[:, 520 * (cch % 2) + 65 * hh:
                             520 * (cch % 2) + 65 * hh + 65],
                        B[4 + hh // 2][r0:r0 + 64, 0:65],
                        (cch % NCH) == 0, (cch % NCH) == NCH - 1))
                v_kvmm[cch] = mm_seq(kvops)

            if SUB_KV:
                if ch > 0:
                    emit_kv(gch - 1)
                if ch == NCH - 1:
                    emit_kv(gch)
            else:
                v_kvmm[gch] = P.now("pe")
            if ch % 4 == 3 and not SUB_JLATE and SITE_A1:
                emit_jobs_site(1)
        if SUB_KV and SUB_CH == NCH:
            w("scalar", "pe", v_kvmm[gch])
            for b4 in range(3):
                act(lambda e, oq=oq, b4=b4: e.copy(
                    kv_mine[:, 260 * oq + 65 * b4:260 * oq + 65 * b4 + 65],
                    B[4 + b4][:, 0:65]))
            v_kvev = act(lambda e, oq=oq: e.copy(
                kv_mine[:, 260 * oq + 195:260 * oq + 260],
                B[7][:, 0:65]), inc=True)
            if oq < 3:
                w("tensor", "ac", v_kvev)
        pass

    if SUB_EXC >= 1:
        wnow("gpsimd", "ac")

        def _kv_out(e):
            return e.dma_start(out=kvx[bass.ts(greg["e"], 128), :],
                               in_=kv_mine).then_inc(GQ, 16)
        P.inc("gq", 16)
        P.em("gpsimd", _kv_out)
        wnow("gpsimd", "gq")
        w("gpsimd", "sq", v_consts)

        def _kv_flag(e):
            return e.dma_start(out=flags[0, 0:1, bass.ts(greg["e"], 1)],
                               in_=rid_s[:]).then_inc(GQ, 16)
        P.inc("gq", 16)
        P.em("gpsimd", _kv_flag)
    if SUB_EXC >= 2:
        v_kvpoll = emit_poll(0)
    if SUB_EXC >= 3:
        w("gpsimd", "gc", v_kvpoll)

        def _kv_in(e):
            return e.dma_start(out=kv_peer,
                               in_=kvx[bass.ts(greg["peer"], 128), :]
                               ).then_inc(GQ, 16)
        v_peer = P.inc("gq", 16)
        P.em("gpsimd", _kv_in)
        w("gpsimd", "gq", v_peer)
        P.em("gpsimd", lambda e: e.tensor_add(kv_mine, kv_mine, kv_peer))
    v_kvfull = P.inc("gc", 1)
    P.em("gpsimd", lambda e: e.tensor_copy(kv_b[:], kv_mine
                                           ).then_inc(GC, 1))

    if DEBUG and STAGE == 1:
        wnow("gpsimd", "ve")
        dma("gpsimd", dbg_kv[:], kv_mine)

    qb = h_hi[:, 0:16384]
    if STAGE >= 2:
        v_qev = {}
        wnow("tensor", "ac")
        wnow("tensor", "ve")
        for oh in range(2):
            wnow("gpsimd", "pe")
            v_wq = dma("gpsimd",
                       wsl[:, 0:16384].rearrange("p (k o) -> p k o", o=1024),
                       wq_d[:, 1024 * oh:1024 * oh + 1024].rearrange(
                           "(k p) o -> p k o", p=128))
            w("tensor", "gq", v_wq)
            for t8 in range(8):
                t = 8 * oh + t8
                if t >= SUB_QT:
                    continue
                w("tensor", "ac", v_qev.get(t - 2, 0))
                for sc in range(2):
                    pb = B[2 * (t8 % 2) + sc]
                    v_qmm = mm_seq([(wsl[:, 1024 * k + 128 * t8:
                                         1024 * k + 128 * t8 + 128],
                                     xb[:, S_OWN * k + 512 * sc:
                                        S_OWN * k + 512 * sc + 512],
                                     pb[:, :], k == 0, k == NKT - 1)
                                    for k in range(NKT)])
                    w("scalar", "pe", v_qmm)
                    v_qev[t] = act(lambda e, pb=pb, t=t, sc=sc: e.activation(
                        qb[:, 1024 * t + 512 * sc:1024 * t + 512 * sc + 512],
                        pb[:, :], AF.Relu), inc=True)
        v_q_done = P.now("ac")

    v_spill = {}
    v_apply = {}
    if STAGE >= 2 and SUB_A3 >= 2:
        for sh in range(2):
            qkv_b = xb[:, 0:8192]
            v_nev = {}
            v_ndma = {}
            for h in range(H):
                r0, c0 = head_slice(h)
                pb = B[2 + h % 2]
                if h < 2:
                    w("tensor", "gc", v_kvfull)
                    w("tensor", "ac", v_q_done)
                    if sh == 1:
                        w("tensor", "ve", v_apply.get(0, 0))
                if h >= 2:
                    w("tensor", "ac", v_nev[h - 2])
                mm_n = mm_inc(
                    kv_b[r0:r0 + 64, c0 + 64:c0 + 65],
                    qb[64 * (h % 2):64 * (h % 2) + 64,
                       1024 * (h // 2) + 512 * sh:
                       1024 * (h // 2) + 512 * sh + 512],
                    pb[0:1, :], start=True, stop=True)
                w("scalar", "pe", mm_n)
                w("scalar", "gq", v_ndma.get(h - 2, 0))
                nr = nrow0 if h % 2 == 0 else nrow1
                v_nev[h] = act(lambda e, pb=pb, nr=nr: e.activation(
                    nr[:], pb[0:1, :], AF.Identity), inc=True)
                w("gpsimd", "ac", v_nev[h])
                v_ndma[h] = dma("gpsimd", norms[h:h + 1, :], nr[:])
            wnow("vector", "gq")
            vec(lambda e: e.tensor_scalar(norms[:], norms[:], 1e-6, None,
                                          OP.max))
            vec(lambda e: e.reciprocal(norms[:], norms[:]))
            v_rnb = vec(lambda e: e.tensor_copy(rnb[:], norms[:]), inc=True)
            if SUB_A3 < 3:
                continue

            v_qkvb = {}
            for t in range(NKT):
                if SITE_QK:
                    emit_jobs_site(1)
                pb = B[t % 2]
                pbr = B[5]
                wv_prev = v_qkvb.get(t - 2, 0)
                w("tensor", "ve", wv_prev)
                qkops = []
                for hh in range(2):
                    h = 2 * t + hh
                    r0, c0 = head_slice(h)
                    qkops.append((kv_b[r0:r0 + 64, c0:c0 + 64],
                                  qb[64 * hh:64 * hh + 64,
                                     1024 * t + 512 * sh:
                                     1024 * t + 512 * sh + 512],
                                  pb[64 * hh:64 * hh + 64, :], True, True))
                v_qkvmm = mm_seq(qkops)
                w("tensor", "ve", v_rnb)
                if t >= 1:
                    w("tensor", "ve", v_qkvb.get(t - 1, 0))
                v_bcmm = mm_inc(sel_s[:, 128 * t:128 * t + 128], rnb[:, :],
                                pbr[:, :], start=True, stop=True)
                w("scalar", "pe", v_qkvmm)
                if t >= 1:
                    w("scalar", "ve", v_qkvb.get(t - 1, 0))
                v_qe = act(lambda e, pb=pb: e.activation(
                    qsc[:], pb[:, :], AF.Identity), inc=True)
                w("vector", "ac", v_qe)
                w("vector", "pe", v_bcmm)
                v_qkvb[t] = vec(lambda e, t=t, pbr=pbr: e.tensor_tensor(
                    qkv_b[:, 512 * t:512 * t + 512], qsc[:], pbr[:, :],
                    OP.mult), inc=True)

            if SUB_A3 < 4:
                continue
            wnow("tensor", "ac")
            if sh == 1:
                wnow("vector", "gq")
            v_y1ev = {}
            v_add = {}
            for oh in range(2):
                wnow("gpsimd", "pe")
                v_wo = dma("gpsimd",
                           wsl[:, 0:16384].rearrange("p (k o) -> p k o",
                                                     o=1024),
                           wo_d[:, 1024 * oh:1024 * oh + 1024].rearrange(
                               "(k p) o -> p k o", p=128))
                w("tensor", "gq", v_wo)
                for t8 in range(8):
                    if SITE_WO:
                        emit_jobs_site(1)
                    g = 8 * oh + t8
                    pb = B[2 + t8 % 2]
                    w("gpsimd", "ve", v_add.get(g - 2, 0))
                    v_xr = dma("gpsimd",
                               xres[:, 512 * (g % 2):512 * (g % 2) + 512],
                               x_own[128 * g:128 * g + 128,
                                     512 * sh:512 * sh + 512])
                    w("tensor", "ve", v_qkvb[NKT - 1])
                    w("tensor", "ac", v_y1ev.get(g - 2, 0))
                    v_womm = mm_seq([(wsl[:, 1024 * k + 128 * t8:
                                           1024 * k + 128 * t8 + 128],
                                       qkv_b[:, 512 * k:512 * k + 512],
                                       pb[:, :], k == 0, k == NKT - 1)
                                      for k in range(NKT)])
                    w("scalar", "pe", v_womm)
                    w("scalar", "ve", v_add.get(g - 1, 0))
                    v_y1ev[g] = act(lambda e, pb=pb, g=g: e.activation(
                        hsil[:], pb[:, :], AF.Identity,
                        bias=bo_sb[:, g:g + 1]), inc=True)
                    w("vector", "ac", v_y1ev[g])
                    w("vector", "gq", v_xr)
                    v_add[g] = vec(lambda e, g=g: e.tensor_add(
                        big1[:, 512 * g:512 * g + 512], hsil[:],
                        xres[:, 512 * (g % 2):512 * (g % 2) + 512]),
                        inc=True)
                    v_y1b = vec(lambda e, g=g: e.tensor_copy(
                        y1b[:], big1[:, 512 * g:512 * g + 512]), inc=True)
                    w("scalar", "ve", v_y1b)
                    v_sqa = act(lambda e: e.activation(
                        y1sq[:], y1b[:], AF.Square), inc=True)
                    w("tensor", "ve", v_y1b)
                    mm(ones_c[:], y1b[:], B[6][0:1, :],
                       start=(g == 0), stop=(g == 15))
                    w("tensor", "ac", v_sqa)
                    v_stmm = mm_inc(ones_c[:], y1sq[:], B[7][0:1, :],
                                    start=(g == 0), stop=(g == 15))
                    w("vector", "pe", v_stmm)
            if SUB_A3 < 5:
                v_apply[sh] = P.now("ve")
                v_spill[sh] = P.now("gq")
                continue
            v_stats = P.now("pe")
            w("vector", "pe", v_stats)
            vec(lambda e: e.tensor_scalar(s_mean[:], B[6][0:1, :],
                                          1.0 / D, None, OP.mult))
            vec(lambda e: e.tensor_scalar(s_sq[:], B[7][0:1, :],
                                          1.0 / D, None, OP.mult))
            vec(lambda e: e.tensor_mul(s_var[:], s_mean[:], s_mean[:]))
            v_var = vec(lambda e: e.tensor_sub(s_var[:], s_sq[:], s_var[:]),
                        inc=True)
            w("scalar", "ve", v_var)
            act(lambda e: e.activation(s_rstd[:], s_var[:], AF.Ln,
                                       bias=eps1[:]))
            v_rstd = act(lambda e: e.activation(s_rstd[:], s_rstd[:],
                                                AF.Exp, scale=-0.5),
                         inc=True)
            w("tensor", "ac", v_rstd)
            w("tensor", "ve", v_var)
            mm(ones_r32[:], s_mean[:], B[2][:, :], start=True, stop=True)
            v_bc = mm_inc(ones_r32[:], s_rstd[:], B[3][:, :],
                          start=True, stop=True)
            w("vector", "pe", v_bc)
            last = None
            for g in range(NKT):
                vec(lambda e, g=g: e.tensor_sub(
                    hsil[:], big1[:, 512 * g:512 * g + 512], B[2][:, :]))
                last = vec(lambda e, g=g: e.tensor_mul(
                    big1[:, 512 * g:512 * g + 512], hsil[:], B[3][:, :]),
                    inc=True)
            v_apply[sh] = last
            w("gpsimd", "ve", v_apply[sh])
            v_spill[sh] = dma("gpsimd",
                              x1_spill[sh, :, :].rearrange(
                                  "(k p) t -> p k t", p=128),
                              big1[:].rearrange("p (k t) -> p k t", t=512))
            if DEBUG and STAGE == 2:
                dma("gpsimd", dbg_x1[:, 8192 * sh:8192 * sh + 8192],
                    big1[:])
            pass

    if gstate["done"] < NG + NU:
        emit_all_jobs()

    if STAGE >= 4 and SUB_B >= 1:
        v_pollg = emit_poll(1)
        v_pollu = emit_poll(2)
        v_polld = None

        def h_of(k):
            if k < 16:
                return xb[:, 8192 + 512 * k:8192 + 512 * k + 512]
            return h_hi[:, 512 * (k - 16):512 * (k - 16) + 512]

        v_out_store = 0
        for sh in (range(2) if SUB_B >= 2 else []):
            if sh == 1:
                w("sync", "gq", v_out_store)
            w("sync", "gq", v_spill[1])
            v_x1r = dma("sync", big1[:].rearrange("p (k t) -> p k t", t=512),
                        x1_spill[sh, :, :].rearrange("(k p) t -> p k t",
                                                     p=128))
            x1b_sh = xb[:, 0:8192]
            w("gpsimd", "gq", v_spill[1])
            wnow("gpsimd", "pe")
            v_x1bc = dma("gpsimd",
                         x1b_sh.rearrange("p (k t) -> p k t", t=512),
                         x1_spill[sh, :, :].rearrange("(k p) t -> p k t",
                                                      p=128))
            if sh == 0:
                emit_all_jobs()
                v_polld = emit_poll(3)
            if SUB_B < 3:
                w("gpsimd", "gq", v_x1bc)
                v_out_store = dma(
                    "gpsimd",
                    out_d.rearrange("(k p) t -> p k t", p=128)[
                        :, :, 512 * sh:512 * sh + 512],
                    big1[:].rearrange("p (k t) -> p k t", t=512))
                v_out_ve = P.now("ve")
                continue
            w("sync", "gc", v_pollu)
            w("sync", "gc", v_pollg)
            NM = 22
            GSL = [wsl[:, 0:4096], wsl[:, 4096:8192]]
            USL = [wsl[:, 8192:12288], wsl[:, 12288:16384]]
            v_ht = {}
            v_pe_after = {}
            MACROS = [(m, min(256, INTER - 256 * m)) for m in range(NM)]

            def plan_b1_load(m):
                mw = MACROS[m][1]
                mpar = m % 2
                uq = "gpsimd" if m >= 4 else "sync"
                w("sync", "pe", v_pe_after.get(m - 2, P.now("pe")
                                               if m < 2 else 0))
                if uq == "gpsimd":
                    w("gpsimd", "pe", v_pe_after[m - 2])
                vg_ = 0
                vu_ = 0
                for k in range(NKT):
                    vg_ = dma("sync", GSL[mpar][:, 256 * k:256 * k + mw],
                              Wg_n[128 * k:128 * k + 128,
                                   256 * m:256 * m + mw])
                    vu_ = dma(uq, USL[mpar][:, 256 * k:256 * k + mw],
                              Wu_n[128 * k:128 * k + 128,
                                   256 * m:256 * m + mw])
                return (vg_, vu_ if uq == "gpsimd" else 0)

            v_msl = {0: plan_b1_load(0), 1: plan_b1_load(1)}
            for m, mw in MACROS:
                mpar = m % 2
                nt = (mw + 127) // 128
                for otl in range(nt):
                    ot = 2 * m + otl
                    par = ot % 2
                    pbg = B[par]
                    pbu = B[2 + par]
                    w("tensor", "sq", v_msl[m][0])
                    w("tensor", "gq", v_msl[m][1])
                    w("tensor", "gq", v_x1bc)
                    if sh == 1 or ot >= 2:
                        w("tensor", "ve", v_ht.get(ot - 2, 0))
                    if ot == 0:
                        w("tensor", "ve", v_apply[1] if sh == 0
                          else v_out_ve)
                    v_gmm = mm_seq([(GSL[mpar][:, 256 * k + 128 * otl:
                                               256 * k + 128 * otl + 128],
                                     x1b_sh[:, 512 * k:512 * k + 512],
                                     pbg[:, :], k == 0, k == NKT - 1)
                                    for k in range(NKT)])
                    v_umm = mm_seq([(USL[mpar][:, 256 * k + 128 * otl:
                                               256 * k + 128 * otl + 128],
                                     x1b_sh[:, 512 * k:512 * k + 512],
                                     pbu[:, :], k == 0, k == NKT - 1)
                                    for k in range(NKT)])
                    w("scalar", "pe", v_gmm)
                    w("scalar", "ve", v_ht.get(ot - 1, 0))
                    v_sil = act(lambda e, pbg=pbg: e.activation(
                        hsil[:], pbg[:, :], AF.Silu), inc=True)
                    w("vector", "ac", v_sil)
                    w("vector", "pe", v_umm)
                    v_ht[ot] = vec(lambda e, ot=ot, pbu=pbu: e.tensor_tensor(
                        h_of(ot), hsil[:], pbu[:, :], OP.mult), inc=True)
                v_pe_after[m] = P.now("pe")
                if m + 2 < NM:
                    v_msl[m + 2] = plan_b1_load(m + 2)
            v_h_done = P.now("ve")
            if SUB_B < 4:
                w("gpsimd", "ve", v_h_done)
                v_out_store = dma(
                    "gpsimd",
                    out_d.rearrange("(k p) t -> p k t", p=128)[
                        :, :, 512 * sh:512 * sh + 512],
                    big1[:].rearrange("p (k t) -> p k t", t=512))
                v_out_ve = P.now("ve")
                continue
            if sh == 0:
                w("gpsimd", "gc", v_polld)
                for ot2 in range(16):
                    v_rp = dma("gpsimd", Wd4[ot2, :, :, :],
                               Wd_n[:, 128 * ot2:128 * ot2 + 128].rearrange(
                                   "(k p) o -> p k o", p=128))
                v_repack = v_rp
            DS = [wsl[:, 0:5632], wsl[:, 5632:11264]]
            v_dsl = {}
            v_pe_d = {}

            def plan_b2_load(ot2):
                par = ot2 % 2
                if ot2 < 2:
                    w("sync", "pe", v_pe_after[21])
                    w("sync", "gq", v_repack)
                else:
                    w("sync", "pe", v_pe_d[ot2 - 2])
                v_dsl[ot2] = dma("sync", DS[par].rearrange(
                    "p (k o) -> p k o", o=128), Wd4[ot2, :, :, :])

            plan_b2_load(0)
            plan_b2_load(1)
            v_y2b_prev = 0
            for ot2 in range(16):
                par = ot2 % 2
                pbd = B[4 + par]
                w("tensor", "sq", v_dsl[ot2])
                w("tensor", "ve", v_h_done)
                if ot2 == 0:
                    wnow("tensor", "ve")
                w("tensor", "ve", v_y2b_prev)
                for k in range(43):
                    mm(DS[par][:, 128 * k:128 * k + 128], h_of(k),
                       pbd[:, :], k == 0, k == 42)
                mm(ones_c[0:1, 0:1], ones_c[0:1, 0:1],
                   B[par][0:1, 0:1], True, True)
                v_dmm = mm_inc(ones_c[0:1, 0:1], ones_c[0:1, 0:1],
                               B[par][0:1, 0:1], True, True)
                v_pe_d[ot2] = v_dmm
                if ot2 + 2 <= 15:
                    plan_b2_load(ot2 + 2)
                w("vector", "pe", v_dmm)
                w("vector", "sq", v_x1r)
                vec(lambda e, ot2=ot2, pbd=pbd: e.tensor_add(
                    big1[:, 512 * ot2:512 * ot2 + 512],
                    big1[:, 512 * ot2:512 * ot2 + 512], pbd[:, :]))
                v_y2b = vec(lambda e, ot2=ot2: e.tensor_copy(
                    y1b[:], big1[:, 512 * ot2:512 * ot2 + 512]), inc=True)
                v_y2b_prev = v_y2b
                w("scalar", "ve", v_y2b)
                v_sqa = act(lambda e: e.activation(
                    y1sq[:], y1b[:], AF.Square), inc=True)
                w("tensor", "ve", v_y2b)
                mm(ones_c[:], y1b[:], B[6][0:1, :],
                   start=(ot2 == 0), stop=(ot2 == 15))
                w("tensor", "ac", v_sqa)
                v_stmm = mm_inc(ones_c[:], y1sq[:], B[7][0:1, :],
                                start=(ot2 == 0), stop=(ot2 == 15))
                w("vector", "pe", v_stmm)
            v_stats = P.now("pe")
            w("vector", "pe", v_stats)
            vec(lambda e: e.tensor_scalar(s_mean[:], B[6][0:1, :],
                                          1.0 / D, None, OP.mult))
            vec(lambda e: e.tensor_scalar(s_sq[:], B[7][0:1, :],
                                          1.0 / D, None, OP.mult))
            vec(lambda e: e.tensor_mul(s_var[:], s_mean[:], s_mean[:]))
            v_var = vec(lambda e: e.tensor_sub(s_var[:], s_sq[:], s_var[:]),
                        inc=True)
            w("scalar", "ve", v_var)
            act(lambda e: e.activation(s_rstd[:], s_var[:], AF.Ln,
                                       bias=eps1[:]))
            v_rstd = act(lambda e: e.activation(s_rstd[:], s_rstd[:],
                                                AF.Exp, scale=-0.5),
                         inc=True)
            w("tensor", "ac", v_rstd)
            w("tensor", "ve", v_var)
            mm(ones_r32[:], s_mean[:], B[2][:, :], start=True, stop=True)
            v_bc = mm_inc(ones_r32[:], s_rstd[:], B[3][:, :],
                          start=True, stop=True)
            w("vector", "pe", v_bc)
            last = None
            for g in range(NKT):
                vec(lambda e, g=g: e.tensor_sub(
                    hsil[:], big1[:, 512 * g:512 * g + 512], B[2][:, :]))
                last = vec(lambda e, g=g: e.tensor_mul(
                    big1[:, 512 * g:512 * g + 512], hsil[:], B[3][:, :]),
                    inc=True)
            v_out_ve = last
            w("gpsimd", "ve", v_out_ve)
            v_out_store = dma(
                "gpsimd",
                out_d.rearrange("(k p) t -> p k t", p=128)[
                    :, :, 512 * sh:512 * sh + 512],
                big1[:].rearrange("p (k t) -> p k t", t=512))

    if STAGE >= 4 and SUB_B < 2:
        wnow("gpsimd", "ve")
        dma("gpsimd", out_d.rearrange("(k p) t -> p k t", p=128)[:, :, 0:512],
            big1[:].rearrange("p (k t) -> p k t", t=512))
    if int(os.environ.get("MK_DBGW2", "0")) and STAGE >= 3:
        dbg_we = dt("dbg_we", [128, 8192], F32, k="ExternalOutput")
        cg, cs = P.now("gq"), P.now("sq")
        P.em("gpsimd", lambda e, cg=cg: e.wait_ge(GQ, cg))
        P.em("gpsimd", lambda e, cs=cs: e.wait_ge(SQ, cs))
        dma("gpsimd", dbg_we[:, 0:2752], Wu_n[0:128, :].bitcast(F32))
        dma("gpsimd", dbg_we[:, 2752:5504], Wu_n[1024:1152, :].bitcast(F32))
        dma("gpsimd", dbg_we[:, 5504:6528], Wd_n[0:128, :].bitcast(F32))
        dma("gpsimd", dbg_we[:, 6528:7552], Wd_n[2816:2944, :].bitcast(F32))
        dma("gpsimd", dbg_we[:, 7552:8192], Wg_n[0:128, 0:1280].bitcast(F32))

    if int(os.environ.get("MK_DBGW3", "0")) and STAGE >= 3:
        dbg_sb = dt("dbg_sb", [128, 16384], F32, k="ExternalOutput")
        cg2, cs2, cv2 = P.now("gq"), P.now("sq"), P.now("ve")
        P.em("gpsimd", lambda e, c=cg2: e.wait_ge(GQ, c))
        P.em("gpsimd", lambda e, c=cs2: e.wait_ge(SQ, c))
        P.em("gpsimd", lambda e, c=cv2: e.wait_ge(VE, c))
        dma("gpsimd", dbg_sb[:, 0:8192], xb[:, 0:16384].bitcast(F32))
        dma("gpsimd", dbg_sb[:, 8192:15104], h_hi[:, 0:13824].bitcast(F32))

    P.em("gpsimd", lambda e: e.wait_ge(GQ, P.now("gq")))
    P.em("gpsimd", lambda e: e.wait_ge(SQ, P.now("sq")))
    P.em("gpsimd", lambda e: e.wait_ge(VE, P.now("ve")))
    P.em("gpsimd", lambda e: e.wait_ge(VG, P.now("vg")))
    P.em("gpsimd", lambda e: e.wait_ge(AC, P.now("ac")))
    P.em("gpsimd", lambda e: e.wait_ge(PE_, P.now("pe")))

    with es:
        with nc.Block() as block:
            @block.sync
            def _(eng):
                for f in P.ops["sync"]:
                    f(eng)

            @block.gpsimd
            def _(eng):
                for f in P.ops["gpsimd"]:
                    f(eng)

            @block.vector
            def _(eng):
                for f in P.ops["vector"]:
                    f(eng)

            @block.scalar
            def _(eng):
                for f in P.ops["scalar"]:
                    f(eng)

            @block.tensor
            def _(eng):
                for f in P.ops["tensor"]:
                    f(eng)

    return nc


def make_inputs(L0_attn_hidden_states, L0_attn_Wq, L0_attn_Wk, L0_attn_Wv,
                L0_attn_Wo, L0_attn_bo, L0_ffn_lut, L0_ffn_gate_idx,
                L0_ffn_up_idx, L0_ffn_down_idx):
    x = np.asarray(L0_attn_hidden_states)[0]
    xT = np.ascontiguousarray(x.T)
    lut_bc = np.tile(np.asarray(L0_ffn_lut, np.float32)[None, :], (128, 1))
    bo = np.ascontiguousarray(np.asarray(L0_attn_bo, np.float32).reshape(16, 128).T)
    sel = np.zeros((32, D), ml_dtypes.bfloat16)
    for t in range(16):
        sel[2 * t, 128 * t:128 * t + 64] = 1
        sel[2 * t + 1, 128 * t + 64:128 * t + 128] = 1
    gT = np.ascontiguousarray(np.asarray(L0_ffn_gate_idx).T).astype(np.uint8)
    uT = np.ascontiguousarray(np.asarray(L0_ffn_up_idx).T).astype(np.uint8)
    dT = np.ascontiguousarray(np.asarray(L0_ffn_down_idx).T).astype(np.uint8)
    dT_pad = np.zeros((2 * 2816, D), np.uint8)
    dT_pad[0:INTER, :] = dT
    runid = np.random.randint(1, 2**30, dtype=np.int32)
    wq = np.ascontiguousarray(np.asarray(L0_attn_Wq, np.float32))
    wk = np.ascontiguousarray(np.asarray(L0_attn_Wk, np.float32))
    wv = np.ascontiguousarray(np.asarray(L0_attn_Wv, np.float32))
    wo = np.ascontiguousarray(np.asarray(L0_attn_Wo, np.float32))
    in_maps = []
    for c in range(NCORE):
        p, e = c // 2, c % 2
        own = slice(1024 * c, 1024 * c + 1024)
        pair_lo, pair_hi = 2048 * p, 2048 * p + 2048
        comp = np.r_[0:pair_lo, pair_hi:8192]
        extra = comp[:3072] if e == 0 else comp[3072:]
        in_maps.append({
            "x_own": np.ascontiguousarray(xT[:, own]),
            "x_ext": np.ascontiguousarray(xT[:, extra]),
            "wq": wq, "wk": wk, "wv": wv, "wo": wo,
            "bo": bo, "lut": lut_bc, "sel": sel,
            "gidx": np.ascontiguousarray(gT[1024 * e:1024 * e + 1024, :]),
            "uidx": np.ascontiguousarray(uT[1024 * e:1024 * e + 1024, :]),
            "didx": np.ascontiguousarray(dT_pad[2816 * e:2816 * e + 2816, :]),
            "runid": np.full((1, 1), runid, np.int32),
            "eid": np.full((1, 1), e, np.int32),
        })
    return in_maps


def kernel(**inputs):
    in_maps = make_inputs(**inputs)
    nc = build_program()
    res = run_bass_kernel_spmd(nc, in_maps, core_ids=list(range(NCORE)))
    out = np.empty((1, 8192, D), np.float32)
    for c in range(NCORE):
        out[0, 1024 * c:1024 * c + 1024, :] = \
            np.asarray(res.results[c]["out"], np.float32).T
    return out
